# version 25
# baseline (speedup 1.0000x reference)
from collections import defaultdict
from contextlib import ExitStack

import numpy as np
import ml_dtypes
import concourse.bass as bass
import concourse.mybir as mybir
from concourse.bass_utils import run_bass_kernel_spmd

F32 = mybir.dt.float32
BF16 = mybir.dt.bfloat16
AF = mybir.ActivationFunctionType
ALU = mybir.AluOpType
bf16 = ml_dtypes.bfloat16

D, H, KD, KV, FFN, C = 1024, 16, 64, 65, 4096, 128
DC, FC = D // 128, FFN // 128
HKV = H * KV
LN_EPS, LAM_MIN, DEN_EPS = 1e-5, 1e-8, 1e-6


class Geo:
    def __init__(self, t_seq=8192, group=8, midgroup=512):
        self.t_seq = t_seq
        self.t_own = t_seq // 2
        self.nc_own = self.t_own // C
        self.group = min(group, self.nc_own)
        assert self.nc_own % self.group == 0
        self.n_groups = self.nc_own // self.group
        self.midgroup = min(midgroup, self.t_own)
        assert self.t_own % self.midgroup == 0
        self.n_mid = self.t_own // self.midgroup


def AP(t, off, pat):
    return bass.AP(t, off, pat)


class Sched:
    ENGS = ("sync", "gpsimd", "tensor", "vector", "scalar")
    PARANOID = False

    def __init__(self):
        self.ops = {e: [] for e in self.ENGS}
        self.ctr = defaultdict(int)
        self.dma_issuer = {}
        self.progress = {}
        self.serial_sems = set()

    def set_progress_sems(self, mapping):
        self.progress = mapping

    def _emit(self, eng, fn):
        self.ops[eng].append(fn)

    DRAIN_ENGS = ("vector", "scalar")

    def step(self, eng, make, sem=None, n=1):
        drain = eng in self.DRAIN_ENGS or self.PARANOID
        if sem is None:
            if drain:
                self._emit(eng, lambda e, mk=make: (mk(e), e.drain()))
            else:
                self._emit(eng, lambda e, mk=make: mk(e))
            return None
        self.ctr[sem.name] += n
        if drain and eng in self.DRAIN_ENGS:
            self._emit(eng, lambda e, mk=make, s=sem, k=n:
                       (mk(e), e.drain().then_inc(s, k)))
        elif drain:
            self._emit(eng, lambda e, mk=make, s=sem, k=n:
                       (mk(e).then_inc(s, k), e.drain()))
        else:
            self._emit(eng, lambda e, mk=make, s=sem, k=n:
                       mk(e).then_inc(s, k))
        return (sem, self.ctr[sem.name])

    def dma(self, eng, dst, src, sem, slow=False):
        self.dma_issuer[sem.name] = (eng, sem)
        if slow:
            return self.step(
                eng, lambda e, d=dst, s=src: e.dma_start(
                    d, s, allow_slow_non_contiguous=True), sem, 16)
        return self.step(eng, lambda e, d=dst, s=src: e.dma_start(d, s),
                         sem, 16)

    def dmat(self, eng, dst, src, sem):
        self.dma_issuer[sem.name] = (eng, sem)
        return self.step(
            eng, lambda e, d=dst, s=src: e.dma_start(d, s, transpose=True),
            sem, 16)

    def val(self, sem):
        return (sem, self.ctr[sem.name])

    def wait(self, eng, ev):
        if ev is None:
            return
        sem, v = ev
        if v <= 0:
            return
        iss = self.dma_issuer.get(sem.name)
        if iss is not None and sem.name in self.serial_sems:
            self._emit(eng, lambda e, s=sem, vv=v: e.wait_ge(s, vv))
            return
        if iss is not None and iss[0] != eng:
            ieng = iss[0]
            full = self.ctr[sem.name]
            self._emit(ieng, lambda e, s=sem, vv=full: e.wait_ge(s, vv))
            pev = self.step(ieng, lambda e: e.nop(), self.progress[ieng])
            self._emit(eng, lambda e, s=pev[0], vv=pev[1]: e.wait_ge(s, vv))
        else:
            self._emit(eng, lambda e, s=sem, vv=v: e.wait_ge(s, vv))

    def barrier(self, evs):
        for eng in self.ENGS:
            for ev in evs:
                self.wait(eng, ev)

    def run_into(self, nc):
        with nc.Block() as block:
            for name in self.ENGS:
                lst = self.ops[name]

                def make_body(lst_):
                    def body(e):
                        for fn in lst_:
                            fn(e)
                    return body

                getattr(block, name)(make_body(lst))


def build_program(g: Geo, n_cores=8, stop_after=None, paranoid=False):
    nc = bass.Bass("TRN2", target_bir_lowering=False,
                   detect_race_conditions=paranoid)
    es = ExitStack()
    sch = Sched()
    sch.PARANOID = paranoid
    inv_d = 1.0 / D

    def din(name, shape, dt=F32):
        return nc.dram_tensor(name, list(shape), dt, kind="ExternalInput")

    x_own = din("x_own", (g.t_own, D))
    x_rem = din("x_rem", (g.t_own, D))
    q_w = din("q_w", (D, D)); q_b = din("q_b", (1, D))
    k_w = din("k_w", (D, D)); k_b = din("k_b", (1, D))
    v_w = din("v_w", (D, D)); v_b = din("v_b", (1, D))
    o_w = din("o_w", (D, D)); o_b = din("o_b", (1, D))
    dec_w = din("dec_w", (D, H)); dec_b = din("dec_b", (1, H))
    ln1_w = din("ln1_w", (1, D)); ln1_b = din("ln1_b", (1, D))
    ln2_w = din("ln2_w", (1, D)); ln2_b = din("ln2_b", (1, D))
    f1_w = din("f1_w", (D, FFN)); f1_b = din("f1_b", (1, FFN))
    f2_w = din("f2_w", (FFN, D)); f2_b = din("f2_b", (1, D))
    w1_w = din("w1_w", (D, FFN)); w1_b = din("w1_b", (1, FFN))
    w2_w = din("w2_w", (FFN, D)); w2_b = din("w2_b", (1, D))
    hmask_in = din("hmask", (1, 2))
    ident_in = din("ident", (128, 128), BF16)
    identf_in = din("identf", (128, 128), F32)
    ones_in = din("onesrow", (1, 128), BF16)
    onesf_in = din("onesrowf", (1, 128), F32)

    out_t = nc.dram_tensor("out", [g.t_own, D], F32, kind="ExternalOutput")

    def dscr(name, shape, dt):
        return nc.dram_tensor(name, list(shape), dt, kind="Internal")

    XN32 = dscr("XN32", (g.t_own, D), F32)
    XNB = dscr("XNB", (g.t_own, D), BF16)
    ZFMs = dscr("ZFMs", (DC, 128, g.t_own), BF16)
    Z2B = dscr("Z2B", (g.t_own, D), BF16)
    PBs = dscr("PBs", (g.t_own, D), F32)
    LROW = dscr("LROW", (g.nc_own, H * C), F32)
    E12ROW = dscr("E12ROW", (g.nc_own, 2 * H * C), BF16)
    PHIKs = dscr("PHIKs", (g.nc_own, 128, D), BF16)
    VREs = dscr("VREs", (g.nc_own, 128, HKV), BF16)
    CQB = dscr("CQB", (1, D), F32)
    CKB = dscr("CKB", (1, D), F32)
    CVB = dscr("CVB", (1, D), F32)
    CDB = dscr("CDB", (1, H), F32)
    CF1B = dscr("CF1B", (1, FFN), F32)
    EXCI = dscr("EXCI", (128, HKV), F32)
    EXCO = dscr("EXCO", (2 * 128, HKV), F32)

    SEM = {}
    for s in ("sx0", "sx1", "sy0", "sy1", "sz0", "sz1", "sw0", "sw1",
              "sp0", "sp1", "sst", "sld", "sout",
              "spe", "sdv", "sac", "sgp", "sqp", "gqp", "scc", "sca", "sw2"):
        SEM[s] = es.enter_context(nc.semaphore(s))
    S = type("S", (), SEM)()
    sch.set_progress_sems({"sync": S.sqp, "gpsimd": S.gqp})
    sch.serial_sems = {"sx0", "sx1", "sy0", "sy1", "sz0", "sz1",
                       "sp0", "sp1"}

    def sb(name, shape, dt, stack=None):
        return (stack or es).enter_context(nc.sbuf_tensor(name, list(shape), dt))

    def ps(name, shape, dt=F32, stack=None):
        return (stack or es).enter_context(nc.psum_tensor(name, list(shape), dt))

    IDENT = sb("IDENT", (128, 128), BF16)
    IDF = sb("IDF", (128, 128), F32)
    ONES1 = sb("ONES1", (1, 128), BF16)
    ONES1F = sb("ONES1F", (1, 128), F32)
    HM = sb("HM", (128, 2), F32)
    OBb = sb("OBb", (128, D), F32)
    F2Bb = sb("F2Bb", (128, D), F32)
    W2Bb = sb("W2Bb", (128, D), F32)
    W1B = sb("W1B", (128, FC), F32)
    CF1 = sb("CF1", (128, FC), F32)
    LW2 = sb("LW2", (128, DC), F32)

    att = ExitStack()
    WD = sb("WD", (128, DC * H), BF16, att)
    NCD = sb("NCD", (H, 1), F32, att)
    LW1 = sb("LW1", (128, DC), F32, att)
    LB1 = sb("LB1", (128, DC), BF16, att)
    E1T = sb("E1T", (128, g.nc_own * H), F32, att)
    E2T = sb("E2T", (128, g.nc_own * H), F32, att)
    LTM = sb("LTM", (128, g.nc_own * H), F32, att)
    GROW = sb("GROW", (1, g.nc_own * H), F32, att)
    RANCH = sb("RANCH", (128, g.n_groups * HKV), BF16, att)
    STATE = sb("STATE", (128, HKV), F32, att)
    GAMR = sb("GAMR", (128, H), F32, att)
    GB = sb("GB", (128, H), F32, att)
    FA = sb("FA", (128, H), F32, att)
    FB = sb("FB", (128, H), F32, att)

    EROW = g.nc_own * H

    def _early_out():
        evd = sch.dma("sync", AP(out_t, 0, [[D, 128], [1, D]]), OBb[:], S.sout)
        sch.wait("sync", evd)
        sch.run_into(nc)
        return nc

    p0s = ExitStack()
    sch.dma("sync", IDENT[:], ident_in.ap(), S.sld)
    sch.dma("sync", IDF[:], identf_in.ap(), S.sld)
    sch.dma("sync", ONES1[:], ones_in.ap(), S.sld)
    sch.dma("sync", ONES1F[:], onesf_in.ap(), S.sld)
    sch.dma("sync", HM[:], AP(hmask_in, 0, [[0, 128], [1, 2]]), S.sld)
    for dst, srct, nn_ in ((OBb, o_b, D), (F2Bb, f2_b, D), (W2Bb, w2_b, D)):
        sch.dma("sync", dst[:], AP(srct, 0, [[0, 128], [1, nn_]]), S.sld)
    sch.dma("sync", W1B[:], AP(w1_b, 0, [[1, 128], [128, FC]]), S.sld, slow=True)
    sch.dma("sync", LW1[:], AP(ln1_w, 0, [[1, 128], [128, DC]]), S.sld, slow=True)
    sch.dma("gpsimd", LB1[:], AP(ln1_b, 0, [[1, 128], [128, DC]]), S.sw0, slow=True)
    sch.dma("sync", LW2[:], AP(ln2_w, 0, [[1, 128], [128, DC]]), S.sld, slow=True)
    LB2 = sb("LB2", (128, DC), BF16, p0s)
    sch.dma("gpsimd", LB2[:], AP(ln2_b, 0, [[1, 128], [128, DC]]), S.sw0, slow=True)
    ev_sld0 = sch.val(S.sld)

    def wload(dst, srct, douts):
        pat = [[douts, 128], [128 * douts, DC], [1, douts]]
        return sch.dma("gpsimd", dst[:], AP(srct, 0, pat), S.sw0)

    wload(WD, dec_w, H)
    ev_w_all = sch.val(S.sw0)

    PF = [ps(f"PF{i}", (1, 512), F32, stack=p0s) for i in range(FFN // 512)]
    TB0 = sb("TB0", (1, FFN), F32, p0s)
    BR0 = sb("BR0", (1, FFN), F32, p0s)
    sch.wait("tensor", ev_w_all)
    sch.wait("tensor", ev_sld0)

    def bias_fold(W, b_t, cdram, douts):
        evv = None
        ev_br = sch.dma("sync", BR0[0:1, 0:douts],
                        AP(b_t, 0, [[0, 1], [1, douts]]), S.sld)
        sch.wait("vector", ev_br)
        sch.wait("vector", sch.val(S.sst))
        for gi in range((douts + 511) // 512):
            nn_ = min(512, douts - gi * 512)
            evm = None
            for dc in range(DC):
                def mk(e, dc=dc, gi=gi, nn_=nn_, W=W, douts=douts):
                    return e.matmul(PF[0][0:1, 0:nn_], LB1[:, dc:dc + 1],
                                    W[:, dc * douts + gi * 512:
                                      dc * douts + gi * 512 + nn_],
                                    start=(dc == 0), stop=(dc == DC - 1))
                evm = sch.step("tensor", mk, S.spe if dc == DC - 1 else None)
            sch.wait("vector", evm)
            def mkv(e, gi=gi, nn_=nn_):
                return e.tensor_tensor(TB0[0:1, gi * 512: gi * 512 + nn_],
                                       PF[0][0:1, 0:nn_],
                                       BR0[0:1, gi * 512: gi * 512 + nn_],
                                       ALU.add)
            evv = sch.step("vector", mkv, S.sdv)
            sch.wait("tensor", evv)
        sch.wait("sync", evv)
        return sch.dma("sync", AP(cdram, 0, [[0, 1], [1, douts]]),
                       TB0[0:1, 0:douts], S.sst)

    evfd = bias_fold(WD, dec_b, CDB, H)

    sch.wait("sync", evfd)
    ev_cd = sch.dma("sync", NCD[:], AP(CDB, 0, [[1, H], [1, 1]]), S.sld)
    sch.wait("vector", ev_cd)
    sch.step("vector", lambda e: e.tensor_scalar_mul(NCD[:], NCD[:], -1.0), S.sdv)

    for W, douts in ((WD, H),):
        for dc in range(DC):
            def mks(e, W=W, dc=dc, douts=douts):
                return e.tensor_scalar_mul(W[:, dc * douts:(dc + 1) * douts],
                                           W[:, dc * douts:(dc + 1) * douts],
                                           LW1[:, dc:dc + 1])
            sch.step("vector", mks, S.sdv)

    sch.step("vector", lambda e: e.memset(FA[64:128, :], 1.0))
    sch.step("vector", lambda e: e.memset(FB[0:64, :], 1.0), S.sdv)

    FST = [sb(f"FST{i}", (128, FFN), BF16, p0s) for i in range(2)]
    evm = None
    for dc in range(DC):
        par = dc % 2
        if dc >= 2 and evm is not None:
            sch.wait("gpsimd", evm)
        evl = sch.dma("gpsimd", FST[par][:],
                      AP(f1_w, dc * 128 * FFN, [[FFN, 128], [1, FFN]]),
                      (S.sw0, S.sw1)[par])
        sch.wait("tensor", evl)
        for gi in range(FFN // 512):
            def mk(e, dc=dc, gi=gi, par=par):
                return e.matmul(PF[gi][0:1, :], LB2[:, dc:dc + 1],
                                FST[par][:, gi * 512:(gi + 1) * 512],
                                start=(dc == 0), stop=(dc == DC - 1))
            evm = sch.step("tensor", mk,
                           S.spe if gi == FFN // 512 - 1 else None)
    ev_br2 = sch.dma("sync", BR0[0:1, :], AP(f1_b, 0, [[0, 1], [1, FFN]]),
                     S.sld)
    sch.wait("vector", evm)
    sch.wait("vector", ev_br2)
    sch.wait("vector", sch.val(S.sst))
    evv2 = None
    for gi in range(FFN // 512):
        def mkv(e, gi=gi):
            return e.tensor_tensor(TB0[0:1, gi * 512:(gi + 1) * 512],
                                   PF[gi][0:1, :],
                                   BR0[0:1, gi * 512:(gi + 1) * 512],
                                   ALU.add)
        evv2 = sch.step("vector", mkv, S.sdv)
    sch.wait("sync", evv2)
    ev_cf = sch.dma("sync", AP(CF1B, 0, [[0, 1], [1, FFN]]), TB0[0:1, :], S.sst)
    sch.wait("sync", ev_cf)
    sch.dma("sync", CF1[:], AP(CF1B, 0, [[1, 128], [128, FC]]), S.sld, slow=True)

    sch.barrier([sch.val(S.sdv), sch.val(S.sld), sch.val(S.spe)])
    p0s.close()
    if stop_after == "p0":
        return _early_out()

    p1s = ExitStack()
    WK = sb("WK", (128, DC * D), BF16, p1s)
    WV = sb("WV", (128, DC * D), BF16, p1s)
    CKb = sb("CKb", (128, D), F32, p1s)
    CVb = sb("CVb", (128, D), F32, p1s)
    TB1 = sb("TB1", (1, D), F32, p1s)
    BR1 = sb("BR1", (1, D), F32, p1s)
    XT = [sb(f"XT{i}", (128, D), F32, p1s) for i in range(2)]
    XNT = [sb(f"XNT{i}", (128, D), F32, p1s) for i in range(2)]
    ZT = [sb(f"ZT{i}", (128, D), BF16, p1s) for i in range(2)]
    ZF = [sb(f"ZF{i}", (128, D), BF16, p1s) for i in range(2)]
    XNBT_ = sb("XNBT", (128, D), BF16, p1s)
    XNBT = [XNBT_, XNBT_]
    SQ = [sb(f"SQ{i}", (128, D), F32, p1s) for i in range(2)]
    ST1 = [sb(f"ST1{i}", (128, 8), F32, p1s) for i in range(2)]
    PHT = [sb(f"PHT{i}", (128, D), F32, p1s) for i in range(2)]
    PHE = [sb(f"PHE{i}", (128, D), F32, p1s) for i in range(2)]
    PHB = [sb(f"PHB{i}", (128, D), BF16, p1s) for i in range(2)]
    VT = [sb(f"VT{i}", (128, HKV), BF16, p1s) for i in range(2)]
    GSTORE = sb("GSTORE", (128, 2 * g.n_groups * H * (KV + 1)), BF16, p1s)
    GSW = 2 * g.n_groups * H * (KV + 1)
    LAM = [sb(f"LAM{i}", (H, C), F32, p1s) for i in range(2)]
    LLF = [sb(f"LLF{i}", (H, C), F32, p1s) for i in range(2)]
    LF = [sb(f"LF{i}", (H, C), F32, p1s) for i in range(2)]
    ZROF = sb("ZROF", (H, C), F32, p1s)
    E1F = [sb(f"E1F{i}", (H, C), F32, p1s) for i in range(2)]
    E2F = [sb(f"E2F{i}", (H, C), F32, p1s) for i in range(2)]
    ETT = [sb(f"ETT{i}", (128, 48), F32, p1s) for i in range(2)]
    GR1 = [sb(f"GR1{i}", (1, H), F32, p1s) for i in range(2)]
    GB = [sb(f"GBp{i}", (128, H), F32, p1s) for i in range(2)]
    KB = [sb(f"KBp{i}", (128, H * 128), BF16, p1s) for i in range(2)]
    UW = [sb(f"UWp{i}", (128, HKV), F32, p1s) for i in range(2)]
    TTMP = sb("TTMP", (128, HKV), F32, p1s)

    PTZ = ps("PTZ", (128, 1024), BF16, stack=p1s)
    PKV = ps("PKV", (128, 512), F32, stack=p1s)
    PKV2 = ps("PKV2", (128, 512), F32, stack=p1s)
    PDEC = ps("PDEC", (H, C), F32, stack=p1s)
    PUW1 = ps("PUW1", (128, 7 * KV), F32, stack=p1s)
    PUW2 = ps("PUW2", (128, 7 * KV), F32, stack=p1s)
    PUW3 = ps("PUW3", (128, 2 * KV), F32, stack=p1s)
    PMIS = ps("PMIS", (128, 64), F32, stack=p1s)

    def ln_affine(x_ap, out_ap, SQ_, ST_, tmp_ap=None):
        sch.step("vector",
                 lambda e: e.tensor_scalar(SQ_[:], x_ap, 1.0, None, ALU.mult,
                                           ALU.add, accum_out=ST_[:, 0:1]),
                 S.sdv)
        sch.step("vector",
                 lambda e: e.scalar_tensor_tensor(SQ_[:], x_ap, 1.0, x_ap,
                                                  ALU.mult, ALU.mult,
                                                  accum_out=ST_[:, 1:2]), S.sdv)
        sch.step("vector", lambda e: e.tensor_scalar_mul(ST_[:, 2:3], ST_[:, 0:1],
                                                         inv_d))
        sch.step("vector", lambda e: e.tensor_tensor(ST_[:, 3:4], ST_[:, 2:3],
                                                     ST_[:, 2:3], ALU.mult))
        sch.step("vector",
                 lambda e: e.scalar_tensor_tensor(ST_[:, 3:4], ST_[:, 1:2], inv_d,
                                                  ST_[:, 3:4], ALU.mult,
                                                  ALU.subtract))
        sch.step("vector", lambda e: e.tensor_scalar_add(ST_[:, 3:4],
                                                         ST_[:, 3:4], LN_EPS))
        ev3 = sch.step("vector",
                       lambda e: e.tensor_scalar_mul(ST_[:, 5:6], ST_[:, 2:3],
                                                     -1.0), S.sdv)
        sch.wait("scalar", ev3)
        sch.step("scalar", lambda e: e.activation(ST_[:, 4:5], ST_[:, 3:4],
                                                  AF.Ln))
        sch.step("scalar", lambda e: e.activation(ST_[:, 4:5], ST_[:, 4:5],
                                                  AF.Exp, scale=-0.5))
        t_ap = tmp_ap if tmp_ap is not None else SQ_[:]
        sch.step("scalar", lambda e: e.activation(t_ap, x_ap, AF.Identity,
                                                  bias=ST_[:, 5:6]))
        return sch.step("scalar",
                        lambda e: e.activation(out_ap, t_ap, AF.Identity,
                                               scale=ST_[:, 4:5]), S.sac)

    EV = {}

    def p1_stageA(ci, par, x_t, store):
        prev = EV.get(("Aend", ci - 2))
        if prev is not None:
            sch.wait("sync", prev)
        ev_x = sch.dma("sync", XT[par][:],
                       AP(x_t, ci * C * D, [[D, 128], [1, D]]),
                       (S.sx0, S.sx1)[par])
        sch.wait("vector", ev_x)
        sch.wait("scalar", ev_x)
        pz = EV.get(("zf", ci - 2))
        if pz is not None:
            sch.wait("scalar", pz)
        if store:
            sch.wait("scalar", sch.val(S.sst))
            sch.wait("vector", sch.val(S.sst))
            sch.wait("scalar", sch.val(S.sgp))
        ev_xn = ln_affine(x_tile_ap(par), XNT[par][:], SQ[par], ST1[par])
        sch.wait("vector", ev_xn)
        ev_z = ln_affine(XNT[par][:], ZT[par][:], SQ[par], ST1[par])
        if store:
            sch.wait("sync", ev_xn)
            sch.dma("sync", AP(XN32, ci * C * D, [[D, 128], [1, D]]),
                    XNT[par][:], S.sst)
            ev_xb = sch.step("scalar",
                             lambda e, par=par: e.activation(
                                 XNBT[par][:], XNT[par][:], AF.Identity),
                             S.sac)
            sch.wait("gpsimd", ev_xb)
            sch.dma("gpsimd", AP(XNB, ci * C * D, [[D, 128], [1, D]]),
                    XNBT[par][:], S.sgp)
        EV[("x", ci)] = ev_x
        EV[("z", ci)] = ev_z
        EV[("Aend", ci)] = ev_z

    def x_tile_ap(par):
        return XT[par][:]

    def p1_stageB(ci, par, store):
        ev_z = EV[("z", ci)]
        sch.wait("tensor", ev_z)
        pz = EV.get(("zf", ci - 1))
        if pz is not None:
            sch.wait("tensor", pz)
        evt = None
        for dc in range(DC):
            def mkt(e, dc=dc, par=par):
                return e.transpose(PTZ[:, dc * 128:(dc + 1) * 128],
                                   ZT[par][:, dc * 128:(dc + 1) * 128],
                                   IDENT[:])
            evt = sch.step("tensor", mkt, S.spe if dc == DC - 1 else None)
        pm = EV.get(("mmlast", ci - 2))
        if pm is not None:
            sch.wait("vector", pm)
        sch.wait("vector", evt)
        ev_zf = sch.step("vector",
                         lambda e, par=par: e.tensor_copy(ZF[par][:], PTZ[:]),
                         S.sdv)
        EV[("zf", ci)] = ev_zf
        if store:
            sch.wait("sync", ev_zf)
            sch.dma("sync",
                    AP(ZFMs, ci * C, [[g.t_own, 128], [128 * g.t_own, DC],
                                      [1, C]]),
                    ZF[par][:], S.sst)

        pc = EV.get(("kb", ci - 2))
        if pc is not None:
            sch.wait("vector", pc)
        pu = EV.get(("uwmm", ci - 2))
        if pu is not None:
            sch.wait("vector", pu)

        sch.wait("tensor", ev_zf)
        ev_phik = None
        for gi in range(2):
            bank = (PKV, PKV2)[gi]
            evm = None
            for dc in range(DC):
                def mk(e, dc=dc, gi=gi, bank=bank, par=par):
                    return e.matmul(bank[:, :],
                                    ZF[par][:, dc * 128:(dc + 1) * 128],
                                    WK[:, dc * D + gi * 512:
                                       dc * D + gi * 512 + 512],
                                    start=(dc == 0), stop=(dc == DC - 1))
                evm = sch.step("tensor", mk, S.spe if dc == DC - 1 else None)
            sch.wait("vector", evm)
            def mk1(e, gi=gi, bank=bank, par=par):
                return e.tensor_tensor(PHT[par][:, gi * 512:(gi + 1) * 512],
                                       bank[:, :],
                                       CKb[:, gi * 512:(gi + 1) * 512], ALU.add)
            ev1 = sch.step("vector", mk1, S.sdv)
            sch.wait("tensor", ev1)
            sch.wait("scalar", ev1)
            def mk2(e, gi=gi, par=par):
                return e.activation(PHE[par][:, gi * 512:(gi + 1) * 512],
                                    PHT[par][:, gi * 512:(gi + 1) * 512],
                                    AF.Exp)
            ev2 = sch.step("scalar", mk2, S.sac)
            def mk4(e, gi=gi, par=par):
                return e.activation(PHT[par][:, gi * 512:(gi + 1) * 512],
                                    PHT[par][:, gi * 512:(gi + 1) * 512],
                                    AF.Relu)
            ev2b = sch.step("scalar", mk4, S.sac)
            sch.wait("vector", ev2b)
            def mk3(e, gi=gi, par=par):
                return e.tensor_scalar_min(
                    PHE[par][:, gi * 512:(gi + 1) * 512],
                    PHE[par][:, gi * 512:(gi + 1) * 512], 1.0)
            sch.step("vector", mk3)
            def mk5(e, gi=gi, par=par):
                return e.tensor_tensor(
                    AP(PHB[par], gi * 512, [[D, 128], [1, 512]]),
                    PHE[par][:, gi * 512:(gi + 1) * 512],
                    PHT[par][:, gi * 512:(gi + 1) * 512], ALU.add)
            ev_phik = sch.step("vector", mk5, S.sdv)

        for gi in range(2):
            bank = (PKV, PKV2)[gi]
            evm = None
            for dc in range(DC):
                def mk(e, dc=dc, gi=gi, bank=bank, par=par):
                    return e.matmul(bank[:, :],
                                    ZF[par][:, dc * 128:(dc + 1) * 128],
                                    WV[:, dc * D + gi * 512:
                                       dc * D + gi * 512 + 512],
                                    start=(dc == 0), stop=(dc == DC - 1))
                evm = sch.step("tensor", mk, S.spe if dc == DC - 1 else None)
            sch.wait("vector", evm)
            def mkv(e, gi=gi, bank=bank, par=par):
                dst = AP(VT[par], gi * 8 * KV, [[HKV, 128], [KV, 8], [1, KD]])
                srcp = AP(bank, 0, [[512, 128], [KD, 8], [1, KD]])
                bias = AP(CVb, gi * 512, [[D, 128], [KD, 8], [1, KD]])
                return e.tensor_tensor(dst, srcp, bias, ALU.add)
            ev_v = sch.step("vector", mkv, S.sdv)
            sch.wait("tensor", ev_v)

        if store:
            sch.wait("sync", ev_phik)
            sch.wait("sync", sch.val(S.sdv))
            sch.dma("sync", AP(PHIKs, ci * 128 * D, [[D, 128], [1, D]]),
                    PHB[par][:], S.sst)
            sch.dma("sync", AP(VREs, ci * 128 * HKV, [[HKV, 128], [1, HKV]]),
                    VT[par][:], S.sst)

        evm = None
        for dc in range(DC):
            def mk(e, dc=dc, par=par):
                return e.matmul(PDEC[:, :], WD[:, dc * H:(dc + 1) * H],
                                ZF[par][:, dc * 128:(dc + 1) * 128],
                                start=(dc == 0), stop=(dc == DC - 1))
            evm = sch.step("tensor", mk, S.spe if dc == DC - 1 else None)
        EV[("mmlast", ci)] = evm
        sch.wait("scalar", evm)
        ev1 = sch.step("scalar",
                       lambda e, par=par: e.activation(LAM[par][:], PDEC[:, :],
                                                       AF.Exp, scale=-1.0,
                                                       bias=NCD[:]), S.sac)
        sch.wait("vector", ev1)
        sch.wait("tensor", ev1)
        sch.step("vector", lambda e, par=par: e.tensor_scalar_add(
            LAM[par][:], LAM[par][:], 1.0))
        sch.step("vector", lambda e, par=par: e.reciprocal(LAM[par][:],
                                                           LAM[par][:]))
        ev2 = sch.step("vector",
                       lambda e, par=par: e.tensor_scalar_max(
                           LAM[par][:], LAM[par][:], LAM_MIN), S.sdv)
        sch.wait("scalar", ev2)
        ev3 = sch.step("scalar",
                       lambda e, par=par: e.activation(LLF[par][:], LAM[par][:],
                                                       AF.Ln), S.sac)
        sch.wait("vector", ev3)
        ev4 = sch.step("vector",
                       lambda e, par=par: e.tensor_tensor_scan(
                           LF[par][:], LLF[par][:], ZROF[:], 0.0,
                           ALU.add, ALU.add), S.sdv)
        sch.wait("scalar", ev4)
        if store:
            pe12 = EV.get(("e12row", ci - 2))
            if pe12 is not None:
                sch.wait("scalar", pe12)
        ev5 = sch.step("scalar",
                       lambda e, par=par: e.activation(
                           E1F[par][:], LF[par][:], AF.Exp, scale=-1.0,
                           bias=LF[par][:, C - 1:C]), S.sac)
        ev6 = sch.step("scalar",
                       lambda e, par=par: e.activation(E2F[par][:], LF[par][:],
                                                       AF.Exp), S.sac)
        if store:
            sch.wait("gpsimd", ev6)
            sch.dma("gpsimd", AP(E12ROW, ci * 2 * H * C, [[0, 1], [1, H * C]]),
                    E1F[par][:, :], S.sgp)
            ev_e12 = sch.dma("gpsimd",
                             AP(E12ROW, ci * 2 * H * C + H * C,
                                [[0, 1], [1, H * C]]),
                             E2F[par][:, :], S.sgp)
            EV[("e12row", ci)] = ev_e12
        sch.wait("tensor", ev6)
        pg = EV.get(("gb", ci - 1))
        if pg is not None:
            sch.wait("tensor", pg)
        ev7a = sch.step("tensor", lambda e, par=par: e.transpose(
            PMIS[:, 0:H], E1F[par][:], IDF[0:H, 0:H]), S.spe)
        ev7b = sch.step("tensor", lambda e, par=par: e.transpose(
            PMIS[:, H:2 * H], E2F[par][:], IDF[0:H, 0:H]), S.spe)
        ev7c = sch.step("tensor", lambda e, par=par: e.transpose(
            PMIS[:, 2 * H:3 * H], LF[par][:], IDF[0:H, 0:H]), S.spe)
        sch.wait("vector", ev7c)
        if store:
            def mke(e, ci=ci):
                return e.tensor_copy(
                    AP(E1T, ci * H, [[EROW, 128], [1, H]]), PMIS[:, 0:H])
            sch.step("vector", mke)
            def mke2(e, ci=ci):
                return e.tensor_copy(
                    AP(E2T, ci * H, [[EROW, 128], [1, H]]), PMIS[:, H:2 * H])
            sch.step("vector", mke2)
            def mke3(e, ci=ci):
                return e.tensor_copy(
                    AP(LTM, ci * H, [[EROW, 128], [1, H]]), PMIS[:, 2 * H:3 * H])
            ev9 = sch.step("vector", mke3, S.sdv)
            sch.wait("sync", ev4)
            sch.dma("sync", AP(LROW, ci * H * C, [[0, 1], [1, H * C]]),
                    LF[par][:, :], S.sst)
        else:
            ev9 = sch.step("vector",
                           lambda e, par=par: e.tensor_copy(
                               ETT[par][:, 0:3 * H], PMIS[:, 0:3 * H]), S.sdv)
        EV[("e9", ci)] = ev9

        ev10 = sch.step("tensor",
                        lambda e, par=par: e.transpose(
                            PMIS[0:1, 3 * H:3 * H + H],
                            E2F[par][:, C - 1:C], IDF[0:H, 0:H]), S.spe)
        sch.wait("vector", ev10)
        ev11 = sch.step("vector",
                        lambda e, par=par: e.tensor_copy(
                            GR1[par][:], PMIS[0:1, 3 * H:3 * H + H]), S.sdv)
        if store:
            def mkg(e, ci=ci, par=par):
                return e.tensor_copy(GROW[0:1, ci * H:(ci + 1) * H], GR1[par][:])
            sch.step("vector", mkg)
        sch.wait("tensor", ev11)
        ev12 = sch.step("tensor",
                        lambda e, par=par: e.matmul(
                            PMIS[:, 3 * H:4 * H], ONES1F[0:1, :],
                            GR1[par][:], start=True, stop=True), S.spe)
        sch.wait("vector", ev12)
        ev_gb = sch.step("vector",
                         lambda e, par=par: e.tensor_copy(
                             GB[par][:], PMIS[:, 3 * H:4 * H]), S.sdv)
        sch.wait("tensor", ev_gb)
        EV[("gb", ci)] = ev_gb
        EV[("phik", ci)] = ev_phik

    def p1_stageC(ci, par, store, part):
        if store:
            e1_t, e1_off, e1_row = E1T, ci * H, EROW
            e2_t, e2_off, e2_row = E2T, ci * H, EROW
        else:
            e1_t, e1_off, e1_row = ETT[par], 0, 48
            e2_t, e2_off, e2_row = ETT[par], H, 48
        def mkk1(e, par=par, e1_t=e1_t, e1_off=e1_off, e1_row=e1_row):
            dst = AP(KB[par], 0, [[H * 128, 128], [128, H], [1, KD]])
            srcp = AP(PHB[par], 0, [[D, 128], [KD, H], [1, KD]])
            fac = AP(e1_t, e1_off, [[e1_row, 128], [1, H], [0, KD]])
            return e.tensor_tensor(dst, srcp, fac, ALU.mult)
        sch.step("vector", mkk1)
        def mkk2(e, par=par, e2_t=e2_t, e2_off=e2_off, e2_row=e2_row):
            dst = AP(KB[par], KD, [[H * 128, 128], [128, H], [1, KD]])
            srcp = AP(PHB[par], 0, [[D, 128], [KD, H], [1, KD]])
            fac = AP(e2_t, e2_off, [[e2_row, 128], [1, H], [0, KD]])
            return e.tensor_tensor(dst, srcp, fac, ALU.mult)
        evk2 = sch.step("vector", mkk2, S.sdv)
        EV[("kb", ci)] = evk2
        sch.wait("tensor", evk2)
        pe_prev = EV.get(("uwev", ci - 1))
        if pe_prev is not None:
            sch.wait("tensor", pe_prev)
        uw_banks = [(PUW1, 0, 7), (PUW2, 7, 7), (PUW3, 14, 2)]
        ev_uw = None
        for bank, h0, nh in uw_banks:
            for hh in range(nh):
                h = h0 + hh
                def mk(e, h=h, hh=hh, bank=bank, par=par):
                    rhs = AP(VT[par], h * KV, [[HKV, 128], [1, KV]])
                    return e.matmul(bank[:, hh * KV:(hh + 1) * KV],
                                    KB[par][:, h * 128:(h + 1) * 128], rhs,
                                    start=True, stop=True)
                ev_uw = sch.step("tensor", mk, S.spe if h == H - 1 else None)
        EV[("uwmm", ci)] = ev_uw
        sch.wait("scalar", ev_uw)
        sch.step("scalar", lambda e, par=par: e.activation(
            UW[par][:, 0:7 * KV], PUW1[:, :], AF.Identity))
        sch.step("scalar", lambda e, par=par: e.activation(
            UW[par][:, 7 * KV:14 * KV], PUW2[:, :], AF.Identity))
        evc = sch.step("scalar", lambda e, par=par: e.activation(
            UW[par][:, 14 * KV:16 * KV], PUW3[:, :], AF.Identity), S.sac)
        EV[("uwev", ci)] = evc
        sch.wait("vector", evc)
        sch.wait("vector", EV[("gb", ci)])

        sch.step("vector", lambda e, par=par: e.tensor_copy(FA[0:64, :],
                                                            GB[par][0:64, :]))
        sch.step("vector", lambda e: e.tensor_copy(FB[64:128, :],
                                                   GAMR[64:128, :]))
        def mk_t(e, par=par):
            fac = AP(FB, 0, [[H, 128], [1, H], [0, KV]])
            return e.tensor_tensor(TTMP[:, :], UW[par][:, :], fac, ALU.mult)
        sch.step("vector", mk_t)
        def mk_s(e):
            fac = AP(FA, 0, [[H, 128], [1, H], [0, KV]])
            return e.tensor_tensor(STATE[:, :], STATE[:, :], fac, ALU.mult)
        sch.step("vector", mk_s)
        sch.step("vector", lambda e: e.tensor_tensor(STATE[:, :], STATE[:, :],
                                                     TTMP[:, :], ALU.add))
        ev_end = sch.step("vector",
                          lambda e, par=par: e.tensor_tensor(
                              GAMR[64:128, :], GAMR[64:128, :],
                              GB[par][64:128, :], ALU.mult), S.sdv)
        if (ci + 1) % g.group == 0:
            gidx = ci // g.group
            base = (0 if part == "own" else g.n_groups) + gidx
            off = base * (H * (KV + 1))
            def mg1(e, off=off):
                return e.tensor_copy(GSTORE[64:128, off:off + HKV],
                                     STATE[64:128, :])
            sch.step("vector", mg1)
            def mg2(e, off=off):
                return e.tensor_copy(
                    GSTORE[64:128, off + HKV:off + HKV + H], GAMR[64:128, :])
            sch.step("vector", mg2)
            sch.step("vector", lambda e: e.memset(STATE[64:128, :], 0.0))
            ev_end = sch.step("vector",
                              lambda e: e.memset(GAMR[64:128, :], 1.0), S.sdv)
        EV[("Cend", ci)] = ev_end

    def pass1_part(part, x_t, store):
        for _eng in ("vector", "scalar", "tensor"):
            sch.wait(_eng, sch.val(S.spe))
            sch.wait(_eng, sch.val(S.sdv))
            sch.wait(_eng, sch.val(S.sac))
        sch.wait("vector", sch.val(S.sst))
        sch.step("vector", lambda e: e.memset(VT[0][:], 1.0))
        sch.step("vector", lambda e: e.memset(VT[1][:], 1.0))
        sch.step("vector", lambda e: e.memset(STATE[:], 0.0))
        sch.step("vector", lambda e: e.memset(GAMR[64:128, :], 1.0))
        sch.step("vector", lambda e: e.memset(ZROF[:], 0.0), S.sdv)
        EV.clear()
        n = g.nc_own
        for i in range(n + 2):
            if i < n:
                p1_stageA(i, i % 2, x_t, store)
            if 0 <= i - 1 < n:
                p1_stageB(i - 1, (i - 1) % 2, store)
            if 0 <= i - 2 < n:
                p1_stageC(i - 2, (i - 2) % 2, store, part)

    ev_wk = wload(WK, k_w, D)
    ev_wv = wload(WV, v_w, D)
    sch.wait("tensor", ev_wv)

    def bias_fold2(W, b_t, cdram, ckbuf):
        ev_br = sch.dma("sync", BR1[0:1, :], AP(b_t, 0, [[0, 1], [1, D]]),
                        S.sld)
        sch.wait("vector", ev_br)
        evv = None
        for gi in range(2):
            evm = None
            for dc in range(DC):
                def mk(e, dc=dc, gi=gi, W=W):
                    return e.matmul(PKV[0:1, :], LB1[:, dc:dc + 1],
                                    W[:, dc * D + gi * 512:
                                      dc * D + gi * 512 + 512],
                                    start=(dc == 0), stop=(dc == DC - 1))
                evm = sch.step("tensor", mk, S.spe if dc == DC - 1 else None)
            sch.wait("vector", evm)
            def mkv(e, gi=gi):
                return e.tensor_tensor(TB1[0:1, gi * 512:(gi + 1) * 512],
                                       PKV[0:1, :],
                                       BR1[0:1, gi * 512:(gi + 1) * 512],
                                       ALU.add)
            evv = sch.step("vector", mkv, S.sdv)
            sch.wait("tensor", evv)
        sch.wait("sync", evv)
        evs = sch.dma("sync", AP(cdram, 0, [[0, 1], [1, D]]), TB1[0:1, :],
                      S.sst)
        sch.wait("sync", evs)
        evl = sch.dma("sync", ckbuf[:], AP(cdram, 0, [[0, 128], [1, D]]),
                      S.sld)
        return evl

    ev_ck = bias_fold2(WK, k_b, CKB, CKb)
    ev_cv = bias_fold2(WV, v_b, CVB, CVb)
    for W in (WK, WV):
        for dc in range(DC):
            def mks(e, W=W, dc=dc):
                return e.tensor_scalar_mul(W[:, dc * D:(dc + 1) * D],
                                           W[:, dc * D:(dc + 1) * D],
                                           LW1[:, dc:dc + 1])
            sch.step("vector", mks, S.sdv)
    sch.barrier([sch.val(S.sdv), sch.val(S.sld)])

    pass1_part("own", x_own, True)

    sch.step("vector", lambda e: e.memset(TTMP[64:128, :], 0.0))
    def roll_groups_own():
        for gidx in reversed(range(g.n_groups)):
            off = gidx * (H * (KV + 1))
            def mk1(e, off=off):
                fac = AP(GSTORE, 64 * GSW + off + HKV,
                         [[GSW, 64], [1, H], [0, KV]])
                return e.tensor_tensor(TTMP[64:128, :], TTMP[64:128, :], fac,
                                       ALU.mult)
            sch.step("vector", mk1)
            def mk2(e, off=off):
                return e.tensor_tensor(TTMP[64:128, :], TTMP[64:128, :],
                                       GSTORE[64:128, off:off + HKV], ALU.add)
            sch.step("vector", mk2)
    roll_groups_own()
    ev_blob = sch.step("vector", lambda e: e.engine_nop(), S.sdv)
    sch.wait("gpsimd", ev_blob)
    sch.dma("gpsimd", AP(EXCI, 0, [[HKV, 64], [1, HKV]]), STATE[0:64, :],
            S.sgp)
    ev_exp = sch.dma("gpsimd", AP(EXCI, 64 * HKV, [[HKV, 64], [1, HKV]]),
                     TTMP[64:128, :], S.sgp)
    sch.ops["gpsimd"].append(lambda e, v=sch.ctr["sgp"]: e.wait_ge(S.sgp, v))
    groups = [[2 * i, 2 * i + 1] for i in range(max(1, n_cores // 2))]
    ev_cc = sch.step(
        "gpsimd",
        lambda e: e.collective_compute(
            "AllGather", ALU.bypass, replica_groups=groups,
            ins=[EXCI.ap().opt()], outs=[EXCO.ap().opt()]), S.scc)
    sch.wait("sync", ev_cc)
    sch.dma("sync", TTMP[0:64, :], AP(EXCO, 0, [[HKV, 64], [1, HKV]]), S.sld)
    ev_im = sch.dma("sync", TTMP[64:128, :],
                    AP(EXCO, 128 * HKV + 64 * HKV, [[HKV, 64], [1, HKV]]),
                    S.sld)
    sch.wait("vector", ev_im)
    sch.step("vector",
             lambda e: e.tensor_scalar_mul(STATE[0:64, :], TTMP[0:64, :],
                                           HM[0:64, 0:1]))
    sch.step("vector", lambda e: e.tensor_scalar_mul(TTMP[64:128, :],
                                                     TTMP[64:128, :],
                                                     HM[64:128, 1:2]))
    for gidx in reversed(range(g.n_groups)):
        def mka(e, gidx=gidx):
            return e.tensor_copy(RANCH[64:128, gidx * HKV:(gidx + 1) * HKV],
                                 TTMP[64:128, :])
        sch.step("vector", mka)
        if gidx > 0:
            off = gidx * (H * (KV + 1))
            def mk1(e, off=off):
                fac = AP(GSTORE, 64 * GSW + off + HKV,
                         [[GSW, 64], [1, H], [0, KV]])
                return e.tensor_tensor(TTMP[64:128, :], TTMP[64:128, :], fac,
                                       ALU.mult)
            sch.step("vector", mk1)
            def mk2(e, off=off):
                return e.tensor_tensor(TTMP[64:128, :], TTMP[64:128, :],
                                       GSTORE[64:128, off:off + HKV], ALU.add)
            sch.step("vector", mk2)
    ev_p1 = sch.step("vector", lambda e: e.engine_nop(), S.sdv)
    sch.barrier([ev_p1, sch.val(S.sst), sch.val(S.spe), sch.val(S.sac),
                 sch.val(S.sgp)])
    p1s.close()
    if stop_after == "p1":
        return _early_out()

    pas = ExitStack()
    WQ = sb("WQ", (128, DC * D), BF16, pas)
    WO = sb("WO", (128, DC * D), BF16, pas)
    CQ = sb("CQ", (128, DC), F32, pas)
    TB2 = sb("TB2", (1, D), F32, pas)
    BR2 = sb("BR2", (1, D), F32, pas)
    XT2_ = sb("XT2", (128, D), F32, pas)
    XT2 = [XT2_, XT2_]
    XN2 = [sb(f"XN2{i}", (128, D), F32, pas) for i in range(2)]
    ZF2 = [sb(f"ZF2{i}", (128, D), BF16, pas) for i in range(2)]
    LB4 = [sb(f"LB4{i}", (128, H * C), F32, pas) for i in range(2)]
    PKT = [sb(f"PKT{i}", (128, D), BF16, pas) for i in range(2)]
    PVT = [sb(f"PVT{i}", (128, HKV), BF16, pas) for i in range(2)]
    PHQ = [sb(f"PHQ{i}", (128, D), BF16, pas) for i in range(2)]
    PHQ2 = [sb(f"PHQ2{i}", (64, H * 128), BF16, pas) for i in range(2)]
    PHKF2 = [sb(f"PHKF2{i}", (64, H * 128), BF16, pas) for i in range(2)]
    ATT = [sb(f"ATT{i}", (128, D), BF16, pas) for i in range(2)]
    TQ = sb("TQ", (128, D), BF16, pas)
    TE = sb("TE", (128, D), BF16, pas)
    DT = sb("DT", (128, 512), F32, pas)
    ABB = [sb(f"ABB{i}", (128, 512), BF16, pas) for i in range(2)]
    E12B_ = sb("E12B", (64, 2 * H * C), BF16, pas)
    E12B = [E12B_, E12B_]
    PHQA2 = sb("PHQA2", (64, H * 128), BF16, pas)
    PHQB2 = sb("PHQB2", (64, H * 128), BF16, pas)
    RD8 = sb("RD8", (128, 4), F32, pas)
    ATF = sb("ATF", (128, D), BF16, pas)
    YT = sb("YT", (128, D), F32, pas)
    Z2T = sb("Z2T", (128, D), BF16, pas)
    ST2 = sb("ST2", (128, 8), F32, pas)
    RBUF = sb("RBUF", (128, g.group * HKV), BF16, pas)
    KB2 = sb("KB2", (128, H * 128), BF16, pas)
    SBF = sb("SBF", (128, HKV), BF16, pas)
    GKT = sb("GKT", (128, D), BF16, pas)
    GVT = sb("GVT", (128, HKV), BF16, pas)
    UW2 = sb("UW2", (128, HKV), F32, pas)
    GB2 = sb("GB2", (128, H), F32, pas)

    PQ = [ps(f"PQ{i}", (128, 512), F32, stack=pas) for i in range(2)]
    PTR2 = ps("PTR2", (128, 1024), BF16, stack=pas)
    PSC = ps("PSC", (128, 512), F32, stack=pas)
    PG = ps("PG", (128, 512), F32, stack=pas)
    PO = [ps(f"PO{i}", (128, 4 * KV), F32, stack=pas) for i in range(2)]

    def ln_affine2(x_ap, out_ap, junk):
        return ln_affine(x_ap, out_ap, junk, ST2, tmp_ap=x_ap)

    def gamma_bcast(ci, evgate=None):
        if evgate is not None:
            sch.wait("tensor", evgate)
        def mk(e, ci=ci):
            return e.matmul(PO[1][:, 0:H], ONES1F[0:1, :],
                            GROW[0:1, ci * H:(ci + 1) * H],
                            start=True, stop=True)
        ev = sch.step("tensor", mk, S.spe)
        sch.wait("vector", ev)
        evc = sch.step("vector", lambda e: e.tensor_copy(GB2[:], PO[1][:, 0:H]),
                       S.sdv)
        sch.wait("tensor", evc)
        return evc

    def recompute_uw(ci, want_u, want_w, kt, vt):
        def mkk1(e, ci=ci, kt=kt):
            dst = AP(KB2, 0, [[H * 128, 128], [128, H], [1, KD]])
            srcp = AP(kt, 0, [[D, 128], [KD, H], [1, KD]])
            fac = AP(E1T, ci * H, [[EROW, 128], [1, H], [0, KD]])
            return e.tensor_tensor(dst, srcp, fac, ALU.mult)
        def mkk2(e, ci=ci, kt=kt):
            dst = AP(KB2, KD, [[H * 128, 128], [128, H], [1, KD]])
            srcp = AP(kt, 0, [[D, 128], [KD, H], [1, KD]])
            fac = AP(E2T, ci * H, [[EROW, 128], [1, H], [0, KD]])
            return e.tensor_tensor(dst, srcp, fac, ALU.mult)
        sch.step("vector", mkk1)
        evk = sch.step("vector", mkk2, S.sdv)
        sch.wait("tensor", evk)
        banks = [(PSC, 0, 7, 0), (PG, 7, 7, 0), (PO[0], 14, 2, 0)]
        ev_uw = None
        for bank, h0, nh, co in banks:
            for hh in range(nh):
                h = h0 + hh
                def mk(e, h=h, hh=hh, bank=bank, vt=vt):
                    rhs = AP(vt, h * KV, [[HKV, 128], [1, KV]])
                    return e.matmul(bank[:, hh * KV:(hh + 1) * KV],
                                    KB2[:, h * 128:(h + 1) * 128], rhs,
                                    start=True, stop=True)
                ev_uw = sch.step("tensor", mk, S.spe if h == H - 1 else None)
        sch.wait("scalar", ev_uw)
        ev = None
        rows = []
        if want_u:
            rows.append((0, 64))
        if want_w:
            rows.append((64, 128))
        for r0, r1 in rows:
            sch.step("scalar", lambda e, r0=r0, r1=r1:
                     e.activation(UW2[r0:r1, 0:7 * KV], PSC[r0:r1, 0:7 * KV],
                                  AF.Identity))
            sch.step("scalar", lambda e, r0=r0, r1=r1:
                     e.activation(UW2[r0:r1, 7 * KV:14 * KV],
                                  PG[r0:r1, 0:7 * KV], AF.Identity))
            ev = sch.step("scalar", lambda e, r0=r0, r1=r1:
                          e.activation(UW2[r0:r1, 14 * KV:16 * KV],
                                       PO[0][r0:r1, 0:2 * KV], AF.Identity),
                          S.sac)
        sch.wait("tensor", ev)
        sch.wait("vector", ev)
        return ev

    xs = (S.sx0, S.sx1)
    ys = (S.sy0, S.sy1)
    zs = (S.sz0, S.sz1)
    EV2 = {}
    ev_sb0 = sch.step("vector",
                      lambda e: e.tensor_copy(SBF[0:64, :], STATE[0:64, :]),
                      S.sdv)
    sch.wait("tensor", ev_sb0)

    ev_wq = wload(WQ, q_w, D)
    ev_wo = wload(WO, o_w, D)
    sch.wait("tensor", ev_wq)
    ev_br = sch.dma("sync", BR2[0:1, :], AP(q_b, 0, [[0, 1], [1, D]]), S.sld)
    sch.wait("vector", ev_br)
    evv = None
    for gi in range(2):
        evm = None
        for dc in range(DC):
            def mk(e, dc=dc, gi=gi):
                return e.matmul(PQ[0][0:1, :], LB1[:, dc:dc + 1],
                                WQ[:, dc * D + gi * 512:
                                   dc * D + gi * 512 + 512],
                                start=(dc == 0), stop=(dc == DC - 1))
            evm = sch.step("tensor", mk, S.spe if dc == DC - 1 else None)
        sch.wait("vector", evm)
        def mkv(e, gi=gi):
            return e.tensor_tensor(TB2[0:1, gi * 512:(gi + 1) * 512],
                                   PQ[0][0:1, :],
                                   BR2[0:1, gi * 512:(gi + 1) * 512], ALU.add)
        evv = sch.step("vector", mkv, S.sdv)
        sch.wait("tensor", evv)
    sch.wait("sync", evv)
    ev_cq = sch.dma("sync", AP(CQB, 0, [[0, 1], [1, D]]), TB2[0:1, :], S.sst)
    sch.wait("sync", ev_cq)
    sch.dma("sync", CQ[:], AP(CQB, 0, [[1, 128], [128, DC]]), S.sld, slow=True)
    for dc in range(DC):
        def mks(e, dc=dc):
            return e.tensor_scalar_mul(WQ[:, dc * D:(dc + 1) * D],
                                       WQ[:, dc * D:(dc + 1) * D],
                                       LW1[:, dc:dc + 1])
        sch.step("vector", mks, S.sdv)
    sch.wait("tensor", ev_wo)
    sch.barrier([sch.val(S.sdv), sch.val(S.sld)])

    def pa_stageA(ci, par):
        if ci % g.group == 0:
            prev_sr = EV2.get(("sroll", ci - 1))
            if prev_sr is not None:
                sch.wait("vector", prev_sr)
                sch.wait("gpsimd", prev_sr)
            gidx = ci // g.group
            last = min(ci + g.group, g.nc_own) - 1
            def mkan(e, gidx=gidx):
                return e.tensor_copy(STATE[64:128, :],
                                     RANCH[64:128, gidx * HKV:(gidx + 1) * HKV])
            sch.step("vector", mkan)
            def mkst(e, lc=last - ci):
                return e.tensor_copy(
                    AP(RBUF, 64 * (g.group * HKV) + lc * HKV,
                       [[g.group * HKV, 64], [1, HKV]]),
                    STATE[64:128, :])
            ev_r = sch.step("vector", mkst, S.sdv)
            sch.wait("gpsimd", ev_r)
            sch.dma("gpsimd",
                    AP(RBUF, (last - ci) * HKV, [[g.group * HKV, 64], [1, HKV]]),
                    AP(RBUF, 64 * (g.group * HKV) + (last - ci) * HKV,
                       [[g.group * HKV, 64], [1, HKV]]), S.sgp)
            ev_gload = None
            for c2 in range(last - 1, ci - 1, -1):
                sch.dma("sync", GKT[:],
                        AP(PHIKs, (c2 + 1) * 128 * D, [[D, 128], [1, D]]),
                        S.sld)
                ev_gload = sch.dma(
                    "sync", GVT[:],
                    AP(VREs, (c2 + 1) * 128 * HKV, [[HKV, 128], [1, HKV]]),
                    S.sld)
                sch.wait("vector", ev_gload)
                sch.wait("tensor", ev_gload)
                ev_gb2 = gamma_bcast(c2 + 1)
                ev_w = recompute_uw(c2 + 1, False, True, GKT, GVT)
                sch.wait("vector", ev_w)
                sch.wait("sync", ev_w)
                def mkr1(e):
                    fac = AP(GB2, 64 * H, [[H, 64], [1, H], [0, KV]])
                    return e.tensor_tensor(STATE[64:128, :], STATE[64:128, :],
                                           fac, ALU.mult)
                sch.step("vector", mkr1)
                sch.step("vector",
                         lambda e: e.tensor_tensor(STATE[64:128, :],
                                                   STATE[64:128, :],
                                                   UW2[64:128, :], ALU.add))
                def mkst2(e, lc=c2 - ci):
                    return e.tensor_copy(
                        AP(RBUF, 64 * (g.group * HKV) + lc * HKV,
                           [[g.group * HKV, 64], [1, HKV]]),
                        STATE[64:128, :])
                ev_r = sch.step("vector", mkst2, S.sdv)
                sch.wait("gpsimd", ev_r)
                sch.dma("gpsimd",
                        AP(RBUF, (c2 - ci) * HKV, [[g.group * HKV, 64], [1, HKV]]),
                        AP(RBUF, 64 * (g.group * HKV) + (c2 - ci) * HKV,
                           [[g.group * HKV, 64], [1, HKV]]), S.sgp)
            sch.wait("tensor", sch.val(S.sgp))

        pq = EV2.get(("qmm", ci - 2))
        if pq is not None:
            sch.wait("sync", pq)
        sch.dma("sync", ZF2[par][:],
                AP(ZFMs, ci * C, [[g.t_own, 128], [128 * g.t_own, DC], [1, C]]),
                zs[par])
        patt = EV2.get(("att", ci - 2))
        if patt is not None:
            sch.wait("sync", patt)
        sch.dma("sync", LB4[par][:],
                AP(LROW, ci * H * C, [[0, 128], [1, H * C]]), zs[par])
        psr = EV2.get(("sroll", ci - 2))
        if psr is not None:
            sch.wait("sync", psr)
        sch.dma("sync", PKT[par][:],
                AP(PHIKs, ci * 128 * D, [[D, 128], [1, D]]), zs[par])
        ev_z = sch.dma("sync", PVT[par][:],
                       AP(VREs, ci * 128 * HKV, [[HKV, 128], [1, HKV]]),
                       zs[par])
        EV2[("z", ci)] = ev_z

        sch.wait("tensor", ev_z)
        pqf = EV2.get(("qev", ci - 1))
        if pqf is not None:
            sch.wait("tensor", pqf)
        po = EV2.get(("oev", ci - 2))
        if po is not None:
            sch.wait("tensor", po)
        evm = None
        for do in range(DC):
            bank = PQ[do // 4]
            col = (do % 4) * 128
            for dc in range(DC):
                def mk(e, do=do, dc=dc, bank=bank, col=col, par=par):
                    return e.matmul(bank[:, col:col + 128],
                                    WQ[:, dc * D + do * 128:
                                       dc * D + do * 128 + 128],
                                    ZF2[par][:, dc * 128:(dc + 1) * 128],
                                    start=(dc == 0), stop=(dc == DC - 1))
                evm = sch.step("tensor", mk,
                               S.spe if (dc == DC - 1) else None)
        EV2[("qmm", ci)] = evm
        sch.wait("vector", evm)
        def mkb0(e):
            fac = AP(CQ, 0, [[DC, 128], [1, 4], [0, 128]])
            return e.tensor_tensor(TQ[:, 0:512], PQ[0][:, :], fac, ALU.add)
        sch.step("vector", mkb0)
        def mkb1(e):
            fac = AP(CQ, 4, [[DC, 128], [1, 4], [0, 128]])
            return e.tensor_tensor(TQ[:, 512:1024], PQ[1][:, :], fac, ALU.add)
        evq = sch.step("vector", mkb1, S.sdv)
        sch.wait("scalar", evq)
        sch.step("scalar", lambda e: e.activation(TE[:], TQ[:], AF.Exp))
        evr = sch.step("scalar", lambda e: e.activation(TQ[:], TQ[:], AF.Relu),
                       S.sac)
        sch.wait("vector", evr)
        sch.step("vector", lambda e: e.tensor_scalar_min(TE[:], TE[:], 1.0))
        def mkphi(e, par=par):
            return e.tensor_tensor(PHQ[par][:], TE[:], TQ[:], ALU.add)
        evphi = sch.step("vector", mkphi, S.sdv)
        EV2[("qev", ci)] = evphi
        def mkq2(e, par=par):
            dst = AP(PHQ2[par], 0, [[H * 128, 64], [256, 8], [1, 128]])
            srcp = AP(PHQ[par], 0, [[D, 64], [128, 8], [1, 128]])
            return e.tensor_copy(dst, srcp)
        ev_q2a = sch.step("vector", mkq2, S.sdv)
        sch.wait("gpsimd", ev_q2a)
        def mkq2b(e, par=par):
            dst = AP(PHQ2[par], 128, [[H * 128, 64], [256, 8], [1, 128]])
            srcp = AP(PHQ[par], 64 * D, [[D, 64], [128, 8], [1, 128]])
            return e.dma_start(dst, srcp)
        ev_q2b = sch.step("gpsimd", mkq2b, S.sgp, 16)
        EV2[("q2", ci)] = ev_q2b

    def pa_stageB(ci, par):
        pc2 = EV2.get(("c2", ci - 2))
        if pc2 is not None:
            sch.wait("sync", pc2)
        sch.dma("sync", XN2[par][:],
                AP(XN32, ci * C * D, [[D, 128], [1, D]]), ys[par])
        pqs = EV2.get(("qsc", ci - 1))
        if pqs is not None:
            sch.wait("sync", pqs)
        ev_y = sch.dma("sync", E12B[par][:],
                       AP(E12ROW, ci * 2 * H * C, [[0, 64], [1, 2 * H * C]]),
                       ys[par])
        EV2[("y", ci)] = ev_y
        EV2[("e12", ci)] = ev_y
        ps_ = EV2.get(("sroll", ci - 1))
        if ps_ is not None:
            sch.wait("tensor", ps_)
        ptrf = EV2.get(("atf", ci - 1))
        if ptrf is not None:
            sch.wait("tensor", ptrf)
        for rnd in range(2):
            evt = None
            for hh in range(8):
                h = rnd * 8 + hh
                def mkt(e, h=h, hh=hh, par=par):
                    off = (h // 2) * 128 + (h % 2) * 64
                    return e.transpose(
                        PTR2[0:64, hh * 128:(hh + 1) * 128],
                        AP(PKT[par], off, [[D, 128], [1, 64]]), IDENT[:])
                evt = sch.step("tensor", mkt, S.spe if hh == 7 else None)
            sch.wait("vector", evt)
            def mkev(e, rnd=rnd, par=par):
                return e.tensor_copy(PHKF2[par][0:64, rnd * 1024:(rnd + 1) * 1024],
                                     PTR2[0:64, 0:1024])
            ev_pkf = sch.step("vector", mkev, S.sdv)
            sch.wait("tensor", ev_pkf)

        sch.wait("tensor", EV2[("q2", ci)])
        sch.wait("vector", EV2[("q2", ci)])
        sch.wait("vector", EV2[("e12", ci)])
        pao = EV2.get(("abb", (ci - 1, 3)))
        if pao is not None:
            sch.wait("vector", pao)
        def mksa(e, par=par):
            return e.tensor_tensor(PHQA2[:], PHQ2[par][:],
                                   E12B[par][:, H * C:2 * H * C], ALU.mult)
        sch.step("vector", mksa)
        def mksb(e, par=par):
            return e.tensor_tensor(PHQB2[:], PHQ2[par][:],
                                   E12B[par][:, 0:H * C], ALU.mult)
        ev_qsc = sch.step("vector", mksb, S.sdv)
        EV2[("qsc", ci)] = ev_qsc
        sch.wait("tensor", ev_qsc)
        ev_combine = None
        for b4 in range(4):
            def mksub(e, b4=b4, ci=ci, par=par):
                fac = AP(LTM, ci * H + b4 * 4, [[EROW, 128], [1, 4], [0, 128]])
                return e.tensor_tensor(DT[:],
                                       LB4[par][:, b4 * 512:(b4 + 1) * 512],
                                       fac, ALU.subtract)
            ev_g = sch.step("vector", mksub, S.sdv)
            sch.wait("scalar", ev_g)
            sch.step("scalar", lambda e: e.activation(DT[:], DT[:], AF.Abs))
            ev_d = sch.step("scalar",
                            lambda e: e.activation(DT[:], DT[:], AF.Exp,
                                                   scale=-1.0), S.sac)
            ev_sc = None
            for hh in range(4):
                h = b4 * 4 + hh
                def mksc(e, h=h, hh=hh, par=par):
                    lhsT = AP(PHKF2[par], h * 128, [[H * 128, 64], [1, 128]])
                    rhs = AP(PHQ2[par], h * 128, [[H * 128, 64], [1, 128]])
                    return e.matmul(PSC[:, hh * 128:(hh + 1) * 128], lhsT,
                                    rhs, start=True, stop=True)
                ev_sc = sch.step("tensor", mksc, S.spe if hh == 3 else None)
            sch.wait("vector", ev_sc)
            sch.wait("vector", ev_d)
            pab = EV2.get(("abb", (ci, b4 - 2)))
            if pab is not None:
                sch.wait("vector", pab)
            def mka(e, b4=b4):
                return e.tensor_tensor(ABB[b4 % 2][:], PSC[:, :], DT[:],
                                       ALU.mult)
            ev_a = sch.step("vector", mka, S.sdv)
            sch.wait("tensor", ev_a)
            pdv = (EV2.get(("div", (ci, b4 - 2))) if b4 >= 2 else
                   EV2.get(("div", (ci - 1, b4 + 2))))
            if pdv is not None:
                sch.wait("tensor", pdv)
            ev_o = None
            for hh in range(4):
                h = b4 * 4 + hh
                def mko1(e, hh=hh, par=par, h=h, b4=b4):
                    rhs = AP(PVT[par], h * KV, [[HKV, 128], [1, KV]])
                    return e.matmul(PO[b4 % 2][:, hh * KV:(hh + 1) * KV],
                                    AP(ABB[b4 % 2], hh * 128,
                                       [[512, 128], [1, 128]]),
                                    rhs, start=True, stop=False)
                sch.step("tensor", mko1)
                def mko2(e, h=h, hh=hh, b4=b4):
                    lhsT = AP(PHQA2, h * 128, [[H * 128, 64], [1, 128]])
                    rhs = AP(SBF, h * KV, [[HKV, 64], [1, KV]])
                    return e.matmul(PO[b4 % 2][:, hh * KV:(hh + 1) * KV], lhsT,
                                    rhs, start=False, stop=False)
                sch.step("tensor", mko2)
                def mko3(e, h=h, hh=hh, b4=b4, lc=ci % g.group):
                    lhsT = AP(PHQB2, h * 128, [[H * 128, 64], [1, 128]])
                    rhs = AP(RBUF, lc * HKV + h * KV,
                             [[g.group * HKV, 64], [1, KV]])
                    return e.matmul(PO[b4 % 2][:, hh * KV:(hh + 1) * KV], lhsT,
                                    rhs, start=False, stop=True)
                ev_o = sch.step("tensor", mko3, S.spe if hh == 3 else None)
            EV2[("abb", (ci, b4))] = ev_o
            sch.wait("vector", ev_o)
            def mkden(e, b4=b4):
                srcp = AP(PO[b4 % 2], KD, [[4 * KV, 128], [KV, 4]])
                return e.tensor_scalar_max(RD8[:], srcp, DEN_EPS)
            sch.step("vector", mkden)
            sch.step("vector", lambda e: e.reciprocal(RD8[:], RD8[:]))
            if b4 == 0:
                paf = EV2.get(("atf", ci - 2))
                if paf is not None:
                    sch.wait("vector", paf)
            def mkdiv(e, b4=b4, par=par):
                srcp = AP(PO[b4 % 2], 0, [[4 * KV, 128], [KV, 4], [1, KD]])
                fac = AP(RD8, 0, [[4, 128], [1, 4], [0, KD]])
                dst = AP(ATT[par], b4 * 256, [[D, 128], [KD, 4], [1, KD]])
                return e.tensor_tensor(dst, srcp, fac, ALU.mult)
            ev_combine = sch.step("vector", mkdiv, S.sdv)
            EV2[("div", (ci, b4))] = ev_combine
        EV2[("att", ci)] = ev_combine
        ev_gb3 = gamma_bcast(ci, evgate=ev_combine)
        ev_u = recompute_uw(ci, True, False, PKT[par], PVT[par])
        sch.wait("vector", ev_u)
        sch.wait("vector", ev_gb3)
        def mks1(e):
            fac = AP(GB2, 0, [[H, 64], [1, H], [0, KV]])
            return e.tensor_tensor(STATE[0:64, :], STATE[0:64, :], fac, ALU.mult)
        sch.step("vector", mks1)
        sch.step("vector",
                 lambda e: e.tensor_tensor(STATE[0:64, :], STATE[0:64, :],
                                           UW2[0:64, :], ALU.add))
        ev_sr = sch.step("vector",
                         lambda e: e.tensor_copy(SBF[0:64, :], STATE[0:64, :]),
                         S.sdv)
        sch.wait("tensor", ev_sr)
        EV2[("sroll", ci)] = ev_sr

    def pa_stageC(ci, par):
        sch.wait("tensor", EV2[("att", ci)])
        evt = None
        for dc in range(DC):
            def mkt(e, dc=dc, par=par):
                return e.transpose(PTR2[:, dc * 128:(dc + 1) * 128],
                                   ATT[par][:, dc * 128:(dc + 1) * 128],
                                   IDENT[:])
            evt = sch.step("tensor", mkt, S.spe if dc == DC - 1 else None)
        sch.wait("vector", evt)
        ev_atf = sch.step("vector", lambda e: e.tensor_copy(ATF[:], PTR2[:]),
                          S.sdv)
        EV2[("atf", ci)] = ev_atf
        sch.wait("tensor", ev_atf)
        sch.wait("sync", sch.val(S.sst))
        ev_x = sch.dma("sync", XT2[par][:],
                       AP(x_own, ci * C * D, [[D, 128], [1, D]]), xs[par])
        sch.wait("vector", EV2[("y", ci)])
        sch.wait("vector", sch.val(S.sst))
        ev_yb = None
        for got in range(2):
            bank = PQ[got]
            evm = None
            for dc in range(DC):
                def mk(e, dc=dc, got=got, bank=bank):
                    return e.matmul(bank[:, :],
                                    ATF[:, dc * 128:(dc + 1) * 128],
                                    WO[:, dc * D + got * 512:
                                       dc * D + got * 512 + 512],
                                    start=(dc == 0), stop=(dc == DC - 1))
                evm = sch.step("tensor", mk, S.spe if dc == DC - 1 else None)
            sch.wait("vector", evm)
            def mky(e, got=got, bank=bank, par=par):
                return e.scalar_tensor_tensor(
                    YT[:, got * 512:(got + 1) * 512], bank[:, :], 1.0,
                    XN2[par][:, got * 512:(got + 1) * 512], ALU.mult, ALU.add)
            sch.step("vector", mky)
            ev_yb = sch.step("vector",
                             lambda e, got=got:
                             e.tensor_tensor(YT[:, got * 512:(got + 1) * 512],
                                             YT[:, got * 512:(got + 1) * 512],
                                             OBb[:, got * 512:(got + 1) * 512],
                                             ALU.add), S.sdv)
            sch.wait("tensor", ev_yb)
        EV2[("oev", ci)] = ev_yb
        sch.wait("vector", ev_x)
        def mkp(e, par=par):
            return e.tensor_tensor(XT2[par][:], YT[:], XT2[par][:], ALU.add)
        ev_pt = sch.step("vector", mkp, S.sdv)
        sch.wait("sync", ev_pt)
        sch.dma("sync", AP(PBs, ci * C * D, [[D, 128], [1, D]]), XT2[par][:],
                S.sst)
        sch.wait("scalar", sch.val(S.sst))
        ev_z2 = ln_affine2(YT[:], Z2T[:], XN2[par])
        sch.wait("sync", ev_z2)
        sch.dma("sync", AP(Z2B, ci * C * D, [[D, 128], [1, D]]), Z2T[:], S.sst)
        ev_c2 = sch.step("vector", lambda e: e.engine_nop(), S.sca)
        EV2[("c2", ci)] = ev_c2

    npair = g.nc_own // 2
    for j in range(npair + 2):
        for cc in (2 * (j - 2), 2 * (j - 2) + 1):
            if 0 <= cc < g.nc_own:
                pa_stageC(cc, cc % 2)
        for cc in (2 * (j - 1), 2 * (j - 1) + 1):
            if 0 <= cc < g.nc_own:
                pa_stageB(cc, cc % 2)
        for cc in (2 * j, 2 * j + 1):
            if 0 <= cc < g.nc_own:
                pa_stageA(cc, cc % 2)

    ev_pa = sch.step("vector", lambda e: e.engine_nop(), S.sdv)
    sch.barrier([ev_pa, sch.val(S.sst), sch.val(S.spe), sch.val(S.sac)])
    pas.close()
    att.close()
    if stop_after == "pa":
        return _early_out()

    def ffn_phase(Wa_t, Wb_t, bias1_fm, in_scr, bias2_bc, dst_t, dst_sem,
                  scale_lw2, tag):
        fs = ExitStack()
        mg = g.midgroup
        ntg = mg // 128
        W1S = sb(f"W1S{tag}", (128, DC * FFN), BF16, fs)
        W2S = sb(f"W2S{tag}", (128, FC * D), BF16, fs)
        HMID = sb(f"HMID{tag}", (128, FC * mg), BF16, fs)
        XFW = sb(f"XFW{tag}", (128, DC * mg), BF16, fs)
        PSL = [sb(f"PSL{tag}{i}", (128, D), F32, fs) for i in range(2)]
        OT = [sb(f"OT{tag}{i}", (128, 512), F32, fs) for i in range(2)]
        PH = [ps(f"PH{tag}{i}", (128, mg), F32, stack=fs) for i in range(2)]
        PACC = [ps(f"PACC{tag}{i}", (128, 512), F32, stack=fs) for i in range(2)]

        for j in range(8):
            sch.dma("gpsimd",
                    AP(W1S, j * 512,
                       [[DC * FFN, 128], [FFN, DC], [1, 512]]),
                    AP(Wa_t, j * 512,
                       [[FFN, 128], [128 * FFN, DC], [1, 512]]),
                    (S.sw0, S.sw1)[j % 2])
        sch.ops["gpsimd"].append(
            lambda e, v=sch.ctr[S.sw0.name]: e.wait_ge(S.sw0, v))
        sch.ops["gpsimd"].append(
            lambda e, v=sch.ctr[S.sw1.name]: e.wait_ge(S.sw1, v))
        ev_w1rdy = sch.step("gpsimd", lambda e: e.engine_nop(), S.sw2)
        ev_w2a = ev_w2b = None
        for j in range(8):
            ev = sch.dma("gpsimd",
                         AP(W2S, j * 4 * D, [[FC * D, 128], [1, 4 * D]]),
                         AP(Wb_t, j * 4 * 128 * D,
                            [[D, 128], [128 * D, 4], [1, D]]),
                         (S.sw0, S.sw1)[j % 2])
            if j % 2 == 0:
                ev_w2a = ev
            else:
                ev_w2b = ev

        ev_hold = [None, None]
        ev_accfree = [None, None]
        ev_otfree = [None, None]
        ev_pfree = [None, None]
        ev_xfree = None
        ev_hfree = None
        def load_xfw(m):
            t0 = m * mg
            ev_in = None
            for dc in range(DC):
                ev_in = sch.dmat("sync", XFW[:, dc * mg:(dc + 1) * mg],
                                 AP(in_scr, t0 * D + dc * 128,
                                    [[D, mg], [1, 128]]), S.sz0)
            return ev_in
        ev_in = load_xfw(0)
        for m in range(g.n_mid):
            t0 = m * mg
            sch.wait("tensor", ev_in)
            if scale_lw2:
                sch.wait("vector", ev_in)
                evx = None
                for dc in range(DC):
                    def mkx(e, dc=dc):
                        return e.tensor_scalar_mul(
                            XFW[:, dc * mg:(dc + 1) * mg],
                            XFW[:, dc * mg:(dc + 1) * mg],
                            LW2[:, dc:dc + 1])
                    evx = sch.step("vector", mkx,
                                   S.sdv if dc == DC - 1 else None)
                sch.wait("tensor", evx)
            if ev_hfree is not None:
                sch.wait("scalar", ev_hfree)
            ev_f1last = None
            ev_g = None
            for c in range(FC):
                bank = PH[c % 2]
                if m == 0 and c == 0:
                    sch.wait("tensor", ev_w1rdy)
                if ev_hold[c % 2] is not None:
                    sch.wait("tensor", ev_hold[c % 2])
                evm = None
                for dc in range(DC):
                    def mk(e, c=c, dc=dc, bank=bank):
                        return e.matmul(bank[:, :],
                                        W1S[:, dc * FFN + c * 128:
                                            dc * FFN + c * 128 + 128],
                                        XFW[:, dc * mg:(dc + 1) * mg],
                                        start=(dc == 0), stop=(dc == DC - 1))
                    evm = sch.step("tensor", mk,
                                   S.spe if dc == DC - 1 else None)
                ev_f1last = evm
                sch.wait("scalar", evm)
                def mkg(e, c=c, bank=bank):
                    return e.activation(HMID[:, c * mg:(c + 1) * mg],
                                        bank[:, :], AF.Gelu,
                                        bias=bias1_fm[:, c:c + 1])
                ev_g = sch.step("scalar", mkg, S.sac)
                ev_hold[c % 2] = ev_g
            ev_xfree = ev_f1last
            if m + 1 < g.n_mid:
                sch.wait("sync", ev_xfree)
                ev_in = load_xfw(m + 1)
            sch.wait("tensor", ev_g)
            pv = 0
            ev_p = [None, None]
            ev_last_f2 = None
            for tg in range(ntg):
                tpar = tg % 2
                if ev_pfree[tpar] is not None:
                    sch.wait("sync", ev_pfree[tpar])
                ev_p[tpar] = sch.dma("sync", PSL[tpar][:],
                                     AP(PBs, (t0 + tg * 128) * D,
                                        [[D, 128], [1, D]]),
                                     (S.sp0, S.sp1)[tpar])
                for ng in range(2):
                    opar = pv % 2
                    bank = PACC[opar]
                    if ev_accfree[opar] is not None:
                        sch.wait("tensor", ev_accfree[opar])
                    evm = None
                    if m == 0 and tg == 0 and ng == 0:
                        sch.wait("tensor", ev_w2a)
                        sch.wait("tensor", ev_w2b)
                    for c in range(FC):
                        def mk(e, c=c, tg=tg, ng=ng, bank=bank):
                            return e.matmul(
                                bank[:, :],
                                AP(HMID, c * mg + tg * 128,
                                   [[FC * mg, 128], [1, 128]]),
                                W2S[:, c * D + ng * 512:
                                    c * D + ng * 512 + 512],
                                start=(c == 0), stop=(c == FC - 1))
                        evm = sch.step("tensor", mk,
                                       S.spe if c == FC - 1 else None)
                    ev_last_f2 = evm
                    sch.wait("vector", evm)
                    sch.wait("vector", ev_p[tpar])
                    if ev_otfree[opar] is not None:
                        sch.wait("vector", ev_otfree[opar])
                    def mko(e, ng=ng, bank=bank, opar=opar, tpar=tpar):
                        return e.scalar_tensor_tensor(
                            OT[opar][:], bank[:, :], 1.0,
                            PSL[tpar][:, ng * 512:(ng + 1) * 512],
                            ALU.mult, ALU.add)
                    sch.step("vector", mko)
                    def mko2(e, ng=ng, opar=opar):
                        return e.tensor_tensor(
                            OT[opar][:], OT[opar][:],
                            bias2_bc[:, ng * 512:(ng + 1) * 512], ALU.add)
                    ev_ot = sch.step("vector", mko2, S.sdv)
                    ev_accfree[opar] = ev_ot
                    sch.wait("sync", ev_ot)
                    ev_st = sch.dma("sync",
                                    AP(dst_t, (t0 + tg * 128) * D + ng * 512,
                                       [[D, 128], [1, 512]]),
                                    OT[opar][:], dst_sem)
                    ev_otfree[opar] = ev_st
                    if ng == 1:
                        ev_pfree[tpar] = ev_st
                    pv += 1
            ev_hfree = ev_last_f2
        ev_done = [sch.val(S.sdv), sch.val(S.sac), sch.val(S.spe),
                   sch.val(dst_sem), sch.val(S.sw0), sch.val(S.sw1)]
        sch.barrier(ev_done)
        fs.close()

    ffn_phase(w1_w, w2_w, W1B, XNB, W2Bb, PBs, S.sst, False, "w")
    if stop_after == "pw":
        return _early_out()
    ffn_phase(f1_w, f2_w, CF1, Z2B, F2Bb, out_t, S.sout, True, "f")

    sch.wait("sync", sch.val(S.sout))

    sch.run_into(nc)
    return nc



def _prep_common(inputs):
    f = np.ascontiguousarray
    w = {
        "q_w": f(inputs["L0_attn_q_w"], dtype=np.float32),
        "q_b": f(inputs["L0_attn_q_b"], dtype=np.float32).reshape(1, D),
        "k_w": f(inputs["L0_attn_k_w"], dtype=np.float32),
        "k_b": f(inputs["L0_attn_k_b"], dtype=np.float32).reshape(1, D),
        "v_w": f(inputs["L0_attn_v_w"], dtype=np.float32),
        "v_b": f(inputs["L0_attn_v_b"], dtype=np.float32).reshape(1, D),
        "o_w": f(inputs["L0_attn_o_w"], dtype=np.float32),
        "o_b": f(inputs["L0_attn_o_b"], dtype=np.float32).reshape(1, D),
        "dec_w": f(inputs["L0_attn_dec_w"], dtype=np.float32),
        "dec_b": f(inputs["L0_attn_dec_b"], dtype=np.float32).reshape(1, H),
        "ln1_w": f(inputs["L0_attn_ln1_w"], dtype=np.float32).reshape(1, D),
        "ln1_b": f(inputs["L0_attn_ln1_b"], dtype=np.float32).reshape(1, D),
        "ln2_w": f(inputs["L0_attn_ln2_w"], dtype=np.float32).reshape(1, D),
        "ln2_b": f(inputs["L0_attn_ln2_b"], dtype=np.float32).reshape(1, D),
        "f1_w": f(inputs["L0_attn_f1_w"], dtype=np.float32),
        "f1_b": f(inputs["L0_attn_f1_b"], dtype=np.float32).reshape(1, FFN),
        "f2_w": f(inputs["L0_attn_f2_w"], dtype=np.float32),
        "f2_b": f(inputs["L0_attn_f2_b"], dtype=np.float32).reshape(1, D),
        "w1_w": f(inputs["L0_ffn_w1"], dtype=np.float32),
        "w1_b": f(inputs["L0_ffn_b1"], dtype=np.float32).reshape(1, FFN),
        "w2_w": f(inputs["L0_ffn_w2"], dtype=np.float32),
        "w2_b": f(inputs["L0_ffn_b2"], dtype=np.float32).reshape(1, D),
        "ident": np.eye(128, dtype=np.float32).astype(bf16),
        "identf": np.eye(128, dtype=np.float32),
        "onesrow": np.ones((1, 128), dtype=np.float32).astype(bf16),
        "onesrowf": np.ones((1, 128), dtype=np.float32),
    }
    return w


def kernel(**inputs):
    x = np.ascontiguousarray(np.asarray(inputs["L0_attn_x"], dtype=np.float32))
    B, T, _ = x.shape
    g = Geo(t_seq=T)
    n_cores = 2 * B
    w = _prep_common(inputs)
    in_maps = []
    for core in range(n_cores):
        b, half = core // 2, core % 2
        own = x[b, half * g.t_own:(half + 1) * g.t_own]
        rem = x[b, (1 - half) * g.t_own:(2 - half) * g.t_own]
        m = dict(w)
        m["x_own"] = np.ascontiguousarray(own)
        m["x_rem"] = np.ascontiguousarray(rem)
        m["hmask"] = np.array([[float(half), float(1 - half)]],
                              dtype=np.float32)
        in_maps.append(m)
    nc = build_program(g, n_cores=n_cores)
    res = run_bass_kernel_spmd(nc, in_maps, core_ids=list(range(n_cores)))
    outs = [res.results[i]["out"] for i in range(n_cores)]
    full = np.empty((B, T, D), dtype=np.float32)
    for core in range(n_cores):
        b, half = core // 2, core % 2
        full[b, half * g.t_own:(half + 1) * g.t_own] = outs[core]
    return full


# version 26
# speedup vs baseline: 1.0011x; 1.0011x over previous
from collections import defaultdict
from contextlib import ExitStack

import numpy as np
import ml_dtypes
import concourse.bass as bass
import concourse.mybir as mybir
from concourse.bass_utils import run_bass_kernel_spmd

F32 = mybir.dt.float32
BF16 = mybir.dt.bfloat16
AF = mybir.ActivationFunctionType
ALU = mybir.AluOpType
bf16 = ml_dtypes.bfloat16

D, H, KD, KV, FFN, C = 1024, 16, 64, 65, 4096, 128
DC, FC = D // 128, FFN // 128
HKV = H * KV
LN_EPS, LAM_MIN, DEN_EPS = 1e-5, 1e-8, 1e-6


class Geo:
    def __init__(self, t_seq=8192, group=8, midgroup=512):
        self.t_seq = t_seq
        self.t_own = t_seq // 2
        self.nc_own = self.t_own // C
        self.group = min(group, self.nc_own)
        assert self.nc_own % self.group == 0
        self.n_groups = self.nc_own // self.group
        self.midgroup = min(midgroup, self.t_own)
        assert self.t_own % self.midgroup == 0
        self.n_mid = self.t_own // self.midgroup


def AP(t, off, pat):
    return bass.AP(t, off, pat)


class Sched:
    ENGS = ("sync", "gpsimd", "tensor", "vector", "scalar")
    PARANOID = False

    def __init__(self):
        self.ops = {e: [] for e in self.ENGS}
        self.ctr = defaultdict(int)
        self.dma_issuer = {}
        self.progress = {}
        self.serial_sems = set()

    def set_progress_sems(self, mapping):
        self.progress = mapping

    def _emit(self, eng, fn):
        self.ops[eng].append(fn)

    DRAIN_ENGS = ("vector", "scalar")

    def step(self, eng, make, sem=None, n=1):
        drain = eng in self.DRAIN_ENGS or self.PARANOID
        if sem is None:
            if drain:
                self._emit(eng, lambda e, mk=make: (mk(e), e.drain()))
            else:
                self._emit(eng, lambda e, mk=make: mk(e))
            return None
        self.ctr[sem.name] += n
        if drain and eng in self.DRAIN_ENGS:
            self._emit(eng, lambda e, mk=make, s=sem, k=n:
                       (mk(e), e.drain().then_inc(s, k)))
        elif drain:
            self._emit(eng, lambda e, mk=make, s=sem, k=n:
                       (mk(e).then_inc(s, k), e.drain()))
        else:
            self._emit(eng, lambda e, mk=make, s=sem, k=n:
                       mk(e).then_inc(s, k))
        return (sem, self.ctr[sem.name])

    def dma(self, eng, dst, src, sem, slow=False):
        self.dma_issuer[sem.name] = (eng, sem)
        if slow:
            return self.step(
                eng, lambda e, d=dst, s=src: e.dma_start(
                    d, s, allow_slow_non_contiguous=True), sem, 16)
        return self.step(eng, lambda e, d=dst, s=src: e.dma_start(d, s),
                         sem, 16)

    def dmat(self, eng, dst, src, sem):
        self.dma_issuer[sem.name] = (eng, sem)
        return self.step(
            eng, lambda e, d=dst, s=src: e.dma_start(d, s, transpose=True),
            sem, 16)

    def val(self, sem):
        return (sem, self.ctr[sem.name])

    def wait(self, eng, ev):
        if ev is None:
            return
        sem, v = ev
        if v <= 0:
            return
        iss = self.dma_issuer.get(sem.name)
        if iss is not None and sem.name in self.serial_sems:
            self._emit(eng, lambda e, s=sem, vv=v: e.wait_ge(s, vv))
            return
        if iss is not None and iss[0] != eng:
            ieng = iss[0]
            full = self.ctr[sem.name]
            self._emit(ieng, lambda e, s=sem, vv=full: e.wait_ge(s, vv))
            pev = self.step(ieng, lambda e: e.nop(), self.progress[ieng])
            self._emit(eng, lambda e, s=pev[0], vv=pev[1]: e.wait_ge(s, vv))
        else:
            self._emit(eng, lambda e, s=sem, vv=v: e.wait_ge(s, vv))

    def barrier(self, evs):
        for eng in self.ENGS:
            for ev in evs:
                self.wait(eng, ev)

    def run_into(self, nc):
        with nc.Block() as block:
            for name in self.ENGS:
                lst = self.ops[name]

                def make_body(lst_):
                    def body(e):
                        for fn in lst_:
                            fn(e)
                    return body

                getattr(block, name)(make_body(lst))


def build_program(g: Geo, n_cores=8, stop_after=None, paranoid=False):
    nc = bass.Bass("TRN2", target_bir_lowering=False,
                   detect_race_conditions=paranoid)
    es = ExitStack()
    sch = Sched()
    sch.PARANOID = paranoid
    inv_d = 1.0 / D

    def din(name, shape, dt=F32):
        return nc.dram_tensor(name, list(shape), dt, kind="ExternalInput")

    x_own = din("x_own", (g.t_own, D))
    x_rem = din("x_rem", (g.t_own, D))
    q_w = din("q_w", (D, D)); q_b = din("q_b", (1, D))
    k_w = din("k_w", (D, D)); k_b = din("k_b", (1, D))
    v_w = din("v_w", (D, D)); v_b = din("v_b", (1, D))
    o_w = din("o_w", (D, D)); o_b = din("o_b", (1, D))
    dec_w = din("dec_w", (D, H)); dec_b = din("dec_b", (1, H))
    ln1_w = din("ln1_w", (1, D)); ln1_b = din("ln1_b", (1, D))
    ln2_w = din("ln2_w", (1, D)); ln2_b = din("ln2_b", (1, D))
    f1_w = din("f1_w", (D, FFN)); f1_b = din("f1_b", (1, FFN))
    f2_w = din("f2_w", (FFN, D)); f2_b = din("f2_b", (1, D))
    w1_w = din("w1_w", (D, FFN)); w1_b = din("w1_b", (1, FFN))
    w2_w = din("w2_w", (FFN, D)); w2_b = din("w2_b", (1, D))
    hmask_in = din("hmask", (1, 2))
    ident_in = din("ident", (128, 128), BF16)
    identf_in = din("identf", (128, 128), F32)
    ones_in = din("onesrow", (1, 128), BF16)
    onesf_in = din("onesrowf", (1, 128), F32)

    out_t = nc.dram_tensor("out", [g.t_own, D], F32, kind="ExternalOutput")

    def dscr(name, shape, dt):
        return nc.dram_tensor(name, list(shape), dt, kind="Internal")

    XN32 = dscr("XN32", (g.t_own, D), F32)
    XNB = dscr("XNB", (g.t_own, D), BF16)
    ZFMs = dscr("ZFMs", (DC, 128, g.t_own), BF16)
    Z2B = dscr("Z2B", (g.t_own, D), BF16)
    PBs = dscr("PBs", (g.t_own, D), F32)
    LROW = dscr("LROW", (g.nc_own, H * C), F32)
    E12ROW = dscr("E12ROW", (g.nc_own, 2 * H * C), BF16)
    PHIKs = dscr("PHIKs", (g.nc_own, 128, D), BF16)
    VREs = dscr("VREs", (g.nc_own, 128, HKV), BF16)
    CQB = dscr("CQB", (1, D), F32)
    CKB = dscr("CKB", (1, D), F32)
    CVB = dscr("CVB", (1, D), F32)
    CDB = dscr("CDB", (1, H), F32)
    CF1B = dscr("CF1B", (1, FFN), F32)
    EXCI = dscr("EXCI", (128, HKV), F32)
    EXCO = dscr("EXCO", (2 * 128, HKV), F32)

    SEM = {}
    for s in ("sx0", "sx1", "sy0", "sy1", "sz0", "sz1", "sw0", "sw1",
              "sp0", "sp1", "sst", "sld", "sout",
              "spe", "sdv", "sac", "sgp", "sqp", "gqp", "scc", "sca", "sw2"):
        SEM[s] = es.enter_context(nc.semaphore(s))
    S = type("S", (), SEM)()
    sch.set_progress_sems({"sync": S.sqp, "gpsimd": S.gqp})
    sch.serial_sems = {"sx0", "sx1", "sy0", "sy1", "sz0", "sz1",
                       "sp0", "sp1"}

    def sb(name, shape, dt, stack=None):
        return (stack or es).enter_context(nc.sbuf_tensor(name, list(shape), dt))

    def ps(name, shape, dt=F32, stack=None):
        return (stack or es).enter_context(nc.psum_tensor(name, list(shape), dt))

    IDENT = sb("IDENT", (128, 128), BF16)
    IDF = sb("IDF", (128, 128), F32)
    ONES1 = sb("ONES1", (1, 128), BF16)
    ONES1F = sb("ONES1F", (1, 128), F32)
    HM = sb("HM", (128, 2), F32)
    OBb = sb("OBb", (128, D), F32)
    F2Bb = sb("F2Bb", (128, D), F32)
    W2Bb = sb("W2Bb", (128, D), F32)
    W1B = sb("W1B", (128, FC), F32)
    CF1 = sb("CF1", (128, FC), F32)
    LW2 = sb("LW2", (128, DC), F32)

    att = ExitStack()
    WD = sb("WD", (128, DC * H), BF16, att)
    NCD = sb("NCD", (H, 1), F32, att)
    LW1 = sb("LW1", (128, DC), F32, att)
    LB1 = sb("LB1", (128, DC), BF16, att)
    E1T = sb("E1T", (128, g.nc_own * H), F32, att)
    E2T = sb("E2T", (128, g.nc_own * H), F32, att)
    LTM = sb("LTM", (128, g.nc_own * H), F32, att)
    GROW = sb("GROW", (1, g.nc_own * H), F32, att)
    RANCH = sb("RANCH", (128, g.n_groups * HKV), BF16, att)
    STATE = sb("STATE", (128, HKV), F32, att)
    GAMR = sb("GAMR", (128, H), F32, att)
    GB = sb("GB", (128, H), F32, att)
    FA = sb("FA", (128, H), F32, att)
    FB = sb("FB", (128, H), F32, att)

    EROW = g.nc_own * H

    def _early_out():
        evd = sch.dma("sync", AP(out_t, 0, [[D, 128], [1, D]]), OBb[:], S.sout)
        sch.wait("sync", evd)
        sch.run_into(nc)
        return nc

    p0s = ExitStack()
    sch.dma("sync", IDENT[:], ident_in.ap(), S.sld)
    sch.dma("sync", IDF[:], identf_in.ap(), S.sld)
    sch.dma("sync", ONES1[:], ones_in.ap(), S.sld)
    sch.dma("sync", ONES1F[:], onesf_in.ap(), S.sld)
    sch.dma("sync", HM[:], AP(hmask_in, 0, [[0, 128], [1, 2]]), S.sld)
    for dst, srct, nn_ in ((OBb, o_b, D), (F2Bb, f2_b, D), (W2Bb, w2_b, D)):
        sch.dma("sync", dst[:], AP(srct, 0, [[0, 128], [1, nn_]]), S.sld)
    sch.dma("sync", W1B[:], AP(w1_b, 0, [[1, 128], [128, FC]]), S.sld, slow=True)
    sch.dma("sync", LW1[:], AP(ln1_w, 0, [[1, 128], [128, DC]]), S.sld, slow=True)
    sch.dma("gpsimd", LB1[:], AP(ln1_b, 0, [[1, 128], [128, DC]]), S.sw0, slow=True)
    sch.dma("sync", LW2[:], AP(ln2_w, 0, [[1, 128], [128, DC]]), S.sld, slow=True)
    LB2 = sb("LB2", (128, DC), BF16, p0s)
    sch.dma("gpsimd", LB2[:], AP(ln2_b, 0, [[1, 128], [128, DC]]), S.sw0, slow=True)
    ev_sld0 = sch.val(S.sld)

    def wload(dst, srct, douts):
        pat = [[douts, 128], [128 * douts, DC], [1, douts]]
        return sch.dma("gpsimd", dst[:], AP(srct, 0, pat), S.sw0)

    wload(WD, dec_w, H)
    ev_w_all = sch.val(S.sw0)

    PF = [ps(f"PF{i}", (1, 512), F32, stack=p0s) for i in range(FFN // 512)]
    TB0 = sb("TB0", (1, FFN), F32, p0s)
    BR0 = sb("BR0", (1, FFN), F32, p0s)
    sch.wait("tensor", ev_w_all)
    sch.wait("tensor", ev_sld0)

    def bias_fold(W, b_t, cdram, douts):
        evv = None
        ev_br = sch.dma("sync", BR0[0:1, 0:douts],
                        AP(b_t, 0, [[0, 1], [1, douts]]), S.sld)
        sch.wait("vector", ev_br)
        sch.wait("vector", sch.val(S.sst))
        for gi in range((douts + 511) // 512):
            nn_ = min(512, douts - gi * 512)
            evm = None
            for dc in range(DC):
                def mk(e, dc=dc, gi=gi, nn_=nn_, W=W, douts=douts):
                    return e.matmul(PF[0][0:1, 0:nn_], LB1[:, dc:dc + 1],
                                    W[:, dc * douts + gi * 512:
                                      dc * douts + gi * 512 + nn_],
                                    start=(dc == 0), stop=(dc == DC - 1))
                evm = sch.step("tensor", mk, S.spe if dc == DC - 1 else None)
            sch.wait("vector", evm)
            def mkv(e, gi=gi, nn_=nn_):
                return e.tensor_tensor(TB0[0:1, gi * 512: gi * 512 + nn_],
                                       PF[0][0:1, 0:nn_],
                                       BR0[0:1, gi * 512: gi * 512 + nn_],
                                       ALU.add)
            evv = sch.step("vector", mkv, S.sdv)
            sch.wait("tensor", evv)
        sch.wait("sync", evv)
        return sch.dma("sync", AP(cdram, 0, [[0, 1], [1, douts]]),
                       TB0[0:1, 0:douts], S.sst)

    evfd = bias_fold(WD, dec_b, CDB, H)

    sch.wait("sync", evfd)
    ev_cd = sch.dma("sync", NCD[:], AP(CDB, 0, [[1, H], [1, 1]]), S.sld)
    sch.wait("vector", ev_cd)
    sch.step("vector", lambda e: e.tensor_scalar_mul(NCD[:], NCD[:], -1.0), S.sdv)

    for W, douts in ((WD, H),):
        for dc in range(DC):
            def mks(e, W=W, dc=dc, douts=douts):
                return e.tensor_scalar_mul(W[:, dc * douts:(dc + 1) * douts],
                                           W[:, dc * douts:(dc + 1) * douts],
                                           LW1[:, dc:dc + 1])
            sch.step("vector", mks, S.sdv)

    sch.step("vector", lambda e: e.memset(FA[64:128, :], 1.0))
    sch.step("vector", lambda e: e.memset(FB[0:64, :], 1.0), S.sdv)

    FST = [sb(f"FST{i}", (128, FFN), BF16, p0s) for i in range(2)]
    evm = None
    for dc in range(DC):
        par = dc % 2
        if dc >= 2 and evm is not None:
            sch.wait("gpsimd", evm)
        evl = sch.dma("gpsimd", FST[par][:],
                      AP(f1_w, dc * 128 * FFN, [[FFN, 128], [1, FFN]]),
                      (S.sw0, S.sw1)[par])
        sch.wait("tensor", evl)
        for gi in range(FFN // 512):
            def mk(e, dc=dc, gi=gi, par=par):
                return e.matmul(PF[gi][0:1, :], LB2[:, dc:dc + 1],
                                FST[par][:, gi * 512:(gi + 1) * 512],
                                start=(dc == 0), stop=(dc == DC - 1))
            evm = sch.step("tensor", mk,
                           S.spe if gi == FFN // 512 - 1 else None)
    ev_br2 = sch.dma("sync", BR0[0:1, :], AP(f1_b, 0, [[0, 1], [1, FFN]]),
                     S.sld)
    sch.wait("vector", evm)
    sch.wait("vector", ev_br2)
    sch.wait("vector", sch.val(S.sst))
    evv2 = None
    for gi in range(FFN // 512):
        def mkv(e, gi=gi):
            return e.tensor_tensor(TB0[0:1, gi * 512:(gi + 1) * 512],
                                   PF[gi][0:1, :],
                                   BR0[0:1, gi * 512:(gi + 1) * 512],
                                   ALU.add)
        evv2 = sch.step("vector", mkv, S.sdv)
    sch.wait("sync", evv2)
    ev_cf = sch.dma("sync", AP(CF1B, 0, [[0, 1], [1, FFN]]), TB0[0:1, :], S.sst)
    sch.wait("sync", ev_cf)
    sch.dma("sync", CF1[:], AP(CF1B, 0, [[1, 128], [128, FC]]), S.sld, slow=True)

    sch.barrier([sch.val(S.sdv), sch.val(S.sld), sch.val(S.spe)])
    p0s.close()
    if stop_after == "p0":
        return _early_out()

    p1s = ExitStack()
    WK = sb("WK", (128, DC * D), BF16, p1s)
    WV = sb("WV", (128, DC * D), BF16, p1s)
    CKb = sb("CKb", (128, D), F32, p1s)
    CVb = sb("CVb", (128, D), F32, p1s)
    TB1 = sb("TB1", (1, D), F32, p1s)
    BR1 = sb("BR1", (1, D), F32, p1s)
    XT = [sb(f"XT{i}", (128, D), F32, p1s) for i in range(2)]
    XNT = [sb(f"XNT{i}", (128, D), F32, p1s) for i in range(2)]
    ZT = [sb(f"ZT{i}", (128, D), BF16, p1s) for i in range(2)]
    ZF = [sb(f"ZF{i}", (128, D), BF16, p1s) for i in range(2)]
    XNBT_ = sb("XNBT", (128, D), BF16, p1s)
    XNBT = [XNBT_, XNBT_]
    SQ = [sb(f"SQ{i}", (128, D), F32, p1s) for i in range(2)]
    ST1 = [sb(f"ST1{i}", (128, 8), F32, p1s) for i in range(2)]
    PHT = [sb(f"PHT{i}", (128, D), F32, p1s) for i in range(2)]
    PHE = [sb(f"PHE{i}", (128, D), F32, p1s) for i in range(2)]
    PHB = [sb(f"PHB{i}", (128, D), BF16, p1s) for i in range(2)]
    VT = [sb(f"VT{i}", (128, HKV), BF16, p1s) for i in range(2)]
    GSTORE = sb("GSTORE", (128, 2 * g.n_groups * H * (KV + 1)), BF16, p1s)
    GSW = 2 * g.n_groups * H * (KV + 1)
    LAM = [sb(f"LAM{i}", (H, C), F32, p1s) for i in range(2)]
    LLF = [sb(f"LLF{i}", (H, C), F32, p1s) for i in range(2)]
    LF = [sb(f"LF{i}", (H, C), F32, p1s) for i in range(2)]
    ZROF = sb("ZROF", (H, C), F32, p1s)
    E1F = [sb(f"E1F{i}", (H, C), F32, p1s) for i in range(2)]
    E2F = [sb(f"E2F{i}", (H, C), F32, p1s) for i in range(2)]
    ETT = [sb(f"ETT{i}", (128, 48), F32, p1s) for i in range(2)]
    GR1 = [sb(f"GR1{i}", (1, H), F32, p1s) for i in range(2)]
    GB = [sb(f"GBp{i}", (128, H), F32, p1s) for i in range(2)]
    KB = [sb(f"KBp{i}", (128, H * 128), BF16, p1s) for i in range(2)]
    UW = [sb(f"UWp{i}", (128, HKV), F32, p1s) for i in range(2)]
    TTMP = sb("TTMP", (128, HKV), F32, p1s)

    PTZ = ps("PTZ", (128, 1024), BF16, stack=p1s)
    PKV = ps("PKV", (128, 512), F32, stack=p1s)
    PKV2 = ps("PKV2", (128, 512), F32, stack=p1s)
    PDEC = ps("PDEC", (H, C), F32, stack=p1s)
    PUW1 = ps("PUW1", (128, 7 * KV), F32, stack=p1s)
    PUW2 = ps("PUW2", (128, 7 * KV), F32, stack=p1s)
    PUW3 = ps("PUW3", (128, 2 * KV), F32, stack=p1s)
    PMIS = ps("PMIS", (128, 64), F32, stack=p1s)

    def ln_affine(x_ap, out_ap, SQ_, ST_, tmp_ap=None):
        sch.step("vector",
                 lambda e: e.tensor_scalar(SQ_[:], x_ap, 1.0, None, ALU.mult,
                                           ALU.add, accum_out=ST_[:, 0:1]),
                 S.sdv)
        sch.step("vector",
                 lambda e: e.scalar_tensor_tensor(SQ_[:], x_ap, 1.0, x_ap,
                                                  ALU.mult, ALU.mult,
                                                  accum_out=ST_[:, 1:2]), S.sdv)
        sch.step("vector", lambda e: e.tensor_scalar_mul(ST_[:, 2:3], ST_[:, 0:1],
                                                         inv_d))
        sch.step("vector", lambda e: e.tensor_tensor(ST_[:, 3:4], ST_[:, 2:3],
                                                     ST_[:, 2:3], ALU.mult))
        sch.step("vector",
                 lambda e: e.scalar_tensor_tensor(ST_[:, 3:4], ST_[:, 1:2], inv_d,
                                                  ST_[:, 3:4], ALU.mult,
                                                  ALU.subtract))
        sch.step("vector", lambda e: e.tensor_scalar_add(ST_[:, 3:4],
                                                         ST_[:, 3:4], LN_EPS))
        ev3 = sch.step("vector",
                       lambda e: e.tensor_scalar_mul(ST_[:, 5:6], ST_[:, 2:3],
                                                     -1.0), S.sdv)
        sch.wait("scalar", ev3)
        sch.step("scalar", lambda e: e.activation(ST_[:, 4:5], ST_[:, 3:4],
                                                  AF.Ln))
        sch.step("scalar", lambda e: e.activation(ST_[:, 4:5], ST_[:, 4:5],
                                                  AF.Exp, scale=-0.5))
        t_ap = tmp_ap if tmp_ap is not None else SQ_[:]
        sch.step("scalar", lambda e: e.activation(t_ap, x_ap, AF.Identity,
                                                  bias=ST_[:, 5:6]))
        return sch.step("scalar",
                        lambda e: e.activation(out_ap, t_ap, AF.Identity,
                                               scale=ST_[:, 4:5]), S.sac)

    EV = {}

    def p1_stageA(ci, par, x_t, store):
        prev = EV.get(("Aend", ci - 2))
        if prev is not None:
            sch.wait("sync", prev)
        ev_x = sch.dma("sync", XT[par][:],
                       AP(x_t, ci * C * D, [[D, 128], [1, D]]),
                       (S.sx0, S.sx1)[par])
        sch.wait("vector", ev_x)
        sch.wait("scalar", ev_x)
        pz = EV.get(("zf", ci - 2))
        if pz is not None:
            sch.wait("scalar", pz)
        if store:
            sch.wait("scalar", sch.val(S.sst))
            sch.wait("vector", sch.val(S.sst))
            sch.wait("scalar", sch.val(S.sgp))
        ev_xn = ln_affine(x_tile_ap(par), XNT[par][:], SQ[par], ST1[par])
        sch.wait("vector", ev_xn)
        ev_z = ln_affine(XNT[par][:], ZT[par][:], SQ[par], ST1[par])
        if store:
            sch.wait("sync", ev_xn)
            sch.dma("sync", AP(XN32, ci * C * D, [[D, 128], [1, D]]),
                    XNT[par][:], S.sst)
            ev_xb = sch.step("scalar",
                             lambda e, par=par: e.activation(
                                 XNBT[par][:], XNT[par][:], AF.Identity),
                             S.sac)
            sch.wait("gpsimd", ev_xb)
            sch.dma("gpsimd", AP(XNB, ci * C * D, [[D, 128], [1, D]]),
                    XNBT[par][:], S.sgp)
        EV[("x", ci)] = ev_x
        EV[("z", ci)] = ev_z
        EV[("Aend", ci)] = ev_z

    def x_tile_ap(par):
        return XT[par][:]

    def p1_stageB(ci, par, store):
        ev_z = EV[("z", ci)]
        sch.wait("tensor", ev_z)
        pz = EV.get(("zf", ci - 1))
        if pz is not None:
            sch.wait("tensor", pz)
        evt = None
        for dc in range(DC):
            def mkt(e, dc=dc, par=par):
                return e.transpose(PTZ[:, dc * 128:(dc + 1) * 128],
                                   ZT[par][:, dc * 128:(dc + 1) * 128],
                                   IDENT[:])
            evt = sch.step("tensor", mkt, S.spe if dc == DC - 1 else None)
        pm = EV.get(("mmlast", ci - 2))
        if pm is not None:
            sch.wait("vector", pm)
        sch.wait("vector", evt)
        ev_zf = sch.step("vector",
                         lambda e, par=par: e.tensor_copy(ZF[par][:], PTZ[:]),
                         S.sdv)
        EV[("zf", ci)] = ev_zf
        if store:
            sch.wait("sync", ev_zf)
            sch.dma("sync",
                    AP(ZFMs, ci * C, [[g.t_own, 128], [128 * g.t_own, DC],
                                      [1, C]]),
                    ZF[par][:], S.sst)

        pc = EV.get(("kb", ci - 2))
        if pc is not None:
            sch.wait("vector", pc)
        pu = EV.get(("uwmm", ci - 2))
        if pu is not None:
            sch.wait("vector", pu)

        sch.wait("tensor", ev_zf)
        ev_phik = None
        for gi in range(2):
            bank = (PKV, PKV2)[gi]
            evm = None
            for dc in range(DC):
                def mk(e, dc=dc, gi=gi, bank=bank, par=par):
                    return e.matmul(bank[:, :],
                                    ZF[par][:, dc * 128:(dc + 1) * 128],
                                    WK[:, dc * D + gi * 512:
                                       dc * D + gi * 512 + 512],
                                    start=(dc == 0), stop=(dc == DC - 1))
                evm = sch.step("tensor", mk, S.spe if dc == DC - 1 else None)
            sch.wait("vector", evm)
            def mk1(e, gi=gi, bank=bank, par=par):
                return e.tensor_tensor(PHT[par][:, gi * 512:(gi + 1) * 512],
                                       bank[:, :],
                                       CKb[:, gi * 512:(gi + 1) * 512], ALU.add)
            ev1 = sch.step("vector", mk1, S.sdv)
            sch.wait("tensor", ev1)
            sch.wait("scalar", ev1)
            def mk2(e, gi=gi, par=par):
                return e.activation(PHE[par][:, gi * 512:(gi + 1) * 512],
                                    PHT[par][:, gi * 512:(gi + 1) * 512],
                                    AF.Exp)
            ev2 = sch.step("scalar", mk2, S.sac)
            def mk4(e, gi=gi, par=par):
                return e.activation(PHT[par][:, gi * 512:(gi + 1) * 512],
                                    PHT[par][:, gi * 512:(gi + 1) * 512],
                                    AF.Relu)
            ev2b = sch.step("scalar", mk4, S.sac)
            sch.wait("vector", ev2b)
            def mk3(e, gi=gi, par=par):
                return e.tensor_scalar_min(
                    PHE[par][:, gi * 512:(gi + 1) * 512],
                    PHE[par][:, gi * 512:(gi + 1) * 512], 1.0)
            sch.step("vector", mk3)
            def mk5(e, gi=gi, par=par):
                return e.tensor_tensor(
                    AP(PHB[par], gi * 512, [[D, 128], [1, 512]]),
                    PHE[par][:, gi * 512:(gi + 1) * 512],
                    PHT[par][:, gi * 512:(gi + 1) * 512], ALU.add)
            ev_phik = sch.step("vector", mk5, S.sdv)

        for gi in range(2):
            bank = (PKV, PKV2)[gi]
            evm = None
            for dc in range(DC):
                def mk(e, dc=dc, gi=gi, bank=bank, par=par):
                    return e.matmul(bank[:, :],
                                    ZF[par][:, dc * 128:(dc + 1) * 128],
                                    WV[:, dc * D + gi * 512:
                                       dc * D + gi * 512 + 512],
                                    start=(dc == 0), stop=(dc == DC - 1))
                evm = sch.step("tensor", mk, S.spe if dc == DC - 1 else None)
            sch.wait("vector", evm)
            def mkv(e, gi=gi, bank=bank, par=par):
                dst = AP(VT[par], gi * 8 * KV, [[HKV, 128], [KV, 8], [1, KD]])
                srcp = AP(bank, 0, [[512, 128], [KD, 8], [1, KD]])
                bias = AP(CVb, gi * 512, [[D, 128], [KD, 8], [1, KD]])
                return e.tensor_tensor(dst, srcp, bias, ALU.add)
            ev_v = sch.step("vector", mkv, S.sdv)
            sch.wait("tensor", ev_v)

        if store:
            sch.wait("sync", ev_phik)
            sch.wait("sync", sch.val(S.sdv))
            sch.dma("sync", AP(PHIKs, ci * 128 * D, [[D, 128], [1, D]]),
                    PHB[par][:], S.sst)
            sch.dma("sync", AP(VREs, ci * 128 * HKV, [[HKV, 128], [1, HKV]]),
                    VT[par][:], S.sst)

        evm = None
        for dc in range(DC):
            def mk(e, dc=dc, par=par):
                return e.matmul(PDEC[:, :], WD[:, dc * H:(dc + 1) * H],
                                ZF[par][:, dc * 128:(dc + 1) * 128],
                                start=(dc == 0), stop=(dc == DC - 1))
            evm = sch.step("tensor", mk, S.spe if dc == DC - 1 else None)
        EV[("mmlast", ci)] = evm
        sch.wait("scalar", evm)
        ev1 = sch.step("scalar",
                       lambda e, par=par: e.activation(LAM[par][:], PDEC[:, :],
                                                       AF.Exp, scale=-1.0,
                                                       bias=NCD[:]), S.sac)
        sch.wait("vector", ev1)
        sch.wait("tensor", ev1)
        sch.step("vector", lambda e, par=par: e.tensor_scalar_add(
            LAM[par][:], LAM[par][:], 1.0))
        sch.step("vector", lambda e, par=par: e.reciprocal(LAM[par][:],
                                                           LAM[par][:]))
        ev2 = sch.step("vector",
                       lambda e, par=par: e.tensor_scalar_max(
                           LAM[par][:], LAM[par][:], LAM_MIN), S.sdv)
        sch.wait("scalar", ev2)
        ev3 = sch.step("scalar",
                       lambda e, par=par: e.activation(LLF[par][:], LAM[par][:],
                                                       AF.Ln), S.sac)
        sch.wait("vector", ev3)
        ev4 = sch.step("vector",
                       lambda e, par=par: e.tensor_tensor_scan(
                           LF[par][:], LLF[par][:], ZROF[:], 0.0,
                           ALU.add, ALU.add), S.sdv)
        sch.wait("scalar", ev4)
        if store:
            pe12 = EV.get(("e12row", ci - 2))
            if pe12 is not None:
                sch.wait("scalar", pe12)
        ev5 = sch.step("scalar",
                       lambda e, par=par: e.activation(
                           E1F[par][:], LF[par][:], AF.Exp, scale=-1.0,
                           bias=LF[par][:, C - 1:C]), S.sac)
        ev6 = sch.step("scalar",
                       lambda e, par=par: e.activation(E2F[par][:], LF[par][:],
                                                       AF.Exp), S.sac)
        if store:
            sch.wait("gpsimd", ev6)
            sch.dma("gpsimd", AP(E12ROW, ci * 2 * H * C, [[0, 1], [1, H * C]]),
                    E1F[par][:, :], S.sgp)
            ev_e12 = sch.dma("gpsimd",
                             AP(E12ROW, ci * 2 * H * C + H * C,
                                [[0, 1], [1, H * C]]),
                             E2F[par][:, :], S.sgp)
            EV[("e12row", ci)] = ev_e12
        sch.wait("tensor", ev6)
        pg = EV.get(("gb", ci - 1))
        if pg is not None:
            sch.wait("tensor", pg)
        ev7a = sch.step("tensor", lambda e, par=par: e.transpose(
            PMIS[:, 0:H], E1F[par][:], IDF[0:H, 0:H]), S.spe)
        ev7b = sch.step("tensor", lambda e, par=par: e.transpose(
            PMIS[:, H:2 * H], E2F[par][:], IDF[0:H, 0:H]), S.spe)
        ev7c = sch.step("tensor", lambda e, par=par: e.transpose(
            PMIS[:, 2 * H:3 * H], LF[par][:], IDF[0:H, 0:H]), S.spe)
        sch.wait("vector", ev7c)
        if store:
            def mke(e, ci=ci):
                return e.tensor_copy(
                    AP(E1T, ci * H, [[EROW, 128], [1, H]]), PMIS[:, 0:H])
            sch.step("vector", mke)
            def mke2(e, ci=ci):
                return e.tensor_copy(
                    AP(E2T, ci * H, [[EROW, 128], [1, H]]), PMIS[:, H:2 * H])
            sch.step("vector", mke2)
            def mke3(e, ci=ci):
                return e.tensor_copy(
                    AP(LTM, ci * H, [[EROW, 128], [1, H]]), PMIS[:, 2 * H:3 * H])
            ev9 = sch.step("vector", mke3, S.sdv)
            sch.wait("sync", ev4)
            sch.dma("sync", AP(LROW, ci * H * C, [[0, 1], [1, H * C]]),
                    LF[par][:, :], S.sst)
        else:
            ev9 = sch.step("vector",
                           lambda e, par=par: e.tensor_copy(
                               ETT[par][:, 0:3 * H], PMIS[:, 0:3 * H]), S.sdv)
        EV[("e9", ci)] = ev9

        ev10 = sch.step("tensor",
                        lambda e, par=par: e.transpose(
                            PMIS[0:1, 3 * H:3 * H + H],
                            E2F[par][:, C - 1:C], IDF[0:H, 0:H]), S.spe)
        sch.wait("vector", ev10)
        ev11 = sch.step("vector",
                        lambda e, par=par: e.tensor_copy(
                            GR1[par][:], PMIS[0:1, 3 * H:3 * H + H]), S.sdv)
        if store:
            def mkg(e, ci=ci, par=par):
                return e.tensor_copy(GROW[0:1, ci * H:(ci + 1) * H], GR1[par][:])
            sch.step("vector", mkg)
        sch.wait("tensor", ev11)
        ev12 = sch.step("tensor",
                        lambda e, par=par: e.matmul(
                            PMIS[:, 3 * H:4 * H], ONES1F[0:1, :],
                            GR1[par][:], start=True, stop=True), S.spe)
        sch.wait("vector", ev12)
        ev_gb = sch.step("vector",
                         lambda e, par=par: e.tensor_copy(
                             GB[par][:], PMIS[:, 3 * H:4 * H]), S.sdv)
        sch.wait("tensor", ev_gb)
        EV[("gb", ci)] = ev_gb
        EV[("phik", ci)] = ev_phik

    def p1_stageC(ci, par, store, part):
        if store:
            e1_t, e1_off, e1_row = E1T, ci * H, EROW
            e2_t, e2_off, e2_row = E2T, ci * H, EROW
        else:
            e1_t, e1_off, e1_row = ETT[par], 0, 48
            e2_t, e2_off, e2_row = ETT[par], H, 48
        def mkk1(e, par=par, e1_t=e1_t, e1_off=e1_off, e1_row=e1_row):
            dst = AP(KB[par], 0, [[H * 128, 128], [128, H], [1, KD]])
            srcp = AP(PHB[par], 0, [[D, 128], [KD, H], [1, KD]])
            fac = AP(e1_t, e1_off, [[e1_row, 128], [1, H], [0, KD]])
            return e.tensor_tensor(dst, srcp, fac, ALU.mult)
        sch.step("vector", mkk1)
        def mkk2(e, par=par, e2_t=e2_t, e2_off=e2_off, e2_row=e2_row):
            dst = AP(KB[par], KD, [[H * 128, 128], [128, H], [1, KD]])
            srcp = AP(PHB[par], 0, [[D, 128], [KD, H], [1, KD]])
            fac = AP(e2_t, e2_off, [[e2_row, 128], [1, H], [0, KD]])
            return e.tensor_tensor(dst, srcp, fac, ALU.mult)
        evk2 = sch.step("vector", mkk2, S.sdv)
        EV[("kb", ci)] = evk2
        sch.wait("tensor", evk2)
        pe_prev = EV.get(("uwev", ci - 1))
        if pe_prev is not None:
            sch.wait("tensor", pe_prev)
        uw_banks = [(PUW1, 0, 7), (PUW2, 7, 7), (PUW3, 14, 2)]
        ev_uw = None
        for bank, h0, nh in uw_banks:
            for hh in range(nh):
                h = h0 + hh
                def mk(e, h=h, hh=hh, bank=bank, par=par):
                    rhs = AP(VT[par], h * KV, [[HKV, 128], [1, KV]])
                    return e.matmul(bank[:, hh * KV:(hh + 1) * KV],
                                    KB[par][:, h * 128:(h + 1) * 128], rhs,
                                    start=True, stop=True)
                ev_uw = sch.step("tensor", mk, S.spe if h == H - 1 else None)
        EV[("uwmm", ci)] = ev_uw
        sch.wait("scalar", ev_uw)
        sch.step("scalar", lambda e, par=par: e.activation(
            UW[par][:, 0:7 * KV], PUW1[:, :], AF.Identity))
        sch.step("scalar", lambda e, par=par: e.activation(
            UW[par][:, 7 * KV:14 * KV], PUW2[:, :], AF.Identity))
        evc = sch.step("scalar", lambda e, par=par: e.activation(
            UW[par][:, 14 * KV:16 * KV], PUW3[:, :], AF.Identity), S.sac)
        EV[("uwev", ci)] = evc
        sch.wait("vector", evc)
        sch.wait("vector", EV[("gb", ci)])

        sch.step("vector", lambda e, par=par: e.tensor_copy(FA[0:64, :],
                                                            GB[par][0:64, :]))
        sch.step("vector", lambda e: e.tensor_copy(FB[64:128, :],
                                                   GAMR[64:128, :]))
        def mk_t(e, par=par):
            fac = AP(FB, 0, [[H, 128], [1, H], [0, KV]])
            return e.tensor_tensor(TTMP[:, :], UW[par][:, :], fac, ALU.mult)
        sch.step("vector", mk_t)
        def mk_s(e):
            fac = AP(FA, 0, [[H, 128], [1, H], [0, KV]])
            return e.tensor_tensor(STATE[:, :], STATE[:, :], fac, ALU.mult)
        sch.step("vector", mk_s)
        sch.step("vector", lambda e: e.tensor_tensor(STATE[:, :], STATE[:, :],
                                                     TTMP[:, :], ALU.add))
        ev_end = sch.step("vector",
                          lambda e, par=par: e.tensor_tensor(
                              GAMR[64:128, :], GAMR[64:128, :],
                              GB[par][64:128, :], ALU.mult), S.sdv)
        if (ci + 1) % g.group == 0:
            gidx = ci // g.group
            base = (0 if part == "own" else g.n_groups) + gidx
            off = base * (H * (KV + 1))
            def mg1(e, off=off):
                return e.tensor_copy(GSTORE[64:128, off:off + HKV],
                                     STATE[64:128, :])
            sch.step("vector", mg1)
            def mg2(e, off=off):
                return e.tensor_copy(
                    GSTORE[64:128, off + HKV:off + HKV + H], GAMR[64:128, :])
            sch.step("vector", mg2)
            sch.step("vector", lambda e: e.memset(STATE[64:128, :], 0.0))
            ev_end = sch.step("vector",
                              lambda e: e.memset(GAMR[64:128, :], 1.0), S.sdv)
        EV[("Cend", ci)] = ev_end

    def pass1_part(part, x_t, store):
        for _eng in ("vector", "scalar", "tensor"):
            sch.wait(_eng, sch.val(S.spe))
            sch.wait(_eng, sch.val(S.sdv))
            sch.wait(_eng, sch.val(S.sac))
        sch.wait("vector", sch.val(S.sst))
        sch.step("vector", lambda e: e.memset(VT[0][:], 1.0))
        sch.step("vector", lambda e: e.memset(VT[1][:], 1.0))
        sch.step("vector", lambda e: e.memset(STATE[:], 0.0))
        sch.step("vector", lambda e: e.memset(GAMR[64:128, :], 1.0))
        sch.step("vector", lambda e: e.memset(ZROF[:], 0.0), S.sdv)
        EV.clear()
        n = g.nc_own
        for i in range(n + 2):
            if i < n:
                p1_stageA(i, i % 2, x_t, store)
            if 0 <= i - 1 < n:
                p1_stageB(i - 1, (i - 1) % 2, store)
            if 0 <= i - 2 < n:
                p1_stageC(i - 2, (i - 2) % 2, store, part)

    ev_wk = wload(WK, k_w, D)
    ev_wv = wload(WV, v_w, D)
    sch.wait("tensor", ev_wv)

    def bias_fold2(W, b_t, cdram, ckbuf):
        ev_br = sch.dma("sync", BR1[0:1, :], AP(b_t, 0, [[0, 1], [1, D]]),
                        S.sld)
        sch.wait("vector", ev_br)
        evv = None
        for gi in range(2):
            evm = None
            for dc in range(DC):
                def mk(e, dc=dc, gi=gi, W=W):
                    return e.matmul(PKV[0:1, :], LB1[:, dc:dc + 1],
                                    W[:, dc * D + gi * 512:
                                      dc * D + gi * 512 + 512],
                                    start=(dc == 0), stop=(dc == DC - 1))
                evm = sch.step("tensor", mk, S.spe if dc == DC - 1 else None)
            sch.wait("vector", evm)
            def mkv(e, gi=gi):
                return e.tensor_tensor(TB1[0:1, gi * 512:(gi + 1) * 512],
                                       PKV[0:1, :],
                                       BR1[0:1, gi * 512:(gi + 1) * 512],
                                       ALU.add)
            evv = sch.step("vector", mkv, S.sdv)
            sch.wait("tensor", evv)
        sch.wait("sync", evv)
        evs = sch.dma("sync", AP(cdram, 0, [[0, 1], [1, D]]), TB1[0:1, :],
                      S.sst)
        sch.wait("sync", evs)
        evl = sch.dma("sync", ckbuf[:], AP(cdram, 0, [[0, 128], [1, D]]),
                      S.sld)
        return evl

    ev_ck = bias_fold2(WK, k_b, CKB, CKb)
    ev_cv = bias_fold2(WV, v_b, CVB, CVb)
    for W in (WK, WV):
        for dc in range(DC):
            def mks(e, W=W, dc=dc):
                return e.tensor_scalar_mul(W[:, dc * D:(dc + 1) * D],
                                           W[:, dc * D:(dc + 1) * D],
                                           LW1[:, dc:dc + 1])
            sch.step("vector", mks, S.sdv)
    sch.barrier([sch.val(S.sdv), sch.val(S.sld)])

    pass1_part("own", x_own, True)

    sch.step("vector", lambda e: e.memset(TTMP[64:128, :], 0.0))
    def roll_groups_own():
        for gidx in reversed(range(g.n_groups)):
            off = gidx * (H * (KV + 1))
            def mk1(e, off=off):
                fac = AP(GSTORE, 64 * GSW + off + HKV,
                         [[GSW, 64], [1, H], [0, KV]])
                return e.tensor_tensor(TTMP[64:128, :], TTMP[64:128, :], fac,
                                       ALU.mult)
            sch.step("vector", mk1)
            def mk2(e, off=off):
                return e.tensor_tensor(TTMP[64:128, :], TTMP[64:128, :],
                                       GSTORE[64:128, off:off + HKV], ALU.add)
            sch.step("vector", mk2)
    roll_groups_own()
    ev_blob = sch.step("vector", lambda e: e.engine_nop(), S.sdv)
    sch.wait("gpsimd", ev_blob)
    sch.dma("gpsimd", AP(EXCI, 0, [[HKV, 64], [1, HKV]]), STATE[0:64, :],
            S.sgp)
    ev_exp = sch.dma("gpsimd", AP(EXCI, 64 * HKV, [[HKV, 64], [1, HKV]]),
                     TTMP[64:128, :], S.sgp)
    sch.ops["gpsimd"].append(lambda e, v=sch.ctr["sgp"]: e.wait_ge(S.sgp, v))
    groups = [[2 * i, 2 * i + 1] for i in range(max(1, n_cores // 2))]
    ev_cc = sch.step(
        "gpsimd",
        lambda e: e.collective_compute(
            "AllGather", ALU.bypass, replica_groups=groups,
            ins=[EXCI.ap().opt()], outs=[EXCO.ap().opt()]), S.scc)
    sch.wait("sync", ev_cc)
    sch.dma("sync", TTMP[0:64, :], AP(EXCO, 0, [[HKV, 64], [1, HKV]]), S.sld)
    ev_im = sch.dma("sync", TTMP[64:128, :],
                    AP(EXCO, 128 * HKV + 64 * HKV, [[HKV, 64], [1, HKV]]),
                    S.sld)
    sch.wait("vector", ev_im)
    sch.step("vector",
             lambda e: e.tensor_scalar_mul(STATE[0:64, :], TTMP[0:64, :],
                                           HM[0:64, 0:1]))
    sch.step("vector", lambda e: e.tensor_scalar_mul(TTMP[64:128, :],
                                                     TTMP[64:128, :],
                                                     HM[64:128, 1:2]))
    for gidx in reversed(range(g.n_groups)):
        def mka(e, gidx=gidx):
            return e.tensor_copy(RANCH[64:128, gidx * HKV:(gidx + 1) * HKV],
                                 TTMP[64:128, :])
        sch.step("vector", mka)
        if gidx > 0:
            off = gidx * (H * (KV + 1))
            def mk1(e, off=off):
                fac = AP(GSTORE, 64 * GSW + off + HKV,
                         [[GSW, 64], [1, H], [0, KV]])
                return e.tensor_tensor(TTMP[64:128, :], TTMP[64:128, :], fac,
                                       ALU.mult)
            sch.step("vector", mk1)
            def mk2(e, off=off):
                return e.tensor_tensor(TTMP[64:128, :], TTMP[64:128, :],
                                       GSTORE[64:128, off:off + HKV], ALU.add)
            sch.step("vector", mk2)
    ev_p1 = sch.step("vector", lambda e: e.engine_nop(), S.sdv)
    sch.barrier([ev_p1, sch.val(S.sst), sch.val(S.spe), sch.val(S.sac),
                 sch.val(S.sgp)])
    p1s.close()
    if stop_after == "p1":
        return _early_out()

    pas = ExitStack()
    WQ = sb("WQ", (128, DC * D), BF16, pas)
    WO = sb("WO", (128, DC * D), BF16, pas)
    CQ = sb("CQ", (128, DC), F32, pas)
    TB2 = sb("TB2", (1, D), F32, pas)
    BR2 = sb("BR2", (1, D), F32, pas)
    XT2_ = sb("XT2", (128, D), F32, pas)
    XT2 = [XT2_, XT2_]
    XN2 = [sb(f"XN2{i}", (128, D), F32, pas) for i in range(2)]
    ZF2 = [sb(f"ZF2{i}", (128, D), BF16, pas) for i in range(2)]
    LB4 = [sb(f"LB4{i}", (128, H * C), F32, pas) for i in range(2)]
    PKT = [sb(f"PKT{i}", (128, D), BF16, pas) for i in range(2)]
    PVT = [sb(f"PVT{i}", (128, HKV), BF16, pas) for i in range(2)]
    PHQ = [sb(f"PHQ{i}", (128, D), BF16, pas) for i in range(2)]
    PHQ2 = [sb(f"PHQ2{i}", (64, H * 128), BF16, pas) for i in range(2)]
    PHKF2 = [sb(f"PHKF2{i}", (64, H * 128), BF16, pas) for i in range(2)]
    ATT = [sb(f"ATT{i}", (128, D), BF16, pas) for i in range(2)]
    TQ = sb("TQ", (128, D), BF16, pas)
    TE = sb("TE", (128, D), BF16, pas)
    DT = sb("DT", (128, 512), F32, pas)
    ABB = [sb(f"ABB{i}", (128, 512), BF16, pas) for i in range(2)]
    E12B_ = sb("E12B", (64, 2 * H * C), BF16, pas)
    E12B = [E12B_, E12B_]
    PHQA2 = sb("PHQA2", (64, H * 128), BF16, pas)
    PHQB2 = sb("PHQB2", (64, H * 128), BF16, pas)
    RD8 = sb("RD8", (128, 4), F32, pas)
    ATF = sb("ATF", (128, D), BF16, pas)
    YT = sb("YT", (128, D), F32, pas)
    Z2T = sb("Z2T", (128, D), BF16, pas)
    ST2 = sb("ST2", (128, 8), F32, pas)
    RBUF = sb("RBUF", (128, g.group * HKV), BF16, pas)
    KB2 = sb("KB2", (128, H * 128), BF16, pas)
    SBF = sb("SBF", (128, HKV), BF16, pas)
    GKT = sb("GKT", (128, D), BF16, pas)
    GVT = sb("GVT", (128, HKV), BF16, pas)
    UW2 = sb("UW2", (128, HKV), F32, pas)
    GB2 = sb("GB2", (128, H), F32, pas)

    PQ = [ps(f"PQ{i}", (128, 512), F32, stack=pas) for i in range(2)]
    PTR2 = ps("PTR2", (128, 1024), BF16, stack=pas)
    PSC = ps("PSC", (128, 512), F32, stack=pas)
    PG = ps("PG", (128, 512), F32, stack=pas)
    PO = [ps(f"PO{i}", (128, 4 * KV), F32, stack=pas) for i in range(2)]

    def ln_affine2(x_ap, out_ap, junk):
        return ln_affine(x_ap, out_ap, junk, ST2, tmp_ap=x_ap)

    def gamma_bcast(ci, evgate=None):
        if evgate is not None:
            sch.wait("tensor", evgate)
        def mk(e, ci=ci):
            return e.matmul(PO[1][:, 0:H], ONES1F[0:1, :],
                            GROW[0:1, ci * H:(ci + 1) * H],
                            start=True, stop=True)
        ev = sch.step("tensor", mk, S.spe)
        sch.wait("vector", ev)
        evc = sch.step("vector", lambda e: e.tensor_copy(GB2[:], PO[1][:, 0:H]),
                       S.sdv)
        sch.wait("tensor", evc)
        return evc

    def recompute_uw(ci, want_u, want_w, kt, vt):
        def mkk1(e, ci=ci, kt=kt):
            dst = AP(KB2, 0, [[H * 128, 128], [128, H], [1, KD]])
            srcp = AP(kt, 0, [[D, 128], [KD, H], [1, KD]])
            fac = AP(E1T, ci * H, [[EROW, 128], [1, H], [0, KD]])
            return e.tensor_tensor(dst, srcp, fac, ALU.mult)
        def mkk2(e, ci=ci, kt=kt):
            dst = AP(KB2, KD, [[H * 128, 128], [128, H], [1, KD]])
            srcp = AP(kt, 0, [[D, 128], [KD, H], [1, KD]])
            fac = AP(E2T, ci * H, [[EROW, 128], [1, H], [0, KD]])
            return e.tensor_tensor(dst, srcp, fac, ALU.mult)
        sch.step("vector", mkk1)
        evk = sch.step("vector", mkk2, S.sdv)
        sch.wait("tensor", evk)
        banks = [(PSC, 0, 7, 0), (PG, 7, 7, 0), (PO[0], 14, 2, 0)]
        ev_uw = None
        for bank, h0, nh, co in banks:
            for hh in range(nh):
                h = h0 + hh
                def mk(e, h=h, hh=hh, bank=bank, vt=vt):
                    rhs = AP(vt, h * KV, [[HKV, 128], [1, KV]])
                    return e.matmul(bank[:, hh * KV:(hh + 1) * KV],
                                    KB2[:, h * 128:(h + 1) * 128], rhs,
                                    start=True, stop=True)
                ev_uw = sch.step("tensor", mk, S.spe if h == H - 1 else None)
        sch.wait("scalar", ev_uw)
        ev = None
        rows = []
        if want_u:
            rows.append((0, 64))
        if want_w:
            rows.append((64, 128))
        for r0, r1 in rows:
            sch.step("scalar", lambda e, r0=r0, r1=r1:
                     e.activation(UW2[r0:r1, 0:7 * KV], PSC[r0:r1, 0:7 * KV],
                                  AF.Identity))
            sch.step("scalar", lambda e, r0=r0, r1=r1:
                     e.activation(UW2[r0:r1, 7 * KV:14 * KV],
                                  PG[r0:r1, 0:7 * KV], AF.Identity))
            ev = sch.step("scalar", lambda e, r0=r0, r1=r1:
                          e.activation(UW2[r0:r1, 14 * KV:16 * KV],
                                       PO[0][r0:r1, 0:2 * KV], AF.Identity),
                          S.sac)
        sch.wait("tensor", ev)
        sch.wait("vector", ev)
        return ev

    xs = (S.sx0, S.sx1)
    ys = (S.sy0, S.sy1)
    zs = (S.sz0, S.sz1)
    EV2 = {}
    ev_sb0 = sch.step("vector",
                      lambda e: e.tensor_copy(SBF[0:64, :], STATE[0:64, :]),
                      S.sdv)
    sch.wait("tensor", ev_sb0)

    ev_wq = wload(WQ, q_w, D)
    ev_wo = wload(WO, o_w, D)
    sch.wait("tensor", ev_wq)
    ev_br = sch.dma("sync", BR2[0:1, :], AP(q_b, 0, [[0, 1], [1, D]]), S.sld)
    sch.wait("vector", ev_br)
    evv = None
    for gi in range(2):
        evm = None
        for dc in range(DC):
            def mk(e, dc=dc, gi=gi):
                return e.matmul(PQ[0][0:1, :], LB1[:, dc:dc + 1],
                                WQ[:, dc * D + gi * 512:
                                   dc * D + gi * 512 + 512],
                                start=(dc == 0), stop=(dc == DC - 1))
            evm = sch.step("tensor", mk, S.spe if dc == DC - 1 else None)
        sch.wait("vector", evm)
        def mkv(e, gi=gi):
            return e.tensor_tensor(TB2[0:1, gi * 512:(gi + 1) * 512],
                                   PQ[0][0:1, :],
                                   BR2[0:1, gi * 512:(gi + 1) * 512], ALU.add)
        evv = sch.step("vector", mkv, S.sdv)
        sch.wait("tensor", evv)
    sch.wait("sync", evv)
    ev_cq = sch.dma("sync", AP(CQB, 0, [[0, 1], [1, D]]), TB2[0:1, :], S.sst)
    sch.wait("sync", ev_cq)
    sch.dma("sync", CQ[:], AP(CQB, 0, [[1, 128], [128, DC]]), S.sld, slow=True)
    for dc in range(DC):
        def mks(e, dc=dc):
            return e.tensor_scalar_mul(WQ[:, dc * D:(dc + 1) * D],
                                       WQ[:, dc * D:(dc + 1) * D],
                                       LW1[:, dc:dc + 1])
        sch.step("vector", mks, S.sdv)
    sch.wait("tensor", ev_wo)
    sch.barrier([sch.val(S.sdv), sch.val(S.sld)])

    def pa_stageA(ci, par):
        if ci % g.group == 0:
            prev_sr = EV2.get(("sroll", ci - 1))
            if prev_sr is not None:
                sch.wait("vector", prev_sr)
                sch.wait("gpsimd", prev_sr)
            gidx = ci // g.group
            last = min(ci + g.group, g.nc_own) - 1
            def mkan(e, gidx=gidx):
                return e.tensor_copy(STATE[64:128, :],
                                     RANCH[64:128, gidx * HKV:(gidx + 1) * HKV])
            sch.step("vector", mkan)
            def mkst(e, lc=last - ci):
                return e.tensor_copy(
                    AP(RBUF, 64 * (g.group * HKV) + lc * HKV,
                       [[g.group * HKV, 64], [1, HKV]]),
                    STATE[64:128, :])
            ev_r = sch.step("vector", mkst, S.sdv)
            sch.wait("gpsimd", ev_r)
            sch.dma("gpsimd",
                    AP(RBUF, (last - ci) * HKV, [[g.group * HKV, 64], [1, HKV]]),
                    AP(RBUF, 64 * (g.group * HKV) + (last - ci) * HKV,
                       [[g.group * HKV, 64], [1, HKV]]), S.sgp)
            ev_gload = None
            for c2 in range(last - 1, ci - 1, -1):
                sch.dma("sync", GKT[:],
                        AP(PHIKs, (c2 + 1) * 128 * D, [[D, 128], [1, D]]),
                        S.sld)
                ev_gload = sch.dma(
                    "sync", GVT[:],
                    AP(VREs, (c2 + 1) * 128 * HKV, [[HKV, 128], [1, HKV]]),
                    S.sld)
                sch.wait("vector", ev_gload)
                sch.wait("tensor", ev_gload)
                ev_gb2 = gamma_bcast(c2 + 1)
                ev_w = recompute_uw(c2 + 1, False, True, GKT, GVT)
                sch.wait("vector", ev_w)
                sch.wait("sync", ev_w)
                def mkr1(e):
                    fac = AP(GB2, 64 * H, [[H, 64], [1, H], [0, KV]])
                    return e.tensor_tensor(STATE[64:128, :], STATE[64:128, :],
                                           fac, ALU.mult)
                sch.step("vector", mkr1)
                sch.step("vector",
                         lambda e: e.tensor_tensor(STATE[64:128, :],
                                                   STATE[64:128, :],
                                                   UW2[64:128, :], ALU.add))
                def mkst2(e, lc=c2 - ci):
                    return e.tensor_copy(
                        AP(RBUF, 64 * (g.group * HKV) + lc * HKV,
                           [[g.group * HKV, 64], [1, HKV]]),
                        STATE[64:128, :])
                ev_r = sch.step("vector", mkst2, S.sdv)
                sch.wait("gpsimd", ev_r)
                sch.dma("gpsimd",
                        AP(RBUF, (c2 - ci) * HKV, [[g.group * HKV, 64], [1, HKV]]),
                        AP(RBUF, 64 * (g.group * HKV) + (c2 - ci) * HKV,
                           [[g.group * HKV, 64], [1, HKV]]), S.sgp)
            sch.wait("tensor", sch.val(S.sgp))

        pq = EV2.get(("qmm", ci - 2))
        if pq is not None:
            sch.wait("sync", pq)
        sch.dma("sync", ZF2[par][:],
                AP(ZFMs, ci * C, [[g.t_own, 128], [128 * g.t_own, DC], [1, C]]),
                zs[par])
        patt = EV2.get(("att", ci - 2))
        if patt is not None:
            sch.wait("sync", patt)
        sch.dma("sync", LB4[par][:],
                AP(LROW, ci * H * C, [[0, 128], [1, H * C]]), zs[par])
        psr = EV2.get(("sroll", ci - 2))
        if psr is not None:
            sch.wait("sync", psr)
        sch.dma("sync", PKT[par][:],
                AP(PHIKs, ci * 128 * D, [[D, 128], [1, D]]), zs[par])
        ev_z = sch.dma("sync", PVT[par][:],
                       AP(VREs, ci * 128 * HKV, [[HKV, 128], [1, HKV]]),
                       zs[par])
        EV2[("z", ci)] = ev_z

        sch.wait("tensor", ev_z)
        pqf = EV2.get(("qev", ci - 1))
        if pqf is not None:
            sch.wait("tensor", pqf)
        po = EV2.get(("oev", ci - 2))
        if po is not None:
            sch.wait("tensor", po)
        evm = None
        for do in range(DC):
            bank = PQ[do // 4]
            col = (do % 4) * 128
            for dc in range(DC):
                def mk(e, do=do, dc=dc, bank=bank, col=col, par=par):
                    return e.matmul(bank[:, col:col + 128],
                                    WQ[:, dc * D + do * 128:
                                       dc * D + do * 128 + 128],
                                    ZF2[par][:, dc * 128:(dc + 1) * 128],
                                    start=(dc == 0), stop=(dc == DC - 1))
                evm = sch.step("tensor", mk,
                               S.spe if (dc == DC - 1) else None)
        EV2[("qmm", ci)] = evm
        sch.wait("vector", evm)
        def mkb0(e):
            fac = AP(CQ, 0, [[DC, 128], [1, 4], [0, 128]])
            return e.tensor_tensor(TQ[:, 0:512], PQ[0][:, :], fac, ALU.add)
        sch.step("vector", mkb0)
        def mkb1(e):
            fac = AP(CQ, 4, [[DC, 128], [1, 4], [0, 128]])
            return e.tensor_tensor(TQ[:, 512:1024], PQ[1][:, :], fac, ALU.add)
        evq = sch.step("vector", mkb1, S.sdv)
        sch.wait("scalar", evq)
        sch.step("scalar", lambda e: e.activation(TE[:], TQ[:], AF.Exp))
        evr = sch.step("scalar", lambda e: e.activation(TQ[:], TQ[:], AF.Relu),
                       S.sac)
        sch.wait("vector", evr)
        sch.step("vector", lambda e: e.tensor_scalar_min(TE[:], TE[:], 1.0))
        def mkphi(e, par=par):
            return e.tensor_tensor(PHQ[par][:], TE[:], TQ[:], ALU.add)
        evphi = sch.step("vector", mkphi, S.sdv)
        EV2[("qev", ci)] = evphi
        def mkq2(e, par=par):
            dst = AP(PHQ2[par], 0, [[H * 128, 64], [256, 8], [1, 128]])
            srcp = AP(PHQ[par], 0, [[D, 64], [128, 8], [1, 128]])
            return e.tensor_copy(dst, srcp)
        ev_q2a = sch.step("vector", mkq2, S.sdv)
        sch.wait("gpsimd", ev_q2a)
        def mkq2b(e, par=par):
            dst = AP(PHQ2[par], 128, [[H * 128, 64], [256, 8], [1, 128]])
            srcp = AP(PHQ[par], 64 * D, [[D, 64], [128, 8], [1, 128]])
            return e.dma_start(dst, srcp)
        ev_q2b = sch.step("gpsimd", mkq2b, S.sgp, 16)
        EV2[("q2", ci)] = ev_q2b

    def pa_stageB(ci, par):
        pc2 = EV2.get(("c2", ci - 2))
        if pc2 is not None:
            sch.wait("sync", pc2)
        sch.dma("sync", XN2[par][:],
                AP(XN32, ci * C * D, [[D, 128], [1, D]]), ys[par])
        pqs = EV2.get(("qsc", ci - 1))
        if pqs is not None:
            sch.wait("sync", pqs)
        ev_y = sch.dma("sync", E12B[par][:],
                       AP(E12ROW, ci * 2 * H * C, [[0, 64], [1, 2 * H * C]]),
                       ys[par])
        EV2[("y", ci)] = ev_y
        EV2[("e12", ci)] = ev_y
        ps_ = EV2.get(("sroll", ci - 1))
        if ps_ is not None:
            sch.wait("tensor", ps_)
        ptrf = EV2.get(("atf", ci - 1))
        if ptrf is not None:
            sch.wait("tensor", ptrf)
        for rnd in range(2):
            evt = None
            for hh in range(8):
                h = rnd * 8 + hh
                def mkt(e, h=h, hh=hh, par=par):
                    off = (h // 2) * 128 + (h % 2) * 64
                    return e.transpose(
                        PTR2[0:64, hh * 128:(hh + 1) * 128],
                        AP(PKT[par], off, [[D, 128], [1, 64]]), IDENT[:])
                evt = sch.step("tensor", mkt, S.spe if hh == 7 else None)
            sch.wait("vector", evt)
            def mkev(e, rnd=rnd, par=par):
                return e.tensor_copy(PHKF2[par][0:64, rnd * 1024:(rnd + 1) * 1024],
                                     PTR2[0:64, 0:1024])
            ev_pkf = sch.step("vector", mkev, S.sdv)
            sch.wait("tensor", ev_pkf)

        sch.wait("tensor", EV2[("q2", ci)])
        sch.wait("vector", EV2[("q2", ci)])
        sch.wait("vector", EV2[("e12", ci)])
        pao = EV2.get(("abb", (ci - 1, 3)))
        if pao is not None:
            sch.wait("vector", pao)
        def mksa(e, par=par):
            return e.tensor_tensor(PHQA2[:], PHQ2[par][:],
                                   E12B[par][:, H * C:2 * H * C], ALU.mult)
        sch.step("vector", mksa)
        def mksb(e, par=par):
            return e.tensor_tensor(PHQB2[:], PHQ2[par][:],
                                   E12B[par][:, 0:H * C], ALU.mult)
        ev_qsc = sch.step("vector", mksb, S.sdv)
        EV2[("qsc", ci)] = ev_qsc
        sch.wait("tensor", ev_qsc)
        ev_combine = None
        for b4 in range(4):
            def mksub(e, b4=b4, ci=ci, par=par):
                fac = AP(LTM, ci * H + b4 * 4, [[EROW, 128], [1, 4], [0, 128]])
                return e.tensor_tensor(DT[:],
                                       LB4[par][:, b4 * 512:(b4 + 1) * 512],
                                       fac, ALU.subtract)
            ev_g = sch.step("vector", mksub, S.sdv)
            sch.wait("scalar", ev_g)
            sch.step("scalar", lambda e: e.activation(DT[:], DT[:], AF.Abs))
            ev_d = sch.step("scalar",
                            lambda e: e.activation(DT[:], DT[:], AF.Exp,
                                                   scale=-1.0), S.sac)
            ev_sc = None
            for hh in range(4):
                h = b4 * 4 + hh
                def mksc(e, h=h, hh=hh, par=par):
                    lhsT = AP(PHKF2[par], h * 128, [[H * 128, 64], [1, 128]])
                    rhs = AP(PHQ2[par], h * 128, [[H * 128, 64], [1, 128]])
                    return e.matmul(PSC[:, hh * 128:(hh + 1) * 128], lhsT,
                                    rhs, start=True, stop=True)
                ev_sc = sch.step("tensor", mksc, S.spe if hh == 3 else None)
            sch.wait("vector", ev_sc)
            sch.wait("vector", ev_d)
            pab = EV2.get(("abb", (ci, b4 - 2)))
            if pab is not None:
                sch.wait("vector", pab)
            def mka(e, b4=b4):
                return e.tensor_tensor(ABB[b4 % 2][:], PSC[:, :], DT[:],
                                       ALU.mult)
            ev_a = sch.step("vector", mka, S.sdv)
            sch.wait("tensor", ev_a)
            pdv = (EV2.get(("div", (ci, b4 - 2))) if b4 >= 2 else
                   EV2.get(("div", (ci - 1, b4 + 2))))
            if pdv is not None:
                sch.wait("tensor", pdv)
            ev_o = None
            for hh in range(4):
                h = b4 * 4 + hh
                def mko1(e, hh=hh, par=par, h=h, b4=b4):
                    rhs = AP(PVT[par], h * KV, [[HKV, 128], [1, KV]])
                    return e.matmul(PO[b4 % 2][:, hh * KV:(hh + 1) * KV],
                                    AP(ABB[b4 % 2], hh * 128,
                                       [[512, 128], [1, 128]]),
                                    rhs, start=True, stop=False)
                sch.step("tensor", mko1)
                def mko2(e, h=h, hh=hh, b4=b4):
                    lhsT = AP(PHQA2, h * 128, [[H * 128, 64], [1, 128]])
                    rhs = AP(SBF, h * KV, [[HKV, 64], [1, KV]])
                    return e.matmul(PO[b4 % 2][:, hh * KV:(hh + 1) * KV], lhsT,
                                    rhs, start=False, stop=False)
                sch.step("tensor", mko2)
                def mko3(e, h=h, hh=hh, b4=b4, lc=ci % g.group):
                    lhsT = AP(PHQB2, h * 128, [[H * 128, 64], [1, 128]])
                    rhs = AP(RBUF, lc * HKV + h * KV,
                             [[g.group * HKV, 64], [1, KV]])
                    return e.matmul(PO[b4 % 2][:, hh * KV:(hh + 1) * KV], lhsT,
                                    rhs, start=False, stop=True)
                ev_o = sch.step("tensor", mko3, S.spe if hh == 3 else None)
            EV2[("abb", (ci, b4))] = ev_o
            sch.wait("vector", ev_o)
            def mkden(e, b4=b4):
                srcp = AP(PO[b4 % 2], KD, [[4 * KV, 128], [KV, 4]])
                return e.tensor_scalar_max(RD8[:], srcp, DEN_EPS)
            sch.step("vector", mkden)
            sch.step("vector", lambda e: e.reciprocal(RD8[:], RD8[:]))
            if b4 == 0:
                paf = EV2.get(("atf", ci - 2))
                if paf is not None:
                    sch.wait("vector", paf)
            def mkdiv(e, b4=b4, par=par):
                srcp = AP(PO[b4 % 2], 0, [[4 * KV, 128], [KV, 4], [1, KD]])
                fac = AP(RD8, 0, [[4, 128], [1, 4], [0, KD]])
                dst = AP(ATT[par], b4 * 256, [[D, 128], [KD, 4], [1, KD]])
                return e.tensor_tensor(dst, srcp, fac, ALU.mult)
            ev_combine = sch.step("vector", mkdiv, S.sdv)
            EV2[("div", (ci, b4))] = ev_combine
        EV2[("att", ci)] = ev_combine
        ev_gb3 = gamma_bcast(ci, evgate=ev_combine)
        ev_u = recompute_uw(ci, True, False, PKT[par], PVT[par])
        sch.wait("vector", ev_u)
        sch.wait("vector", ev_gb3)
        def mks1(e):
            fac = AP(GB2, 0, [[H, 64], [1, H], [0, KV]])
            return e.tensor_tensor(STATE[0:64, :], STATE[0:64, :], fac, ALU.mult)
        sch.step("vector", mks1)
        sch.step("vector",
                 lambda e: e.tensor_tensor(STATE[0:64, :], STATE[0:64, :],
                                           UW2[0:64, :], ALU.add))
        ev_sr = sch.step("vector",
                         lambda e: e.tensor_copy(SBF[0:64, :], STATE[0:64, :]),
                         S.sdv)
        sch.wait("tensor", ev_sr)
        EV2[("sroll", ci)] = ev_sr

    def pa_stageC(ci, par):
        sch.wait("tensor", EV2[("att", ci)])
        evt = None
        for dc in range(DC):
            def mkt(e, dc=dc, par=par):
                return e.transpose(PTR2[:, dc * 128:(dc + 1) * 128],
                                   ATT[par][:, dc * 128:(dc + 1) * 128],
                                   IDENT[:])
            evt = sch.step("tensor", mkt, S.spe if dc == DC - 1 else None)
        sch.wait("vector", evt)
        ev_atf = sch.step("vector", lambda e: e.tensor_copy(ATF[:], PTR2[:]),
                          S.sdv)
        EV2[("atf", ci)] = ev_atf
        sch.wait("tensor", ev_atf)
        sch.wait("sync", sch.val(S.sst))
        ev_x = sch.dma("sync", XT2[par][:],
                       AP(x_own, ci * C * D, [[D, 128], [1, D]]), xs[par])
        sch.wait("vector", EV2[("y", ci)])
        sch.wait("vector", sch.val(S.sst))
        ev_yb = None
        for got in range(2):
            bank = PQ[got]
            evm = None
            for dc in range(DC):
                def mk(e, dc=dc, got=got, bank=bank):
                    return e.matmul(bank[:, :],
                                    ATF[:, dc * 128:(dc + 1) * 128],
                                    WO[:, dc * D + got * 512:
                                       dc * D + got * 512 + 512],
                                    start=(dc == 0), stop=(dc == DC - 1))
                evm = sch.step("tensor", mk, S.spe if dc == DC - 1 else None)
            sch.wait("vector", evm)
            def mky(e, got=got, bank=bank, par=par):
                return e.scalar_tensor_tensor(
                    YT[:, got * 512:(got + 1) * 512], bank[:, :], 1.0,
                    XN2[par][:, got * 512:(got + 1) * 512], ALU.mult, ALU.add)
            sch.step("vector", mky)
            ev_yb = sch.step("vector",
                             lambda e, got=got:
                             e.tensor_tensor(YT[:, got * 512:(got + 1) * 512],
                                             YT[:, got * 512:(got + 1) * 512],
                                             OBb[:, got * 512:(got + 1) * 512],
                                             ALU.add), S.sdv)
            sch.wait("tensor", ev_yb)
        EV2[("oev", ci)] = ev_yb
        sch.wait("vector", ev_x)
        def mkp(e, par=par):
            return e.tensor_tensor(XT2[par][:], YT[:], XT2[par][:], ALU.add)
        ev_pt = sch.step("vector", mkp, S.sdv)
        sch.wait("sync", ev_pt)
        sch.dma("sync", AP(PBs, ci * C * D, [[D, 128], [1, D]]), XT2[par][:],
                S.sst)
        sch.wait("scalar", sch.val(S.sst))
        ev_z2 = ln_affine2(YT[:], Z2T[:], XN2[par])
        sch.wait("sync", ev_z2)
        sch.dma("sync", AP(Z2B, ci * C * D, [[D, 128], [1, D]]), Z2T[:], S.sst)
        ev_c2 = sch.step("vector", lambda e: e.engine_nop(), S.sca)
        EV2[("c2", ci)] = ev_c2

    npair = g.nc_own // 2
    for j in range(npair + 2):
        for cc in (2 * (j - 2), 2 * (j - 2) + 1):
            if 0 <= cc < g.nc_own:
                pa_stageC(cc, cc % 2)
        for cc in (2 * (j - 1), 2 * (j - 1) + 1):
            if 0 <= cc < g.nc_own:
                pa_stageB(cc, cc % 2)
        for cc in (2 * j, 2 * j + 1):
            if 0 <= cc < g.nc_own:
                pa_stageA(cc, cc % 2)

    ev_pa = sch.step("vector", lambda e: e.engine_nop(), S.sdv)
    sch.barrier([ev_pa, sch.val(S.sst), sch.val(S.spe), sch.val(S.sac)])
    pas.close()
    att.close()
    if stop_after == "pa":
        return _early_out()

    def ffn_phase(Wa_t, Wb_t, bias1_fm, in_scr, bias2_bc, dst_t, dst_sem,
                  scale_lw2, tag):
        fs = ExitStack()
        mg = g.midgroup
        ntg = mg // 128
        W1S = sb(f"W1S{tag}", (128, DC * FFN), BF16, fs)
        W2S = sb(f"W2S{tag}", (128, FC * D), BF16, fs)
        HMID = sb(f"HMID{tag}", (128, FC * mg), BF16, fs)
        XFW = sb(f"XFW{tag}", (128, DC * mg), BF16, fs)
        PSL = [sb(f"PSL{tag}{i}", (128, D), F32, fs) for i in range(2)]
        OT = [sb(f"OT{tag}{i}", (128, 512), F32, fs) for i in range(2)]
        PH = [ps(f"PH{tag}{i}", (128, mg), F32, stack=fs) for i in range(2)]
        PACC = [ps(f"PACC{tag}{i}", (128, 512), F32, stack=fs) for i in range(2)]

        for j in range(8):
            sch.dma("gpsimd",
                    AP(W1S, j * 512,
                       [[DC * FFN, 128], [FFN, DC], [1, 512]]),
                    AP(Wa_t, j * 512,
                       [[FFN, 128], [128 * FFN, DC], [1, 512]]),
                    (S.sw0, S.sw1)[j % 2])
        sch.ops["gpsimd"].append(
            lambda e, v=sch.ctr[S.sw0.name]: e.wait_ge(S.sw0, v))
        sch.ops["gpsimd"].append(
            lambda e, v=sch.ctr[S.sw1.name]: e.wait_ge(S.sw1, v))
        ev_w1rdy = sch.step("gpsimd", lambda e: e.engine_nop(), S.sw2)
        ev_w2a = ev_w2b = None
        for j in range(8):
            ev = sch.dma("gpsimd",
                         AP(W2S, j * 4 * D, [[FC * D, 128], [1, 4 * D]]),
                         AP(Wb_t, j * 4 * 128 * D,
                            [[D, 128], [128 * D, 4], [1, D]]),
                         (S.sw0, S.sw1)[j % 2])
            if j % 2 == 0:
                ev_w2a = ev
            else:
                ev_w2b = ev

        ev_hold = [None, None]
        ev_accfree = [None, None]
        ev_otfree = [None, None]
        ev_pfree = [None, None]
        ev_xfree = None
        ev_hfree = None
        def load_xfw(m):
            t0 = m * mg
            ev_in = None
            for dc in range(DC):
                ev_in = sch.dmat("sync", XFW[:, dc * mg:(dc + 1) * mg],
                                 AP(in_scr, t0 * D + dc * 128,
                                    [[D, mg], [1, 128]]), S.sz0)
            return ev_in
        def fold_xfw(ev_in):
            sch.wait("vector", ev_in)
            evx = None
            for dc in range(DC):
                def mkx(e, dc=dc):
                    return e.tensor_scalar_mul(
                        XFW[:, dc * mg:(dc + 1) * mg],
                        XFW[:, dc * mg:(dc + 1) * mg],
                        LW2[:, dc:dc + 1])
                evx = sch.step("vector", mkx,
                               S.sdv if dc == DC - 1 else None)
            return evx
        ev_in = load_xfw(0)
        if scale_lw2:
            ev_in = fold_xfw(ev_in)
        for m in range(g.n_mid):
            t0 = m * mg
            sch.wait("tensor", ev_in)
            if ev_hfree is not None:
                sch.wait("scalar", ev_hfree)
            ev_f1last = None
            ev_g = None
            for c in range(FC):
                bank = PH[c % 2]
                if m == 0 and c == 0:
                    sch.wait("tensor", ev_w1rdy)
                if ev_hold[c % 2] is not None:
                    sch.wait("tensor", ev_hold[c % 2])
                evm = None
                for dc in range(DC):
                    def mk(e, c=c, dc=dc, bank=bank):
                        return e.matmul(bank[:, :],
                                        W1S[:, dc * FFN + c * 128:
                                            dc * FFN + c * 128 + 128],
                                        XFW[:, dc * mg:(dc + 1) * mg],
                                        start=(dc == 0), stop=(dc == DC - 1))
                    evm = sch.step("tensor", mk,
                                   S.spe if dc == DC - 1 else None)
                ev_f1last = evm
                sch.wait("scalar", evm)
                def mkg(e, c=c, bank=bank):
                    return e.activation(HMID[:, c * mg:(c + 1) * mg],
                                        bank[:, :], AF.Gelu,
                                        bias=bias1_fm[:, c:c + 1])
                ev_g = sch.step("scalar", mkg, S.sac)
                ev_hold[c % 2] = ev_g
            ev_xfree = ev_f1last
            if m + 1 < g.n_mid:
                sch.wait("sync", ev_xfree)
                ev_in = load_xfw(m + 1)
                if scale_lw2:
                    ev_in = fold_xfw(ev_in)
            sch.wait("tensor", ev_g)
            pv = 0
            ev_p = [None, None]
            ev_last_f2 = None
            for tg in range(ntg):
                tpar = tg % 2
                if ev_pfree[tpar] is not None:
                    sch.wait("sync", ev_pfree[tpar])
                ev_p[tpar] = sch.dma("sync", PSL[tpar][:],
                                     AP(PBs, (t0 + tg * 128) * D,
                                        [[D, 128], [1, D]]),
                                     (S.sp0, S.sp1)[tpar])
                for ng in range(2):
                    opar = pv % 2
                    bank = PACC[opar]
                    if ev_accfree[opar] is not None:
                        sch.wait("tensor", ev_accfree[opar])
                    evm = None
                    if m == 0 and tg == 0 and ng == 0:
                        sch.wait("tensor", ev_w2a)
                        sch.wait("tensor", ev_w2b)
                    for c in range(FC):
                        def mk(e, c=c, tg=tg, ng=ng, bank=bank):
                            return e.matmul(
                                bank[:, :],
                                AP(HMID, c * mg + tg * 128,
                                   [[FC * mg, 128], [1, 128]]),
                                W2S[:, c * D + ng * 512:
                                    c * D + ng * 512 + 512],
                                start=(c == 0), stop=(c == FC - 1))
                        evm = sch.step("tensor", mk,
                                       S.spe if c == FC - 1 else None)
                    ev_last_f2 = evm
                    sch.wait("vector", evm)
                    sch.wait("vector", ev_p[tpar])
                    if ev_otfree[opar] is not None:
                        sch.wait("vector", ev_otfree[opar])
                    def mko(e, ng=ng, bank=bank, opar=opar, tpar=tpar):
                        return e.scalar_tensor_tensor(
                            OT[opar][:], bank[:, :], 1.0,
                            PSL[tpar][:, ng * 512:(ng + 1) * 512],
                            ALU.mult, ALU.add)
                    sch.step("vector", mko)
                    def mko2(e, ng=ng, opar=opar):
                        return e.tensor_tensor(
                            OT[opar][:], OT[opar][:],
                            bias2_bc[:, ng * 512:(ng + 1) * 512], ALU.add)
                    ev_ot = sch.step("vector", mko2, S.sdv)
                    ev_accfree[opar] = ev_ot
                    sch.wait("sync", ev_ot)
                    ev_st = sch.dma("sync",
                                    AP(dst_t, (t0 + tg * 128) * D + ng * 512,
                                       [[D, 128], [1, 512]]),
                                    OT[opar][:], dst_sem)
                    ev_otfree[opar] = ev_st
                    if ng == 1:
                        ev_pfree[tpar] = ev_st
                    pv += 1
            ev_hfree = ev_last_f2
        ev_done = [sch.val(S.sdv), sch.val(S.sac), sch.val(S.spe),
                   sch.val(dst_sem), sch.val(S.sw0), sch.val(S.sw1)]
        sch.barrier(ev_done)
        fs.close()

    ffn_phase(w1_w, w2_w, W1B, XNB, W2Bb, PBs, S.sst, False, "w")
    if stop_after == "pw":
        return _early_out()
    ffn_phase(f1_w, f2_w, CF1, Z2B, F2Bb, out_t, S.sout, True, "f")

    sch.wait("sync", sch.val(S.sout))

    sch.run_into(nc)
    return nc



def _prep_common(inputs):
    f = np.ascontiguousarray
    w = {
        "q_w": f(inputs["L0_attn_q_w"], dtype=np.float32),
        "q_b": f(inputs["L0_attn_q_b"], dtype=np.float32).reshape(1, D),
        "k_w": f(inputs["L0_attn_k_w"], dtype=np.float32),
        "k_b": f(inputs["L0_attn_k_b"], dtype=np.float32).reshape(1, D),
        "v_w": f(inputs["L0_attn_v_w"], dtype=np.float32),
        "v_b": f(inputs["L0_attn_v_b"], dtype=np.float32).reshape(1, D),
        "o_w": f(inputs["L0_attn_o_w"], dtype=np.float32),
        "o_b": f(inputs["L0_attn_o_b"], dtype=np.float32).reshape(1, D),
        "dec_w": f(inputs["L0_attn_dec_w"], dtype=np.float32),
        "dec_b": f(inputs["L0_attn_dec_b"], dtype=np.float32).reshape(1, H),
        "ln1_w": f(inputs["L0_attn_ln1_w"], dtype=np.float32).reshape(1, D),
        "ln1_b": f(inputs["L0_attn_ln1_b"], dtype=np.float32).reshape(1, D),
        "ln2_w": f(inputs["L0_attn_ln2_w"], dtype=np.float32).reshape(1, D),
        "ln2_b": f(inputs["L0_attn_ln2_b"], dtype=np.float32).reshape(1, D),
        "f1_w": f(inputs["L0_attn_f1_w"], dtype=np.float32),
        "f1_b": f(inputs["L0_attn_f1_b"], dtype=np.float32).reshape(1, FFN),
        "f2_w": f(inputs["L0_attn_f2_w"], dtype=np.float32),
        "f2_b": f(inputs["L0_attn_f2_b"], dtype=np.float32).reshape(1, D),
        "w1_w": f(inputs["L0_ffn_w1"], dtype=np.float32),
        "w1_b": f(inputs["L0_ffn_b1"], dtype=np.float32).reshape(1, FFN),
        "w2_w": f(inputs["L0_ffn_w2"], dtype=np.float32),
        "w2_b": f(inputs["L0_ffn_b2"], dtype=np.float32).reshape(1, D),
        "ident": np.eye(128, dtype=np.float32).astype(bf16),
        "identf": np.eye(128, dtype=np.float32),
        "onesrow": np.ones((1, 128), dtype=np.float32).astype(bf16),
        "onesrowf": np.ones((1, 128), dtype=np.float32),
    }
    return w


def kernel(**inputs):
    x = np.ascontiguousarray(np.asarray(inputs["L0_attn_x"], dtype=np.float32))
    B, T, _ = x.shape
    g = Geo(t_seq=T)
    n_cores = 2 * B
    w = _prep_common(inputs)
    in_maps = []
    for core in range(n_cores):
        b, half = core // 2, core % 2
        own = x[b, half * g.t_own:(half + 1) * g.t_own]
        rem = x[b, (1 - half) * g.t_own:(2 - half) * g.t_own]
        m = dict(w)
        m["x_own"] = np.ascontiguousarray(own)
        m["x_rem"] = np.ascontiguousarray(rem)
        m["hmask"] = np.array([[float(half), float(1 - half)]],
                              dtype=np.float32)
        in_maps.append(m)
    nc = build_program(g, n_cores=n_cores)
    res = run_bass_kernel_spmd(nc, in_maps, core_ids=list(range(n_cores)))
    outs = [res.results[i]["out"] for i in range(n_cores)]
    full = np.empty((B, T, D), dtype=np.float32)
    for core in range(n_cores):
        b, half = core // 2, core % 2
        full[b, half * g.t_own:(half + 1) * g.t_own] = outs[core]
    return full


# version 27 (final)
# speedup vs baseline: 1.0123x; 1.0112x over previous
from collections import defaultdict
from contextlib import ExitStack

import numpy as np
import ml_dtypes
import concourse.bass as bass
import concourse.mybir as mybir
from concourse.bass_utils import run_bass_kernel_spmd

F32 = mybir.dt.float32
BF16 = mybir.dt.bfloat16
AF = mybir.ActivationFunctionType
ALU = mybir.AluOpType
bf16 = ml_dtypes.bfloat16

D, H, KD, KV, FFN, C = 1024, 16, 64, 65, 4096, 128
DC, FC = D // 128, FFN // 128
HKV = H * KV
LN_EPS, LAM_MIN, DEN_EPS = 1e-5, 1e-8, 1e-6


class Geo:
    def __init__(self, t_seq=8192, group=8, midgroup=512):
        self.t_seq = t_seq
        self.t_own = t_seq // 2
        self.nc_own = self.t_own // C
        self.group = min(group, self.nc_own)
        assert self.nc_own % self.group == 0
        self.n_groups = self.nc_own // self.group
        self.midgroup = min(midgroup, self.t_own)
        assert self.t_own % self.midgroup == 0
        self.n_mid = self.t_own // self.midgroup


def AP(t, off, pat):
    return bass.AP(t, off, pat)


class Sched:
    ENGS = ("sync", "gpsimd", "tensor", "vector", "scalar")
    PARANOID = False

    def __init__(self):
        self.ops = {e: [] for e in self.ENGS}
        self.ctr = defaultdict(int)
        self.dma_issuer = {}
        self.progress = {}
        self.serial_sems = set()

    def set_progress_sems(self, mapping):
        self.progress = mapping

    def _emit(self, eng, fn):
        self.ops[eng].append(fn)

    DRAIN_ENGS = ("vector", "scalar")

    def step(self, eng, make, sem=None, n=1):
        drain = eng in self.DRAIN_ENGS or self.PARANOID
        if sem is None:
            if drain:
                self._emit(eng, lambda e, mk=make: (mk(e), e.drain()))
            else:
                self._emit(eng, lambda e, mk=make: mk(e))
            return None
        self.ctr[sem.name] += n
        if drain and eng in self.DRAIN_ENGS:
            self._emit(eng, lambda e, mk=make, s=sem, k=n:
                       (mk(e), e.drain().then_inc(s, k)))
        elif drain:
            self._emit(eng, lambda e, mk=make, s=sem, k=n:
                       (mk(e).then_inc(s, k), e.drain()))
        else:
            self._emit(eng, lambda e, mk=make, s=sem, k=n:
                       mk(e).then_inc(s, k))
        return (sem, self.ctr[sem.name])

    def dma(self, eng, dst, src, sem, slow=False):
        self.dma_issuer[sem.name] = (eng, sem)
        if slow:
            return self.step(
                eng, lambda e, d=dst, s=src: e.dma_start(
                    d, s, allow_slow_non_contiguous=True), sem, 16)
        return self.step(eng, lambda e, d=dst, s=src: e.dma_start(d, s),
                         sem, 16)

    def dmat(self, eng, dst, src, sem):
        self.dma_issuer[sem.name] = (eng, sem)
        return self.step(
            eng, lambda e, d=dst, s=src: e.dma_start(d, s, transpose=True),
            sem, 16)

    def val(self, sem):
        return (sem, self.ctr[sem.name])

    def wait(self, eng, ev):
        if ev is None:
            return
        sem, v = ev
        if v <= 0:
            return
        iss = self.dma_issuer.get(sem.name)
        if iss is not None and sem.name in self.serial_sems:
            self._emit(eng, lambda e, s=sem, vv=v: e.wait_ge(s, vv))
            return
        if iss is not None and iss[0] != eng:
            ieng = iss[0]
            full = self.ctr[sem.name]
            self._emit(ieng, lambda e, s=sem, vv=full: e.wait_ge(s, vv))
            pev = self.step(ieng, lambda e: e.nop(), self.progress[ieng])
            self._emit(eng, lambda e, s=pev[0], vv=pev[1]: e.wait_ge(s, vv))
        else:
            self._emit(eng, lambda e, s=sem, vv=v: e.wait_ge(s, vv))

    def barrier(self, evs):
        for eng in self.ENGS:
            for ev in evs:
                self.wait(eng, ev)

    def run_into(self, nc):
        with nc.Block() as block:
            for name in self.ENGS:
                lst = self.ops[name]

                def make_body(lst_):
                    def body(e):
                        for fn in lst_:
                            fn(e)
                    return body

                getattr(block, name)(make_body(lst))


def build_program(g: Geo, n_cores=8, stop_after=None, paranoid=False):
    nc = bass.Bass("TRN2", target_bir_lowering=False,
                   detect_race_conditions=paranoid)
    es = ExitStack()
    sch = Sched()
    sch.PARANOID = paranoid
    inv_d = 1.0 / D

    def din(name, shape, dt=F32):
        return nc.dram_tensor(name, list(shape), dt, kind="ExternalInput")

    x_own = din("x_own", (g.t_own, D))
    x_rem = din("x_rem", (g.t_own, D))
    q_w = din("q_w", (D, D)); q_b = din("q_b", (1, D))
    k_w = din("k_w", (D, D)); k_b = din("k_b", (1, D))
    v_w = din("v_w", (D, D)); v_b = din("v_b", (1, D))
    o_w = din("o_w", (D, D)); o_b = din("o_b", (1, D))
    dec_w = din("dec_w", (D, H)); dec_b = din("dec_b", (1, H))
    ln1_w = din("ln1_w", (1, D)); ln1_b = din("ln1_b", (1, D))
    ln2_w = din("ln2_w", (1, D)); ln2_b = din("ln2_b", (1, D))
    f1_w = din("f1_w", (D, FFN)); f1_b = din("f1_b", (1, FFN))
    f2_w = din("f2_w", (FFN, D)); f2_b = din("f2_b", (1, D))
    w1_w = din("w1_w", (D, FFN)); w1_b = din("w1_b", (1, FFN))
    w2_w = din("w2_w", (FFN, D)); w2_b = din("w2_b", (1, D))
    hmask_in = din("hmask", (1, 2))
    ident_in = din("ident", (128, 128), BF16)
    identf_in = din("identf", (128, 128), F32)
    ones_in = din("onesrow", (1, 128), BF16)
    onesf_in = din("onesrowf", (1, 128), F32)

    out_t = nc.dram_tensor("out", [g.t_own, D], F32, kind="ExternalOutput")

    def dscr(name, shape, dt):
        return nc.dram_tensor(name, list(shape), dt, kind="Internal")

    XN32 = dscr("XN32", (g.t_own, D), F32)
    XNB = dscr("XNB", (g.t_own, D), BF16)
    ZFMs = dscr("ZFMs", (DC, 128, g.t_own), BF16)
    Z2B = dscr("Z2B", (g.t_own, D), BF16)
    PBs = dscr("PBs", (g.t_own, D), F32)
    LROW = dscr("LROW", (g.nc_own, H * C), F32)
    E12ROW = dscr("E12ROW", (g.nc_own, 2 * H * C), BF16)
    PHIKs = dscr("PHIKs", (g.nc_own, 128, D), BF16)
    VREs = dscr("VREs", (g.nc_own, 128, HKV), BF16)
    CQB = dscr("CQB", (1, D), F32)
    CKB = dscr("CKB", (1, D), F32)
    CVB = dscr("CVB", (1, D), F32)
    CDB = dscr("CDB", (1, H), F32)
    CF1B = dscr("CF1B", (1, FFN), F32)
    EXCI = dscr("EXCI", (128, HKV), F32)
    EXCO = dscr("EXCO", (2 * 128, HKV), F32)

    SEM = {}
    for s in ("sx0", "sx1", "sy0", "sy1", "sz0", "sz1", "sw0", "sw1",
              "sp0", "sp1", "sst", "sld", "sout",
              "spe", "sdv", "sac", "sgp", "sqp", "gqp", "scc", "sca"):
        SEM[s] = es.enter_context(nc.semaphore(s))
    S = type("S", (), SEM)()
    sch.set_progress_sems({"sync": S.sqp, "gpsimd": S.gqp})
    sch.serial_sems = {"sx0", "sx1", "sy0", "sy1", "sz0", "sz1",
                       "sp0", "sp1"}

    def sb(name, shape, dt, stack=None):
        return (stack or es).enter_context(nc.sbuf_tensor(name, list(shape), dt))

    def ps(name, shape, dt=F32, stack=None):
        return (stack or es).enter_context(nc.psum_tensor(name, list(shape), dt))

    IDENT = sb("IDENT", (128, 128), BF16)
    IDF = sb("IDF", (128, 128), F32)
    ONES1 = sb("ONES1", (1, 128), BF16)
    ONES1F = sb("ONES1F", (1, 128), F32)
    HM = sb("HM", (128, 2), F32)
    OBb = sb("OBb", (128, D), F32)
    F2Bb = sb("F2Bb", (128, D), F32)
    W2Bb = sb("W2Bb", (128, D), F32)
    W1B = sb("W1B", (128, FC), F32)
    CF1 = sb("CF1", (128, FC), F32)
    LW2 = sb("LW2", (128, DC), F32)

    att = ExitStack()
    WD = sb("WD", (128, DC * H), BF16, att)
    NCD = sb("NCD", (H, 1), F32, att)
    LW1 = sb("LW1", (128, DC), F32, att)
    LB1 = sb("LB1", (128, DC), BF16, att)
    E1T = sb("E1T", (128, g.nc_own * H), F32, att)
    E2T = sb("E2T", (128, g.nc_own * H), F32, att)
    LTM = sb("LTM", (128, g.nc_own * H), F32, att)
    GROW = sb("GROW", (1, g.nc_own * H), F32, att)
    RANCH = sb("RANCH", (128, g.n_groups * HKV), BF16, att)
    STATE = sb("STATE", (128, HKV), F32, att)
    GAMR = sb("GAMR", (128, H), F32, att)
    GB = sb("GB", (128, H), F32, att)
    FA = sb("FA", (128, H), F32, att)
    FB = sb("FB", (128, H), F32, att)

    EROW = g.nc_own * H

    def _early_out():
        evd = sch.dma("sync", AP(out_t, 0, [[D, 128], [1, D]]), OBb[:], S.sout)
        sch.wait("sync", evd)
        sch.run_into(nc)
        return nc

    p0s = ExitStack()
    sch.dma("sync", IDENT[:], ident_in.ap(), S.sld)
    sch.dma("sync", IDF[:], identf_in.ap(), S.sld)
    sch.dma("sync", ONES1[:], ones_in.ap(), S.sld)
    sch.dma("sync", ONES1F[:], onesf_in.ap(), S.sld)
    sch.dma("sync", HM[:], AP(hmask_in, 0, [[0, 128], [1, 2]]), S.sld)
    for dst, srct, nn_ in ((OBb, o_b, D), (F2Bb, f2_b, D), (W2Bb, w2_b, D)):
        sch.dma("sync", dst[:], AP(srct, 0, [[0, 128], [1, nn_]]), S.sld)
    sch.dma("sync", W1B[:], AP(w1_b, 0, [[1, 128], [128, FC]]), S.sld, slow=True)
    sch.dma("sync", LW1[:], AP(ln1_w, 0, [[1, 128], [128, DC]]), S.sld, slow=True)
    sch.dma("gpsimd", LB1[:], AP(ln1_b, 0, [[1, 128], [128, DC]]), S.sw0, slow=True)
    sch.dma("sync", LW2[:], AP(ln2_w, 0, [[1, 128], [128, DC]]), S.sld, slow=True)
    LB2 = sb("LB2", (128, DC), BF16, p0s)
    sch.dma("gpsimd", LB2[:], AP(ln2_b, 0, [[1, 128], [128, DC]]), S.sw0, slow=True)
    ev_sld0 = sch.val(S.sld)

    def wload(dst, srct, douts):
        pat = [[douts, 128], [128 * douts, DC], [1, douts]]
        return sch.dma("gpsimd", dst[:], AP(srct, 0, pat), S.sw0)

    wload(WD, dec_w, H)
    ev_w_all = sch.val(S.sw0)

    PF = [ps(f"PF{i}", (1, 512), F32, stack=p0s) for i in range(FFN // 512)]
    TB0 = sb("TB0", (1, FFN), F32, p0s)
    BR0 = sb("BR0", (1, FFN), F32, p0s)
    sch.wait("tensor", ev_w_all)
    sch.wait("tensor", ev_sld0)

    def bias_fold(W, b_t, cdram, douts):
        evv = None
        ev_br = sch.dma("sync", BR0[0:1, 0:douts],
                        AP(b_t, 0, [[0, 1], [1, douts]]), S.sld)
        sch.wait("vector", ev_br)
        sch.wait("vector", sch.val(S.sst))
        for gi in range((douts + 511) // 512):
            nn_ = min(512, douts - gi * 512)
            evm = None
            for dc in range(DC):
                def mk(e, dc=dc, gi=gi, nn_=nn_, W=W, douts=douts):
                    return e.matmul(PF[0][0:1, 0:nn_], LB1[:, dc:dc + 1],
                                    W[:, dc * douts + gi * 512:
                                      dc * douts + gi * 512 + nn_],
                                    start=(dc == 0), stop=(dc == DC - 1))
                evm = sch.step("tensor", mk, S.spe if dc == DC - 1 else None)
            sch.wait("vector", evm)
            def mkv(e, gi=gi, nn_=nn_):
                return e.tensor_tensor(TB0[0:1, gi * 512: gi * 512 + nn_],
                                       PF[0][0:1, 0:nn_],
                                       BR0[0:1, gi * 512: gi * 512 + nn_],
                                       ALU.add)
            evv = sch.step("vector", mkv, S.sdv)
            sch.wait("tensor", evv)
        sch.wait("sync", evv)
        return sch.dma("sync", AP(cdram, 0, [[0, 1], [1, douts]]),
                       TB0[0:1, 0:douts], S.sst)

    evfd = bias_fold(WD, dec_b, CDB, H)

    sch.wait("sync", evfd)
    ev_cd = sch.dma("sync", NCD[:], AP(CDB, 0, [[1, H], [1, 1]]), S.sld)
    sch.wait("vector", ev_cd)
    sch.step("vector", lambda e: e.tensor_scalar_mul(NCD[:], NCD[:], -1.0), S.sdv)

    for W, douts in ((WD, H),):
        for dc in range(DC):
            def mks(e, W=W, dc=dc, douts=douts):
                return e.tensor_scalar_mul(W[:, dc * douts:(dc + 1) * douts],
                                           W[:, dc * douts:(dc + 1) * douts],
                                           LW1[:, dc:dc + 1])
            sch.step("vector", mks, S.sdv)

    sch.step("vector", lambda e: e.memset(FA[64:128, :], 1.0))
    sch.step("vector", lambda e: e.memset(FB[0:64, :], 1.0), S.sdv)

    FST = [sb(f"FST{i}", (128, FFN), BF16, p0s) for i in range(2)]
    evm = None
    for dc in range(DC):
        par = dc % 2
        if dc >= 2 and evm is not None:
            sch.wait("gpsimd", evm)
        evl = sch.dma("gpsimd", FST[par][:],
                      AP(f1_w, dc * 128 * FFN, [[FFN, 128], [1, FFN]]),
                      (S.sw0, S.sw1)[par])
        sch.wait("tensor", evl)
        for gi in range(FFN // 512):
            def mk(e, dc=dc, gi=gi, par=par):
                return e.matmul(PF[gi][0:1, :], LB2[:, dc:dc + 1],
                                FST[par][:, gi * 512:(gi + 1) * 512],
                                start=(dc == 0), stop=(dc == DC - 1))
            evm = sch.step("tensor", mk,
                           S.spe if gi == FFN // 512 - 1 else None)
    ev_br2 = sch.dma("sync", BR0[0:1, :], AP(f1_b, 0, [[0, 1], [1, FFN]]),
                     S.sld)
    sch.wait("vector", evm)
    sch.wait("vector", ev_br2)
    sch.wait("vector", sch.val(S.sst))
    evv2 = None
    for gi in range(FFN // 512):
        def mkv(e, gi=gi):
            return e.tensor_tensor(TB0[0:1, gi * 512:(gi + 1) * 512],
                                   PF[gi][0:1, :],
                                   BR0[0:1, gi * 512:(gi + 1) * 512],
                                   ALU.add)
        evv2 = sch.step("vector", mkv, S.sdv)
    sch.wait("sync", evv2)
    ev_cf = sch.dma("sync", AP(CF1B, 0, [[0, 1], [1, FFN]]), TB0[0:1, :], S.sst)
    sch.wait("sync", ev_cf)
    sch.dma("sync", CF1[:], AP(CF1B, 0, [[1, 128], [128, FC]]), S.sld, slow=True)

    sch.barrier([sch.val(S.sdv), sch.val(S.sld), sch.val(S.spe)])
    p0s.close()
    if stop_after == "p0":
        return _early_out()

    p1s = ExitStack()
    WK = sb("WK", (128, DC * D), BF16, p1s)
    WV = sb("WV", (128, DC * D), BF16, p1s)
    CKb = sb("CKb", (128, D), F32, p1s)
    CVb = sb("CVb", (128, D), F32, p1s)
    TB1 = sb("TB1", (1, D), F32, p1s)
    BR1 = sb("BR1", (1, D), F32, p1s)
    XT = [sb(f"XT{i}", (128, D), F32, p1s) for i in range(2)]
    XNT = [sb(f"XNT{i}", (128, D), F32, p1s) for i in range(2)]
    ZT = [sb(f"ZT{i}", (128, D), BF16, p1s) for i in range(2)]
    ZF = [sb(f"ZF{i}", (128, D), BF16, p1s) for i in range(2)]
    XNBT_ = sb("XNBT", (128, D), BF16, p1s)
    XNBT = [XNBT_, XNBT_]
    SQ = [sb(f"SQ{i}", (128, D), F32, p1s) for i in range(2)]
    ST1 = [sb(f"ST1{i}", (128, 8), F32, p1s) for i in range(2)]
    PHT = [sb(f"PHT{i}", (128, D), F32, p1s) for i in range(2)]
    PHE = [sb(f"PHE{i}", (128, D), F32, p1s) for i in range(2)]
    PHB = [sb(f"PHB{i}", (128, D), BF16, p1s) for i in range(2)]
    VT = [sb(f"VT{i}", (128, HKV), BF16, p1s) for i in range(2)]
    GSTORE = sb("GSTORE", (128, 2 * g.n_groups * H * (KV + 1)), BF16, p1s)
    GSW = 2 * g.n_groups * H * (KV + 1)
    LAM = [sb(f"LAM{i}", (H, C), F32, p1s) for i in range(2)]
    LLF = [sb(f"LLF{i}", (H, C), F32, p1s) for i in range(2)]
    LF = [sb(f"LF{i}", (H, C), F32, p1s) for i in range(2)]
    ZROF = sb("ZROF", (H, C), F32, p1s)
    E1F = [sb(f"E1F{i}", (H, C), F32, p1s) for i in range(2)]
    E2F = [sb(f"E2F{i}", (H, C), F32, p1s) for i in range(2)]
    ETT = [sb(f"ETT{i}", (128, 48), F32, p1s) for i in range(2)]
    GR1 = [sb(f"GR1{i}", (1, H), F32, p1s) for i in range(2)]
    GB = [sb(f"GBp{i}", (128, H), F32, p1s) for i in range(2)]
    KB = [sb(f"KBp{i}", (128, H * 128), BF16, p1s) for i in range(2)]
    UW = [sb(f"UWp{i}", (128, HKV), F32, p1s) for i in range(2)]
    TTMP = sb("TTMP", (128, HKV), F32, p1s)

    PTZ = ps("PTZ", (128, 1024), BF16, stack=p1s)
    PKV = ps("PKV", (128, 512), F32, stack=p1s)
    PKV2 = ps("PKV2", (128, 512), F32, stack=p1s)
    PDEC = ps("PDEC", (H, C), F32, stack=p1s)
    PUW1 = ps("PUW1", (128, 7 * KV), F32, stack=p1s)
    PUW2 = ps("PUW2", (128, 7 * KV), F32, stack=p1s)
    PUW3 = ps("PUW3", (128, 2 * KV), F32, stack=p1s)
    PMIS = ps("PMIS", (128, 64), F32, stack=p1s)

    def ln_affine(x_ap, out_ap, SQ_, ST_, tmp_ap=None):
        sch.step("vector",
                 lambda e: e.tensor_scalar(SQ_[:], x_ap, 1.0, None, ALU.mult,
                                           ALU.add, accum_out=ST_[:, 0:1]),
                 S.sdv)
        sch.step("vector",
                 lambda e: e.scalar_tensor_tensor(SQ_[:], x_ap, 1.0, x_ap,
                                                  ALU.mult, ALU.mult,
                                                  accum_out=ST_[:, 1:2]), S.sdv)
        sch.step("vector", lambda e: e.tensor_scalar_mul(ST_[:, 2:3], ST_[:, 0:1],
                                                         inv_d))
        sch.step("vector", lambda e: e.tensor_tensor(ST_[:, 3:4], ST_[:, 2:3],
                                                     ST_[:, 2:3], ALU.mult))
        sch.step("vector",
                 lambda e: e.scalar_tensor_tensor(ST_[:, 3:4], ST_[:, 1:2], inv_d,
                                                  ST_[:, 3:4], ALU.mult,
                                                  ALU.subtract))
        sch.step("vector", lambda e: e.tensor_scalar_add(ST_[:, 3:4],
                                                         ST_[:, 3:4], LN_EPS))
        ev3 = sch.step("vector",
                       lambda e: e.tensor_scalar_mul(ST_[:, 5:6], ST_[:, 2:3],
                                                     -1.0), S.sdv)
        sch.wait("scalar", ev3)
        sch.step("scalar", lambda e: e.activation(ST_[:, 4:5], ST_[:, 3:4],
                                                  AF.Ln))
        sch.step("scalar", lambda e: e.activation(ST_[:, 4:5], ST_[:, 4:5],
                                                  AF.Exp, scale=-0.5))
        t_ap = tmp_ap if tmp_ap is not None else SQ_[:]
        sch.step("scalar", lambda e: e.activation(t_ap, x_ap, AF.Identity,
                                                  bias=ST_[:, 5:6]))
        return sch.step("scalar",
                        lambda e: e.activation(out_ap, t_ap, AF.Identity,
                                               scale=ST_[:, 4:5]), S.sac)

    EV = {}

    def p1_stageA(ci, par, x_t, store):
        prev = EV.get(("Aend", ci - 2))
        if prev is not None:
            sch.wait("sync", prev)
        ev_x = sch.dma("sync", XT[par][:],
                       AP(x_t, ci * C * D, [[D, 128], [1, D]]),
                       (S.sx0, S.sx1)[par])
        sch.wait("vector", ev_x)
        sch.wait("scalar", ev_x)
        pz = EV.get(("zf", ci - 2))
        if pz is not None:
            sch.wait("scalar", pz)
        if store:
            sch.wait("scalar", sch.val(S.sst))
            sch.wait("vector", sch.val(S.sst))
            sch.wait("scalar", sch.val(S.sgp))
        ev_xn = ln_affine(x_tile_ap(par), XNT[par][:], SQ[par], ST1[par])
        sch.wait("vector", ev_xn)
        ev_z = ln_affine(XNT[par][:], ZT[par][:], SQ[par], ST1[par])
        if store:
            sch.wait("sync", ev_xn)
            sch.dma("sync", AP(XN32, ci * C * D, [[D, 128], [1, D]]),
                    XNT[par][:], S.sst)
            ev_xb = sch.step("scalar",
                             lambda e, par=par: e.activation(
                                 XNBT[par][:], XNT[par][:], AF.Identity),
                             S.sac)
            sch.wait("gpsimd", ev_xb)
            sch.dma("gpsimd", AP(XNB, ci * C * D, [[D, 128], [1, D]]),
                    XNBT[par][:], S.sgp)
        EV[("x", ci)] = ev_x
        EV[("z", ci)] = ev_z
        EV[("Aend", ci)] = ev_z

    def x_tile_ap(par):
        return XT[par][:]

    def p1_stageB(ci, par, store):
        ev_z = EV[("z", ci)]
        sch.wait("tensor", ev_z)
        pz = EV.get(("zf", ci - 1))
        if pz is not None:
            sch.wait("tensor", pz)
        evt = None
        for dc in range(DC):
            def mkt(e, dc=dc, par=par):
                return e.transpose(PTZ[:, dc * 128:(dc + 1) * 128],
                                   ZT[par][:, dc * 128:(dc + 1) * 128],
                                   IDENT[:])
            evt = sch.step("tensor", mkt, S.spe if dc == DC - 1 else None)
        pm = EV.get(("mmlast", ci - 2))
        if pm is not None:
            sch.wait("vector", pm)
        sch.wait("vector", evt)
        ev_zf = sch.step("vector",
                         lambda e, par=par: e.tensor_copy(ZF[par][:], PTZ[:]),
                         S.sdv)
        EV[("zf", ci)] = ev_zf
        if store:
            sch.wait("sync", ev_zf)
            sch.dma("sync",
                    AP(ZFMs, ci * C, [[g.t_own, 128], [128 * g.t_own, DC],
                                      [1, C]]),
                    ZF[par][:], S.sst)

        pc = EV.get(("kb", ci - 2))
        if pc is not None:
            sch.wait("vector", pc)
        pu = EV.get(("uwmm", ci - 2))
        if pu is not None:
            sch.wait("vector", pu)

        sch.wait("tensor", ev_zf)
        ev_phik = None
        for gi in range(2):
            bank = (PKV, PKV2)[gi]
            evm = None
            for dc in range(DC):
                def mk(e, dc=dc, gi=gi, bank=bank, par=par):
                    return e.matmul(bank[:, :],
                                    ZF[par][:, dc * 128:(dc + 1) * 128],
                                    WK[:, dc * D + gi * 512:
                                       dc * D + gi * 512 + 512],
                                    start=(dc == 0), stop=(dc == DC - 1))
                evm = sch.step("tensor", mk, S.spe if dc == DC - 1 else None)
            sch.wait("vector", evm)
            def mk1(e, gi=gi, bank=bank, par=par):
                return e.tensor_tensor(PHT[par][:, gi * 512:(gi + 1) * 512],
                                       bank[:, :],
                                       CKb[:, gi * 512:(gi + 1) * 512], ALU.add)
            ev1 = sch.step("vector", mk1, S.sdv)
            sch.wait("tensor", ev1)
            sch.wait("scalar", ev1)
            def mk2(e, gi=gi, par=par):
                return e.activation(PHE[par][:, gi * 512:(gi + 1) * 512],
                                    PHT[par][:, gi * 512:(gi + 1) * 512],
                                    AF.Exp)
            ev2 = sch.step("scalar", mk2, S.sac)
            def mk4(e, gi=gi, par=par):
                return e.activation(PHT[par][:, gi * 512:(gi + 1) * 512],
                                    PHT[par][:, gi * 512:(gi + 1) * 512],
                                    AF.Relu)
            ev2b = sch.step("scalar", mk4, S.sac)
            sch.wait("vector", ev2b)
            def mk3(e, gi=gi, par=par):
                return e.tensor_scalar_min(
                    PHE[par][:, gi * 512:(gi + 1) * 512],
                    PHE[par][:, gi * 512:(gi + 1) * 512], 1.0)
            sch.step("vector", mk3)
            def mk5(e, gi=gi, par=par):
                return e.tensor_tensor(
                    AP(PHB[par], gi * 512, [[D, 128], [1, 512]]),
                    PHE[par][:, gi * 512:(gi + 1) * 512],
                    PHT[par][:, gi * 512:(gi + 1) * 512], ALU.add)
            ev_phik = sch.step("vector", mk5, S.sdv)

        for gi in range(2):
            bank = (PKV, PKV2)[gi]
            evm = None
            for dc in range(DC):
                def mk(e, dc=dc, gi=gi, bank=bank, par=par):
                    return e.matmul(bank[:, :],
                                    ZF[par][:, dc * 128:(dc + 1) * 128],
                                    WV[:, dc * D + gi * 512:
                                       dc * D + gi * 512 + 512],
                                    start=(dc == 0), stop=(dc == DC - 1))
                evm = sch.step("tensor", mk, S.spe if dc == DC - 1 else None)
            sch.wait("vector", evm)
            def mkv(e, gi=gi, bank=bank, par=par):
                dst = AP(VT[par], gi * 8 * KV, [[HKV, 128], [KV, 8], [1, KD]])
                srcp = AP(bank, 0, [[512, 128], [KD, 8], [1, KD]])
                bias = AP(CVb, gi * 512, [[D, 128], [KD, 8], [1, KD]])
                return e.tensor_tensor(dst, srcp, bias, ALU.add)
            ev_v = sch.step("vector", mkv, S.sdv)
            sch.wait("tensor", ev_v)

        if store:
            sch.wait("sync", ev_phik)
            sch.wait("sync", sch.val(S.sdv))
            sch.dma("sync", AP(PHIKs, ci * 128 * D, [[D, 128], [1, D]]),
                    PHB[par][:], S.sst)
            sch.dma("sync", AP(VREs, ci * 128 * HKV, [[HKV, 128], [1, HKV]]),
                    VT[par][:], S.sst)

        evm = None
        for dc in range(DC):
            def mk(e, dc=dc, par=par):
                return e.matmul(PDEC[:, :], WD[:, dc * H:(dc + 1) * H],
                                ZF[par][:, dc * 128:(dc + 1) * 128],
                                start=(dc == 0), stop=(dc == DC - 1))
            evm = sch.step("tensor", mk, S.spe if dc == DC - 1 else None)
        EV[("mmlast", ci)] = evm
        sch.wait("scalar", evm)
        ev1 = sch.step("scalar",
                       lambda e, par=par: e.activation(LAM[par][:], PDEC[:, :],
                                                       AF.Exp, scale=-1.0,
                                                       bias=NCD[:]), S.sac)
        sch.wait("vector", ev1)
        sch.wait("tensor", ev1)
        sch.step("vector", lambda e, par=par: e.tensor_scalar_add(
            LAM[par][:], LAM[par][:], 1.0))
        sch.step("vector", lambda e, par=par: e.reciprocal(LAM[par][:],
                                                           LAM[par][:]))
        ev2 = sch.step("vector",
                       lambda e, par=par: e.tensor_scalar_max(
                           LAM[par][:], LAM[par][:], LAM_MIN), S.sdv)
        sch.wait("scalar", ev2)
        ev3 = sch.step("scalar",
                       lambda e, par=par: e.activation(LLF[par][:], LAM[par][:],
                                                       AF.Ln), S.sac)
        sch.wait("vector", ev3)
        ev4 = sch.step("vector",
                       lambda e, par=par: e.tensor_tensor_scan(
                           LF[par][:], LLF[par][:], ZROF[:], 0.0,
                           ALU.add, ALU.add), S.sdv)
        sch.wait("scalar", ev4)
        if store:
            pe12 = EV.get(("e12row", ci - 2))
            if pe12 is not None:
                sch.wait("scalar", pe12)
        ev5 = sch.step("scalar",
                       lambda e, par=par: e.activation(
                           E1F[par][:], LF[par][:], AF.Exp, scale=-1.0,
                           bias=LF[par][:, C - 1:C]), S.sac)
        ev6 = sch.step("scalar",
                       lambda e, par=par: e.activation(E2F[par][:], LF[par][:],
                                                       AF.Exp), S.sac)
        if store:
            sch.wait("gpsimd", ev6)
            sch.dma("gpsimd", AP(E12ROW, ci * 2 * H * C, [[0, 1], [1, H * C]]),
                    E1F[par][:, :], S.sgp)
            ev_e12 = sch.dma("gpsimd",
                             AP(E12ROW, ci * 2 * H * C + H * C,
                                [[0, 1], [1, H * C]]),
                             E2F[par][:, :], S.sgp)
            EV[("e12row", ci)] = ev_e12
        sch.wait("tensor", ev6)
        pg = EV.get(("gb", ci - 1))
        if pg is not None:
            sch.wait("tensor", pg)
        ev7a = sch.step("tensor", lambda e, par=par: e.transpose(
            PMIS[:, 0:H], E1F[par][:], IDF[0:H, 0:H]), S.spe)
        ev7b = sch.step("tensor", lambda e, par=par: e.transpose(
            PMIS[:, H:2 * H], E2F[par][:], IDF[0:H, 0:H]), S.spe)
        ev7c = sch.step("tensor", lambda e, par=par: e.transpose(
            PMIS[:, 2 * H:3 * H], LF[par][:], IDF[0:H, 0:H]), S.spe)
        sch.wait("vector", ev7c)
        if store:
            def mke(e, ci=ci):
                return e.tensor_copy(
                    AP(E1T, ci * H, [[EROW, 128], [1, H]]), PMIS[:, 0:H])
            sch.step("vector", mke)
            def mke2(e, ci=ci):
                return e.tensor_copy(
                    AP(E2T, ci * H, [[EROW, 128], [1, H]]), PMIS[:, H:2 * H])
            sch.step("vector", mke2)
            def mke3(e, ci=ci):
                return e.tensor_copy(
                    AP(LTM, ci * H, [[EROW, 128], [1, H]]), PMIS[:, 2 * H:3 * H])
            ev9 = sch.step("vector", mke3, S.sdv)
            sch.wait("sync", ev4)
            sch.dma("sync", AP(LROW, ci * H * C, [[0, 1], [1, H * C]]),
                    LF[par][:, :], S.sst)
        else:
            ev9 = sch.step("vector",
                           lambda e, par=par: e.tensor_copy(
                               ETT[par][:, 0:3 * H], PMIS[:, 0:3 * H]), S.sdv)
        EV[("e9", ci)] = ev9

        ev10 = sch.step("tensor",
                        lambda e, par=par: e.transpose(
                            PMIS[0:1, 3 * H:3 * H + H],
                            E2F[par][:, C - 1:C], IDF[0:H, 0:H]), S.spe)
        sch.wait("vector", ev10)
        ev11 = sch.step("vector",
                        lambda e, par=par: e.tensor_copy(
                            GR1[par][:], PMIS[0:1, 3 * H:3 * H + H]), S.sdv)
        if store:
            def mkg(e, ci=ci, par=par):
                return e.tensor_copy(GROW[0:1, ci * H:(ci + 1) * H], GR1[par][:])
            sch.step("vector", mkg)
        sch.wait("tensor", ev11)
        ev12 = sch.step("tensor",
                        lambda e, par=par: e.matmul(
                            PMIS[:, 3 * H:4 * H], ONES1F[0:1, :],
                            GR1[par][:], start=True, stop=True), S.spe)
        sch.wait("vector", ev12)
        ev_gb = sch.step("vector",
                         lambda e, par=par: e.tensor_copy(
                             GB[par][:], PMIS[:, 3 * H:4 * H]), S.sdv)
        sch.wait("tensor", ev_gb)
        EV[("gb", ci)] = ev_gb
        EV[("phik", ci)] = ev_phik

    def p1_stageC(ci, par, store, part):
        if store:
            e1_t, e1_off, e1_row = E1T, ci * H, EROW
            e2_t, e2_off, e2_row = E2T, ci * H, EROW
        else:
            e1_t, e1_off, e1_row = ETT[par], 0, 48
            e2_t, e2_off, e2_row = ETT[par], H, 48
        def mkk1(e, par=par, e1_t=e1_t, e1_off=e1_off, e1_row=e1_row):
            dst = AP(KB[par], 0, [[H * 128, 128], [128, H], [1, KD]])
            srcp = AP(PHB[par], 0, [[D, 128], [KD, H], [1, KD]])
            fac = AP(e1_t, e1_off, [[e1_row, 128], [1, H], [0, KD]])
            return e.tensor_tensor(dst, srcp, fac, ALU.mult)
        sch.step("vector", mkk1)
        def mkk2(e, par=par, e2_t=e2_t, e2_off=e2_off, e2_row=e2_row):
            dst = AP(KB[par], KD, [[H * 128, 128], [128, H], [1, KD]])
            srcp = AP(PHB[par], 0, [[D, 128], [KD, H], [1, KD]])
            fac = AP(e2_t, e2_off, [[e2_row, 128], [1, H], [0, KD]])
            return e.tensor_tensor(dst, srcp, fac, ALU.mult)
        evk2 = sch.step("vector", mkk2, S.sdv)
        EV[("kb", ci)] = evk2
        sch.wait("tensor", evk2)
        pe_prev = EV.get(("uwev", ci - 1))
        if pe_prev is not None:
            sch.wait("tensor", pe_prev)
        uw_banks = [(PUW1, 0, 7), (PUW2, 7, 7), (PUW3, 14, 2)]
        ev_uw = None
        for bank, h0, nh in uw_banks:
            for hh in range(nh):
                h = h0 + hh
                def mk(e, h=h, hh=hh, bank=bank, par=par):
                    rhs = AP(VT[par], h * KV, [[HKV, 128], [1, KV]])
                    return e.matmul(bank[:, hh * KV:(hh + 1) * KV],
                                    KB[par][:, h * 128:(h + 1) * 128], rhs,
                                    start=True, stop=True)
                ev_uw = sch.step("tensor", mk, S.spe if h == H - 1 else None)
        EV[("uwmm", ci)] = ev_uw
        sch.wait("scalar", ev_uw)
        sch.step("scalar", lambda e, par=par: e.activation(
            UW[par][:, 0:7 * KV], PUW1[:, :], AF.Identity))
        sch.step("scalar", lambda e, par=par: e.activation(
            UW[par][:, 7 * KV:14 * KV], PUW2[:, :], AF.Identity))
        evc = sch.step("scalar", lambda e, par=par: e.activation(
            UW[par][:, 14 * KV:16 * KV], PUW3[:, :], AF.Identity), S.sac)
        EV[("uwev", ci)] = evc
        sch.wait("vector", evc)
        sch.wait("vector", EV[("gb", ci)])

        sch.step("vector", lambda e, par=par: e.tensor_copy(FA[0:64, :],
                                                            GB[par][0:64, :]))
        sch.step("vector", lambda e: e.tensor_copy(FB[64:128, :],
                                                   GAMR[64:128, :]))
        def mk_t(e, par=par):
            fac = AP(FB, 0, [[H, 128], [1, H], [0, KV]])
            return e.tensor_tensor(TTMP[:, :], UW[par][:, :], fac, ALU.mult)
        sch.step("vector", mk_t)
        def mk_s(e):
            fac = AP(FA, 0, [[H, 128], [1, H], [0, KV]])
            return e.tensor_tensor(STATE[:, :], STATE[:, :], fac, ALU.mult)
        sch.step("vector", mk_s)
        sch.step("vector", lambda e: e.tensor_tensor(STATE[:, :], STATE[:, :],
                                                     TTMP[:, :], ALU.add))
        ev_end = sch.step("vector",
                          lambda e, par=par: e.tensor_tensor(
                              GAMR[64:128, :], GAMR[64:128, :],
                              GB[par][64:128, :], ALU.mult), S.sdv)
        if (ci + 1) % g.group == 0:
            gidx = ci // g.group
            base = (0 if part == "own" else g.n_groups) + gidx
            off = base * (H * (KV + 1))
            def mg1(e, off=off):
                return e.tensor_copy(GSTORE[64:128, off:off + HKV],
                                     STATE[64:128, :])
            sch.step("vector", mg1)
            def mg2(e, off=off):
                return e.tensor_copy(
                    GSTORE[64:128, off + HKV:off + HKV + H], GAMR[64:128, :])
            sch.step("vector", mg2)
            sch.step("vector", lambda e: e.memset(STATE[64:128, :], 0.0))
            ev_end = sch.step("vector",
                              lambda e: e.memset(GAMR[64:128, :], 1.0), S.sdv)
        EV[("Cend", ci)] = ev_end

    def pass1_part(part, x_t, store):
        for _eng in ("vector", "scalar", "tensor"):
            sch.wait(_eng, sch.val(S.spe))
            sch.wait(_eng, sch.val(S.sdv))
            sch.wait(_eng, sch.val(S.sac))
        sch.wait("vector", sch.val(S.sst))
        sch.step("vector", lambda e: e.memset(VT[0][:], 1.0))
        sch.step("vector", lambda e: e.memset(VT[1][:], 1.0))
        sch.step("vector", lambda e: e.memset(STATE[:], 0.0))
        sch.step("vector", lambda e: e.memset(GAMR[64:128, :], 1.0))
        sch.step("vector", lambda e: e.memset(ZROF[:], 0.0), S.sdv)
        EV.clear()
        n = g.nc_own
        for i in range(n + 2):
            if i < n:
                p1_stageA(i, i % 2, x_t, store)
            if 0 <= i - 1 < n:
                p1_stageB(i - 1, (i - 1) % 2, store)
            if 0 <= i - 2 < n:
                p1_stageC(i - 2, (i - 2) % 2, store, part)

    ev_wk = wload(WK, k_w, D)
    ev_wv = wload(WV, v_w, D)
    sch.wait("tensor", ev_wv)

    def bias_fold2(W, b_t, cdram, ckbuf):
        ev_br = sch.dma("sync", BR1[0:1, :], AP(b_t, 0, [[0, 1], [1, D]]),
                        S.sld)
        sch.wait("vector", ev_br)
        evv = None
        for gi in range(2):
            evm = None
            for dc in range(DC):
                def mk(e, dc=dc, gi=gi, W=W):
                    return e.matmul(PKV[0:1, :], LB1[:, dc:dc + 1],
                                    W[:, dc * D + gi * 512:
                                      dc * D + gi * 512 + 512],
                                    start=(dc == 0), stop=(dc == DC - 1))
                evm = sch.step("tensor", mk, S.spe if dc == DC - 1 else None)
            sch.wait("vector", evm)
            def mkv(e, gi=gi):
                return e.tensor_tensor(TB1[0:1, gi * 512:(gi + 1) * 512],
                                       PKV[0:1, :],
                                       BR1[0:1, gi * 512:(gi + 1) * 512],
                                       ALU.add)
            evv = sch.step("vector", mkv, S.sdv)
            sch.wait("tensor", evv)
        sch.wait("sync", evv)
        evs = sch.dma("sync", AP(cdram, 0, [[0, 1], [1, D]]), TB1[0:1, :],
                      S.sst)
        sch.wait("sync", evs)
        evl = sch.dma("sync", ckbuf[:], AP(cdram, 0, [[0, 128], [1, D]]),
                      S.sld)
        return evl

    ev_ck = bias_fold2(WK, k_b, CKB, CKb)
    ev_cv = bias_fold2(WV, v_b, CVB, CVb)
    for W in (WK, WV):
        for dc in range(DC):
            def mks(e, W=W, dc=dc):
                return e.tensor_scalar_mul(W[:, dc * D:(dc + 1) * D],
                                           W[:, dc * D:(dc + 1) * D],
                                           LW1[:, dc:dc + 1])
            sch.step("vector", mks, S.sdv)
    sch.barrier([sch.val(S.sdv), sch.val(S.sld)])

    pass1_part("own", x_own, True)

    sch.step("vector", lambda e: e.memset(TTMP[64:128, :], 0.0))
    def roll_groups_own():
        for gidx in reversed(range(g.n_groups)):
            off = gidx * (H * (KV + 1))
            def mk1(e, off=off):
                fac = AP(GSTORE, 64 * GSW + off + HKV,
                         [[GSW, 64], [1, H], [0, KV]])
                return e.tensor_tensor(TTMP[64:128, :], TTMP[64:128, :], fac,
                                       ALU.mult)
            sch.step("vector", mk1)
            def mk2(e, off=off):
                return e.tensor_tensor(TTMP[64:128, :], TTMP[64:128, :],
                                       GSTORE[64:128, off:off + HKV], ALU.add)
            sch.step("vector", mk2)
    roll_groups_own()
    ev_blob = sch.step("vector", lambda e: e.engine_nop(), S.sdv)
    sch.wait("gpsimd", ev_blob)
    sch.dma("gpsimd", AP(EXCI, 0, [[HKV, 64], [1, HKV]]), STATE[0:64, :],
            S.sgp)
    ev_exp = sch.dma("gpsimd", AP(EXCI, 64 * HKV, [[HKV, 64], [1, HKV]]),
                     TTMP[64:128, :], S.sgp)
    sch.ops["gpsimd"].append(lambda e, v=sch.ctr["sgp"]: e.wait_ge(S.sgp, v))
    groups = [[2 * i, 2 * i + 1] for i in range(max(1, n_cores // 2))]
    ev_cc = sch.step(
        "gpsimd",
        lambda e: e.collective_compute(
            "AllGather", ALU.bypass, replica_groups=groups,
            ins=[EXCI.ap().opt()], outs=[EXCO.ap().opt()]), S.scc)
    sch.wait("sync", ev_cc)
    sch.dma("sync", TTMP[0:64, :], AP(EXCO, 0, [[HKV, 64], [1, HKV]]), S.sld)
    ev_im = sch.dma("sync", TTMP[64:128, :],
                    AP(EXCO, 128 * HKV + 64 * HKV, [[HKV, 64], [1, HKV]]),
                    S.sld)
    sch.wait("vector", ev_im)
    sch.step("vector",
             lambda e: e.tensor_scalar_mul(STATE[0:64, :], TTMP[0:64, :],
                                           HM[0:64, 0:1]))
    sch.step("vector", lambda e: e.tensor_scalar_mul(TTMP[64:128, :],
                                                     TTMP[64:128, :],
                                                     HM[64:128, 1:2]))
    for gidx in reversed(range(g.n_groups)):
        def mka(e, gidx=gidx):
            return e.tensor_copy(RANCH[64:128, gidx * HKV:(gidx + 1) * HKV],
                                 TTMP[64:128, :])
        sch.step("vector", mka)
        if gidx > 0:
            off = gidx * (H * (KV + 1))
            def mk1(e, off=off):
                fac = AP(GSTORE, 64 * GSW + off + HKV,
                         [[GSW, 64], [1, H], [0, KV]])
                return e.tensor_tensor(TTMP[64:128, :], TTMP[64:128, :], fac,
                                       ALU.mult)
            sch.step("vector", mk1)
            def mk2(e, off=off):
                return e.tensor_tensor(TTMP[64:128, :], TTMP[64:128, :],
                                       GSTORE[64:128, off:off + HKV], ALU.add)
            sch.step("vector", mk2)
    ev_p1 = sch.step("vector", lambda e: e.engine_nop(), S.sdv)
    sch.barrier([ev_p1, sch.val(S.sst), sch.val(S.spe), sch.val(S.sac),
                 sch.val(S.sgp)])
    p1s.close()
    if stop_after == "p1":
        return _early_out()

    pas = ExitStack()
    WQ = sb("WQ", (128, DC * D), BF16, pas)
    WO = sb("WO", (128, DC * D), BF16, pas)
    CQ = sb("CQ", (128, DC), F32, pas)
    TB2 = sb("TB2", (1, D), F32, pas)
    BR2 = sb("BR2", (1, D), F32, pas)
    XT2_ = sb("XT2", (128, D), F32, pas)
    XT2 = [XT2_, XT2_]
    XN2 = [sb(f"XN2{i}", (128, D), F32, pas) for i in range(2)]
    ZF2 = [sb(f"ZF2{i}", (128, D), BF16, pas) for i in range(2)]
    LB4 = [sb(f"LB4{i}", (128, H * C), F32, pas) for i in range(2)]
    PKT = [sb(f"PKT{i}", (128, D), BF16, pas) for i in range(2)]
    PVT = [sb(f"PVT{i}", (128, HKV), BF16, pas) for i in range(2)]
    PHQ = [sb(f"PHQ{i}", (128, D), BF16, pas) for i in range(2)]
    PHQ2 = [sb(f"PHQ2{i}", (64, H * 128), BF16, pas) for i in range(2)]
    PHKF2 = [sb(f"PHKF2{i}", (64, H * 128), BF16, pas) for i in range(2)]
    ATT = [sb(f"ATT{i}", (128, D), BF16, pas) for i in range(2)]
    TQ = sb("TQ", (128, D), BF16, pas)
    TE = sb("TE", (128, D), BF16, pas)
    DT = sb("DT", (128, 512), F32, pas)
    ABB = [sb(f"ABB{i}", (128, 512), BF16, pas) for i in range(2)]
    E12B_ = sb("E12B", (64, 2 * H * C), BF16, pas)
    E12B = [E12B_, E12B_]
    PHQA2 = sb("PHQA2", (64, H * 128), BF16, pas)
    PHQB2 = sb("PHQB2", (64, H * 128), BF16, pas)
    RD8 = sb("RD8", (128, 4), F32, pas)
    ATF = sb("ATF", (128, D), BF16, pas)
    YT = sb("YT", (128, D), F32, pas)
    Z2T = sb("Z2T", (128, D), BF16, pas)
    ST2 = sb("ST2", (128, 8), F32, pas)
    RBUF = sb("RBUF", (128, g.group * HKV), BF16, pas)
    KB2 = sb("KB2", (128, H * 128), BF16, pas)
    SBF = sb("SBF", (128, HKV), BF16, pas)
    GKT = sb("GKT", (128, D), BF16, pas)
    GVT = sb("GVT", (128, HKV), BF16, pas)
    UW2 = sb("UW2", (128, HKV), F32, pas)
    GB2 = sb("GB2", (128, H), F32, pas)

    PQ = [ps(f"PQ{i}", (128, 512), F32, stack=pas) for i in range(2)]
    PTR2 = ps("PTR2", (128, 1024), BF16, stack=pas)
    PSC = ps("PSC", (128, 512), F32, stack=pas)
    PG = ps("PG", (128, 512), F32, stack=pas)
    PO = [ps(f"PO{i}", (128, 4 * KV), F32, stack=pas) for i in range(2)]

    def ln_affine2(x_ap, out_ap, junk):
        return ln_affine(x_ap, out_ap, junk, ST2, tmp_ap=x_ap)

    def gamma_bcast(ci, evgate=None):
        if evgate is not None:
            sch.wait("tensor", evgate)
        def mk(e, ci=ci):
            return e.matmul(PO[1][:, 0:H], ONES1F[0:1, :],
                            GROW[0:1, ci * H:(ci + 1) * H],
                            start=True, stop=True)
        ev = sch.step("tensor", mk, S.spe)
        sch.wait("vector", ev)
        evc = sch.step("vector", lambda e: e.tensor_copy(GB2[:], PO[1][:, 0:H]),
                       S.sdv)
        sch.wait("tensor", evc)
        return evc

    def recompute_uw(ci, want_u, want_w, kt, vt):
        def mkk1(e, ci=ci, kt=kt):
            dst = AP(KB2, 0, [[H * 128, 128], [128, H], [1, KD]])
            srcp = AP(kt, 0, [[D, 128], [KD, H], [1, KD]])
            fac = AP(E1T, ci * H, [[EROW, 128], [1, H], [0, KD]])
            return e.tensor_tensor(dst, srcp, fac, ALU.mult)
        def mkk2(e, ci=ci, kt=kt):
            dst = AP(KB2, KD, [[H * 128, 128], [128, H], [1, KD]])
            srcp = AP(kt, 0, [[D, 128], [KD, H], [1, KD]])
            fac = AP(E2T, ci * H, [[EROW, 128], [1, H], [0, KD]])
            return e.tensor_tensor(dst, srcp, fac, ALU.mult)
        if want_u:
            evk = sch.step("vector", mkk1, S.sdv if not want_w else None)
        if want_w:
            evk = sch.step("vector", mkk2, S.sdv)
        sch.wait("tensor", evk)
        banks = [(PSC, 0, 7, 0), (PG, 7, 7, 0), (PO[0], 14, 2, 0)]
        ev_uw = None
        for bank, h0, nh, co in banks:
            for hh in range(nh):
                h = h0 + hh
                def mk(e, h=h, hh=hh, bank=bank, vt=vt):
                    rhs = AP(vt, h * KV, [[HKV, 128], [1, KV]])
                    return e.matmul(bank[:, hh * KV:(hh + 1) * KV],
                                    KB2[:, h * 128:(h + 1) * 128], rhs,
                                    start=True, stop=True)
                ev_uw = sch.step("tensor", mk, S.spe if h == H - 1 else None)
        sch.wait("scalar", ev_uw)
        ev = None
        rows = []
        if want_u:
            rows.append((0, 64))
        if want_w:
            rows.append((64, 128))
        for r0, r1 in rows:
            sch.step("scalar", lambda e, r0=r0, r1=r1:
                     e.activation(UW2[r0:r1, 0:7 * KV], PSC[r0:r1, 0:7 * KV],
                                  AF.Identity))
            sch.step("scalar", lambda e, r0=r0, r1=r1:
                     e.activation(UW2[r0:r1, 7 * KV:14 * KV],
                                  PG[r0:r1, 0:7 * KV], AF.Identity))
            ev = sch.step("scalar", lambda e, r0=r0, r1=r1:
                          e.activation(UW2[r0:r1, 14 * KV:16 * KV],
                                       PO[0][r0:r1, 0:2 * KV], AF.Identity),
                          S.sac)
        sch.wait("tensor", ev)
        sch.wait("vector", ev)
        return ev

    xs = (S.sx0, S.sx1)
    ys = (S.sy0, S.sy1)
    zs = (S.sz0, S.sz1)
    EV2 = {}
    ev_sb0 = sch.step("vector",
                      lambda e: e.tensor_copy(SBF[0:64, :], STATE[0:64, :]),
                      S.sdv)
    sch.step("vector", lambda e: e.memset(KB2[:], 0.0), S.sdv)
    sch.wait("tensor", ev_sb0)

    ev_wq = wload(WQ, q_w, D)
    ev_wo = wload(WO, o_w, D)
    sch.wait("tensor", ev_wq)
    ev_br = sch.dma("sync", BR2[0:1, :], AP(q_b, 0, [[0, 1], [1, D]]), S.sld)
    sch.wait("vector", ev_br)
    evv = None
    for gi in range(2):
        evm = None
        for dc in range(DC):
            def mk(e, dc=dc, gi=gi):
                return e.matmul(PQ[0][0:1, :], LB1[:, dc:dc + 1],
                                WQ[:, dc * D + gi * 512:
                                   dc * D + gi * 512 + 512],
                                start=(dc == 0), stop=(dc == DC - 1))
            evm = sch.step("tensor", mk, S.spe if dc == DC - 1 else None)
        sch.wait("vector", evm)
        def mkv(e, gi=gi):
            return e.tensor_tensor(TB2[0:1, gi * 512:(gi + 1) * 512],
                                   PQ[0][0:1, :],
                                   BR2[0:1, gi * 512:(gi + 1) * 512], ALU.add)
        evv = sch.step("vector", mkv, S.sdv)
        sch.wait("tensor", evv)
    sch.wait("sync", evv)
    ev_cq = sch.dma("sync", AP(CQB, 0, [[0, 1], [1, D]]), TB2[0:1, :], S.sst)
    sch.wait("sync", ev_cq)
    sch.dma("sync", CQ[:], AP(CQB, 0, [[1, 128], [128, DC]]), S.sld, slow=True)
    for dc in range(DC):
        def mks(e, dc=dc):
            return e.tensor_scalar_mul(WQ[:, dc * D:(dc + 1) * D],
                                       WQ[:, dc * D:(dc + 1) * D],
                                       LW1[:, dc:dc + 1])
        sch.step("vector", mks, S.sdv)
    sch.wait("tensor", ev_wo)
    sch.barrier([sch.val(S.sdv), sch.val(S.sld)])

    def pa_stageA(ci, par):
        if ci % g.group == 0:
            prev_sr = EV2.get(("sroll", ci - 1))
            if prev_sr is not None:
                sch.wait("vector", prev_sr)
                sch.wait("gpsimd", prev_sr)
            gidx = ci // g.group
            last = min(ci + g.group, g.nc_own) - 1
            def mkan(e, gidx=gidx):
                return e.tensor_copy(STATE[64:128, :],
                                     RANCH[64:128, gidx * HKV:(gidx + 1) * HKV])
            sch.step("vector", mkan)
            def mkst(e, lc=last - ci):
                return e.tensor_copy(
                    AP(RBUF, 64 * (g.group * HKV) + lc * HKV,
                       [[g.group * HKV, 64], [1, HKV]]),
                    STATE[64:128, :])
            ev_r = sch.step("vector", mkst, S.sdv)
            sch.wait("gpsimd", ev_r)
            sch.dma("gpsimd",
                    AP(RBUF, (last - ci) * HKV, [[g.group * HKV, 64], [1, HKV]]),
                    AP(RBUF, 64 * (g.group * HKV) + (last - ci) * HKV,
                       [[g.group * HKV, 64], [1, HKV]]), S.sgp)
            ev_gload = None
            for c2 in range(last - 1, ci - 1, -1):
                sch.dma("sync", GKT[:],
                        AP(PHIKs, (c2 + 1) * 128 * D, [[D, 128], [1, D]]),
                        S.sld)
                ev_gload = sch.dma(
                    "sync", GVT[:],
                    AP(VREs, (c2 + 1) * 128 * HKV, [[HKV, 128], [1, HKV]]),
                    S.sld)
                sch.wait("vector", ev_gload)
                sch.wait("tensor", ev_gload)
                ev_gb2 = gamma_bcast(c2 + 1)
                ev_w = recompute_uw(c2 + 1, False, True, GKT, GVT)
                sch.wait("vector", ev_w)
                sch.wait("sync", ev_w)
                def mkr1(e):
                    fac = AP(GB2, 64 * H, [[H, 64], [1, H], [0, KV]])
                    return e.tensor_tensor(STATE[64:128, :], STATE[64:128, :],
                                           fac, ALU.mult)
                sch.step("vector", mkr1)
                sch.step("vector",
                         lambda e: e.tensor_tensor(STATE[64:128, :],
                                                   STATE[64:128, :],
                                                   UW2[64:128, :], ALU.add))
                def mkst2(e, lc=c2 - ci):
                    return e.tensor_copy(
                        AP(RBUF, 64 * (g.group * HKV) + lc * HKV,
                           [[g.group * HKV, 64], [1, HKV]]),
                        STATE[64:128, :])
                ev_r = sch.step("vector", mkst2, S.sdv)
                sch.wait("gpsimd", ev_r)
                sch.dma("gpsimd",
                        AP(RBUF, (c2 - ci) * HKV, [[g.group * HKV, 64], [1, HKV]]),
                        AP(RBUF, 64 * (g.group * HKV) + (c2 - ci) * HKV,
                           [[g.group * HKV, 64], [1, HKV]]), S.sgp)
            sch.wait("tensor", sch.val(S.sgp))

        pq = EV2.get(("qmm", ci - 2))
        if pq is not None:
            sch.wait("sync", pq)
        sch.dma("sync", ZF2[par][:],
                AP(ZFMs, ci * C, [[g.t_own, 128], [128 * g.t_own, DC], [1, C]]),
                zs[par])
        patt = EV2.get(("att", ci - 2))
        if patt is not None:
            sch.wait("sync", patt)
        sch.dma("sync", LB4[par][:],
                AP(LROW, ci * H * C, [[0, 128], [1, H * C]]), zs[par])
        psr = EV2.get(("sroll", ci - 2))
        if psr is not None:
            sch.wait("sync", psr)
        sch.dma("sync", PKT[par][:],
                AP(PHIKs, ci * 128 * D, [[D, 128], [1, D]]), zs[par])
        ev_z = sch.dma("sync", PVT[par][:],
                       AP(VREs, ci * 128 * HKV, [[HKV, 128], [1, HKV]]),
                       zs[par])
        EV2[("z", ci)] = ev_z

        sch.wait("tensor", ev_z)
        pqf = EV2.get(("qev", ci - 1))
        if pqf is not None:
            sch.wait("tensor", pqf)
        po = EV2.get(("oev", ci - 2))
        if po is not None:
            sch.wait("tensor", po)
        evm = None
        for do in range(DC):
            bank = PQ[do // 4]
            col = (do % 4) * 128
            for dc in range(DC):
                def mk(e, do=do, dc=dc, bank=bank, col=col, par=par):
                    return e.matmul(bank[:, col:col + 128],
                                    WQ[:, dc * D + do * 128:
                                       dc * D + do * 128 + 128],
                                    ZF2[par][:, dc * 128:(dc + 1) * 128],
                                    start=(dc == 0), stop=(dc == DC - 1))
                evm = sch.step("tensor", mk,
                               S.spe if (dc == DC - 1) else None)
        EV2[("qmm", ci)] = evm
        sch.wait("vector", evm)
        def mkb0(e):
            fac = AP(CQ, 0, [[DC, 128], [1, 4], [0, 128]])
            return e.tensor_tensor(TQ[:, 0:512], PQ[0][:, :], fac, ALU.add)
        sch.step("vector", mkb0)
        def mkb1(e):
            fac = AP(CQ, 4, [[DC, 128], [1, 4], [0, 128]])
            return e.tensor_tensor(TQ[:, 512:1024], PQ[1][:, :], fac, ALU.add)
        evq = sch.step("vector", mkb1, S.sdv)
        sch.wait("scalar", evq)
        sch.step("scalar", lambda e: e.activation(TE[:], TQ[:], AF.Exp))
        evr = sch.step("scalar", lambda e: e.activation(TQ[:], TQ[:], AF.Relu),
                       S.sac)
        sch.wait("vector", evr)
        sch.step("vector", lambda e: e.tensor_scalar_min(TE[:], TE[:], 1.0))
        def mkphi(e, par=par):
            return e.tensor_tensor(PHQ[par][:], TE[:], TQ[:], ALU.add)
        evphi = sch.step("vector", mkphi, S.sdv)
        EV2[("qev", ci)] = evphi
        def mkq2(e, par=par):
            dst = AP(PHQ2[par], 0, [[H * 128, 64], [256, 8], [1, 128]])
            srcp = AP(PHQ[par], 0, [[D, 64], [128, 8], [1, 128]])
            return e.tensor_copy(dst, srcp)
        ev_q2a = sch.step("vector", mkq2, S.sdv)
        sch.wait("gpsimd", ev_q2a)
        def mkq2b(e, par=par):
            dst = AP(PHQ2[par], 128, [[H * 128, 64], [256, 8], [1, 128]])
            srcp = AP(PHQ[par], 64 * D, [[D, 64], [128, 8], [1, 128]])
            return e.dma_start(dst, srcp)
        ev_q2b = sch.step("gpsimd", mkq2b, S.sgp, 16)
        EV2[("q2", ci)] = ev_q2b

    def pa_stageB(ci, par):
        pc2 = EV2.get(("c2", ci - 2))
        if pc2 is not None:
            sch.wait("sync", pc2)
        sch.dma("sync", XN2[par][:],
                AP(XN32, ci * C * D, [[D, 128], [1, D]]), ys[par])
        pqs = EV2.get(("qsc", ci - 1))
        if pqs is not None:
            sch.wait("sync", pqs)
        ev_y = sch.dma("sync", E12B[par][:],
                       AP(E12ROW, ci * 2 * H * C, [[0, 64], [1, 2 * H * C]]),
                       ys[par])
        EV2[("y", ci)] = ev_y
        EV2[("e12", ci)] = ev_y
        ps_ = EV2.get(("sroll", ci - 1))
        if ps_ is not None:
            sch.wait("tensor", ps_)
        ptrf = EV2.get(("atf", ci - 1))
        if ptrf is not None:
            sch.wait("tensor", ptrf)
        for rnd in range(2):
            evt = None
            for hh in range(8):
                h = rnd * 8 + hh
                def mkt(e, h=h, hh=hh, par=par):
                    off = (h // 2) * 128 + (h % 2) * 64
                    return e.transpose(
                        PTR2[0:64, hh * 128:(hh + 1) * 128],
                        AP(PKT[par], off, [[D, 128], [1, 64]]), IDENT[:])
                evt = sch.step("tensor", mkt, S.spe if hh == 7 else None)
            sch.wait("vector", evt)
            def mkev(e, rnd=rnd, par=par):
                return e.tensor_copy(PHKF2[par][0:64, rnd * 1024:(rnd + 1) * 1024],
                                     PTR2[0:64, 0:1024])
            ev_pkf = sch.step("vector", mkev, S.sdv)
            sch.wait("tensor", ev_pkf)

        sch.wait("tensor", EV2[("q2", ci)])
        sch.wait("vector", EV2[("q2", ci)])
        sch.wait("vector", EV2[("e12", ci)])
        pao = EV2.get(("abb", (ci - 1, 3)))
        if pao is not None:
            sch.wait("vector", pao)
        def mksa(e, par=par):
            return e.tensor_tensor(PHQA2[:], PHQ2[par][:],
                                   E12B[par][:, H * C:2 * H * C], ALU.mult)
        sch.step("vector", mksa)
        def mksb(e, par=par):
            return e.tensor_tensor(PHQB2[:], PHQ2[par][:],
                                   E12B[par][:, 0:H * C], ALU.mult)
        ev_qsc = sch.step("vector", mksb, S.sdv)
        EV2[("qsc", ci)] = ev_qsc
        sch.wait("tensor", ev_qsc)
        ev_combine = None
        for b4 in range(4):
            def mksub(e, b4=b4, ci=ci, par=par):
                fac = AP(LTM, ci * H + b4 * 4, [[EROW, 128], [1, 4], [0, 128]])
                return e.tensor_tensor(DT[:],
                                       LB4[par][:, b4 * 512:(b4 + 1) * 512],
                                       fac, ALU.subtract)
            ev_g = sch.step("vector", mksub, S.sdv)
            sch.wait("scalar", ev_g)
            sch.step("scalar", lambda e: e.activation(DT[:], DT[:], AF.Abs))
            ev_d = sch.step("scalar",
                            lambda e: e.activation(DT[:], DT[:], AF.Exp,
                                                   scale=-1.0), S.sac)
            ev_sc = None
            for hh in range(4):
                h = b4 * 4 + hh
                def mksc(e, h=h, hh=hh, par=par):
                    lhsT = AP(PHKF2[par], h * 128, [[H * 128, 64], [1, 128]])
                    rhs = AP(PHQ2[par], h * 128, [[H * 128, 64], [1, 128]])
                    return e.matmul(PSC[:, hh * 128:(hh + 1) * 128], lhsT,
                                    rhs, start=True, stop=True)
                ev_sc = sch.step("tensor", mksc, S.spe if hh == 3 else None)
            sch.wait("vector", ev_sc)
            sch.wait("vector", ev_d)
            pab = EV2.get(("abb", (ci, b4 - 2)))
            if pab is not None:
                sch.wait("vector", pab)
            def mka(e, b4=b4):
                return e.tensor_tensor(ABB[b4 % 2][:], PSC[:, :], DT[:],
                                       ALU.mult)
            ev_a = sch.step("vector", mka, S.sdv)
            sch.wait("tensor", ev_a)
            pdv = (EV2.get(("div", (ci, b4 - 2))) if b4 >= 2 else
                   EV2.get(("div", (ci - 1, b4 + 2))))
            if pdv is not None:
                sch.wait("tensor", pdv)
            ev_o = None
            for hh in range(4):
                h = b4 * 4 + hh
                def mko1(e, hh=hh, par=par, h=h, b4=b4):
                    rhs = AP(PVT[par], h * KV, [[HKV, 128], [1, KV]])
                    return e.matmul(PO[b4 % 2][:, hh * KV:(hh + 1) * KV],
                                    AP(ABB[b4 % 2], hh * 128,
                                       [[512, 128], [1, 128]]),
                                    rhs, start=True, stop=False)
                sch.step("tensor", mko1)
                def mko2(e, h=h, hh=hh, b4=b4):
                    lhsT = AP(PHQA2, h * 128, [[H * 128, 64], [1, 128]])
                    rhs = AP(SBF, h * KV, [[HKV, 64], [1, KV]])
                    return e.matmul(PO[b4 % 2][:, hh * KV:(hh + 1) * KV], lhsT,
                                    rhs, start=False, stop=False)
                sch.step("tensor", mko2)
                def mko3(e, h=h, hh=hh, b4=b4, lc=ci % g.group):
                    lhsT = AP(PHQB2, h * 128, [[H * 128, 64], [1, 128]])
                    rhs = AP(RBUF, lc * HKV + h * KV,
                             [[g.group * HKV, 64], [1, KV]])
                    return e.matmul(PO[b4 % 2][:, hh * KV:(hh + 1) * KV], lhsT,
                                    rhs, start=False, stop=True)
                ev_o = sch.step("tensor", mko3, S.spe if hh == 3 else None)
            EV2[("abb", (ci, b4))] = ev_o
            sch.wait("vector", ev_o)
            def mkden(e, b4=b4):
                srcp = AP(PO[b4 % 2], KD, [[4 * KV, 128], [KV, 4]])
                return e.tensor_scalar_max(RD8[:], srcp, DEN_EPS)
            sch.step("vector", mkden)
            sch.step("vector", lambda e: e.reciprocal(RD8[:], RD8[:]))
            if b4 == 0:
                paf = EV2.get(("atf", ci - 2))
                if paf is not None:
                    sch.wait("vector", paf)
            def mkdiv(e, b4=b4, par=par):
                srcp = AP(PO[b4 % 2], 0, [[4 * KV, 128], [KV, 4], [1, KD]])
                fac = AP(RD8, 0, [[4, 128], [1, 4], [0, KD]])
                dst = AP(ATT[par], b4 * 256, [[D, 128], [KD, 4], [1, KD]])
                return e.tensor_tensor(dst, srcp, fac, ALU.mult)
            ev_combine = sch.step("vector", mkdiv, S.sdv)
            EV2[("div", (ci, b4))] = ev_combine
        EV2[("att", ci)] = ev_combine
        ev_gb3 = gamma_bcast(ci, evgate=ev_combine)
        ev_u = recompute_uw(ci, True, False, PKT[par], PVT[par])
        sch.wait("vector", ev_u)
        sch.wait("vector", ev_gb3)
        def mks1(e):
            fac = AP(GB2, 0, [[H, 64], [1, H], [0, KV]])
            return e.tensor_tensor(STATE[0:64, :], STATE[0:64, :], fac, ALU.mult)
        sch.step("vector", mks1)
        sch.step("vector",
                 lambda e: e.tensor_tensor(STATE[0:64, :], STATE[0:64, :],
                                           UW2[0:64, :], ALU.add))
        ev_sr = sch.step("vector",
                         lambda e: e.tensor_copy(SBF[0:64, :], STATE[0:64, :]),
                         S.sdv)
        sch.wait("tensor", ev_sr)
        EV2[("sroll", ci)] = ev_sr

    def pa_stageC(ci, par):
        sch.wait("tensor", EV2[("att", ci)])
        evt = None
        for dc in range(DC):
            def mkt(e, dc=dc, par=par):
                return e.transpose(PTR2[:, dc * 128:(dc + 1) * 128],
                                   ATT[par][:, dc * 128:(dc + 1) * 128],
                                   IDENT[:])
            evt = sch.step("tensor", mkt, S.spe if dc == DC - 1 else None)
        sch.wait("vector", evt)
        ev_atf = sch.step("vector", lambda e: e.tensor_copy(ATF[:], PTR2[:]),
                          S.sdv)
        EV2[("atf", ci)] = ev_atf
        sch.wait("tensor", ev_atf)
        sch.wait("sync", sch.val(S.sst))
        ev_x = sch.dma("sync", XT2[par][:],
                       AP(x_own, ci * C * D, [[D, 128], [1, D]]), xs[par])
        sch.wait("vector", EV2[("y", ci)])
        sch.wait("vector", sch.val(S.sst))
        ev_yb = None
        for got in range(2):
            bank = PQ[got]
            evm = None
            for dc in range(DC):
                def mk(e, dc=dc, got=got, bank=bank):
                    return e.matmul(bank[:, :],
                                    ATF[:, dc * 128:(dc + 1) * 128],
                                    WO[:, dc * D + got * 512:
                                       dc * D + got * 512 + 512],
                                    start=(dc == 0), stop=(dc == DC - 1))
                evm = sch.step("tensor", mk, S.spe if dc == DC - 1 else None)
            sch.wait("vector", evm)
            def mky(e, got=got, bank=bank, par=par):
                return e.scalar_tensor_tensor(
                    YT[:, got * 512:(got + 1) * 512], bank[:, :], 1.0,
                    XN2[par][:, got * 512:(got + 1) * 512], ALU.mult, ALU.add)
            sch.step("vector", mky)
            ev_yb = sch.step("vector",
                             lambda e, got=got:
                             e.tensor_tensor(YT[:, got * 512:(got + 1) * 512],
                                             YT[:, got * 512:(got + 1) * 512],
                                             OBb[:, got * 512:(got + 1) * 512],
                                             ALU.add), S.sdv)
            sch.wait("tensor", ev_yb)
        EV2[("oev", ci)] = ev_yb
        sch.wait("vector", ev_x)
        def mkp(e, par=par):
            return e.tensor_tensor(XT2[par][:], YT[:], XT2[par][:], ALU.add)
        ev_pt = sch.step("vector", mkp, S.sdv)
        sch.wait("sync", ev_pt)
        sch.dma("sync", AP(PBs, ci * C * D, [[D, 128], [1, D]]), XT2[par][:],
                S.sst)
        sch.wait("scalar", sch.val(S.sst))
        ev_z2 = ln_affine2(YT[:], Z2T[:], XN2[par])
        sch.wait("sync", ev_z2)
        sch.dma("sync", AP(Z2B, ci * C * D, [[D, 128], [1, D]]), Z2T[:], S.sst)
        ev_c2 = sch.step("vector", lambda e: e.engine_nop(), S.sca)
        EV2[("c2", ci)] = ev_c2

    npair = g.nc_own // 2
    for j in range(npair + 2):
        for cc in (2 * (j - 2), 2 * (j - 2) + 1):
            if 0 <= cc < g.nc_own:
                pa_stageC(cc, cc % 2)
        for cc in (2 * (j - 1), 2 * (j - 1) + 1):
            if 0 <= cc < g.nc_own:
                pa_stageB(cc, cc % 2)
        for cc in (2 * j, 2 * j + 1):
            if 0 <= cc < g.nc_own:
                pa_stageA(cc, cc % 2)

    ev_pa = sch.step("vector", lambda e: e.engine_nop(), S.sdv)
    sch.barrier([ev_pa, sch.val(S.sst), sch.val(S.spe), sch.val(S.sac)])
    pas.close()
    att.close()
    if stop_after == "pa":
        return _early_out()

    def ffn_phase(Wa_t, Wb_t, bias1_fm, in_scr, bias2_bc, dst_t, dst_sem,
                  scale_lw2, tag):
        fs = ExitStack()
        mg = g.midgroup
        ntg = mg // 128
        W1S = sb(f"W1S{tag}", (128, DC * FFN), BF16, fs)
        W2S = sb(f"W2S{tag}", (128, FC * D), BF16, fs)
        HMID = sb(f"HMID{tag}", (128, FC * mg), BF16, fs)
        XFW = sb(f"XFW{tag}", (128, DC * mg), BF16, fs)
        PSL = [sb(f"PSL{tag}{i}", (128, D), F32, fs) for i in range(2)]
        OT = [sb(f"OT{tag}{i}", (128, 512), F32, fs) for i in range(2)]
        PH = [ps(f"PH{tag}{i}", (128, mg), F32, stack=fs) for i in range(2)]
        PACC = [ps(f"PACC{tag}{i}", (128, 512), F32, stack=fs) for i in range(2)]

        ev_w1s = []
        for j in range(8):
            ev = sch.dma("gpsimd",
                         AP(W1S, j * 512,
                            [[DC * FFN, 128], [FFN, DC], [1, 512]]),
                         AP(Wa_t, j * 512,
                            [[FFN, 128], [128 * FFN, DC], [1, 512]]),
                         S.sw0)
            ev_w1s.append(ev)
        ev_w2s = []
        for j in range(8):
            ev = sch.dma("gpsimd",
                         AP(W2S, j * 4 * D, [[FC * D, 128], [1, 4 * D]]),
                         AP(Wb_t, j * 4 * 128 * D,
                            [[D, 128], [128 * D, 4], [1, D]]),
                         S.sw1)
            ev_w2s.append(ev)
        if scale_lw2:
            for j in range(8):
                sch.wait("vector", ev_w1s[j])
                evf = None
                for dc in range(DC):
                    def mks(e, dc=dc, j=j):
                        sl = slice(dc * FFN + j * 512, dc * FFN + j * 512 + 512)
                        return e.tensor_scalar_mul(W1S[:, sl], W1S[:, sl],
                                                   LW2[:, dc:dc + 1])
                    evf = sch.step("vector", mks,
                                   S.sdv if dc == DC - 1 else None)
                ev_w1s[j] = evf

        ev_hold = [None, None]
        ev_accfree = [None, None]
        ev_otfree = [None, None]
        ev_pfree = [None, None]
        ev_xfree = None
        ev_hfree = None
        def load_xfw(m):
            t0 = m * mg
            ev_in = None
            for dc in range(DC):
                ev_in = sch.dmat("sync", XFW[:, dc * mg:(dc + 1) * mg],
                                 AP(in_scr, t0 * D + dc * 128,
                                    [[D, mg], [1, 128]]), S.sz0)
            return ev_in
        ev_in = load_xfw(0)
        for m in range(g.n_mid):
            t0 = m * mg
            sch.wait("tensor", ev_in)
            if ev_hfree is not None:
                sch.wait("scalar", ev_hfree)
            ev_f1last = None
            ev_g = None
            for c in range(FC):
                bank = PH[c % 2]
                if m == 0:
                    sch.wait("tensor", ev_w1s[c // 4])
                if ev_hold[c % 2] is not None:
                    sch.wait("tensor", ev_hold[c % 2])
                evm = None
                for dc in range(DC):
                    def mk(e, c=c, dc=dc, bank=bank):
                        return e.matmul(bank[:, :],
                                        W1S[:, dc * FFN + c * 128:
                                            dc * FFN + c * 128 + 128],
                                        XFW[:, dc * mg:(dc + 1) * mg],
                                        start=(dc == 0), stop=(dc == DC - 1))
                    evm = sch.step("tensor", mk,
                                   S.spe if dc == DC - 1 else None)
                ev_f1last = evm
                sch.wait("scalar", evm)
                def mkg(e, c=c, bank=bank):
                    return e.activation(HMID[:, c * mg:(c + 1) * mg],
                                        bank[:, :], AF.Gelu,
                                        bias=bias1_fm[:, c:c + 1])
                ev_g = sch.step("scalar", mkg, S.sac)
                ev_hold[c % 2] = ev_g
            ev_xfree = ev_f1last
            if m + 1 < g.n_mid:
                sch.wait("sync", ev_xfree)
                ev_in = load_xfw(m + 1)
            sch.wait("tensor", ev_g)
            pv = 0
            ev_p = [None, None]
            ev_last_f2 = None
            for tg in range(ntg):
                tpar = tg % 2
                if ev_pfree[tpar] is not None:
                    sch.wait("sync", ev_pfree[tpar])
                ev_p[tpar] = sch.dma("sync", PSL[tpar][:],
                                     AP(PBs, (t0 + tg * 128) * D,
                                        [[D, 128], [1, D]]),
                                     (S.sp0, S.sp1)[tpar])
                for ng in range(2):
                    opar = pv % 2
                    bank = PACC[opar]
                    if ev_accfree[opar] is not None:
                        sch.wait("tensor", ev_accfree[opar])
                    evm = None
                    for c in range(FC):
                        if m == 0 and tg == 0 and ng == 0:
                            sch.wait("tensor", ev_w2s[c // 4])
                        def mk(e, c=c, tg=tg, ng=ng, bank=bank):
                            return e.matmul(
                                bank[:, :],
                                AP(HMID, c * mg + tg * 128,
                                   [[FC * mg, 128], [1, 128]]),
                                W2S[:, c * D + ng * 512:
                                    c * D + ng * 512 + 512],
                                start=(c == 0), stop=(c == FC - 1))
                        evm = sch.step("tensor", mk,
                                       S.spe if c == FC - 1 else None)
                    ev_last_f2 = evm
                    sch.wait("vector", evm)
                    sch.wait("vector", ev_p[tpar])
                    if ev_otfree[opar] is not None:
                        sch.wait("vector", ev_otfree[opar])
                    def mko(e, ng=ng, bank=bank, opar=opar, tpar=tpar):
                        return e.scalar_tensor_tensor(
                            OT[opar][:], bank[:, :], 1.0,
                            PSL[tpar][:, ng * 512:(ng + 1) * 512],
                            ALU.mult, ALU.add)
                    sch.step("vector", mko)
                    def mko2(e, ng=ng, opar=opar):
                        return e.tensor_tensor(
                            OT[opar][:], OT[opar][:],
                            bias2_bc[:, ng * 512:(ng + 1) * 512], ALU.add)
                    ev_ot = sch.step("vector", mko2, S.sdv)
                    ev_accfree[opar] = ev_ot
                    sch.wait("sync", ev_ot)
                    ev_st = sch.dma("sync",
                                    AP(dst_t, (t0 + tg * 128) * D + ng * 512,
                                       [[D, 128], [1, 512]]),
                                    OT[opar][:], dst_sem)
                    ev_otfree[opar] = ev_st
                    if ng == 1:
                        ev_pfree[tpar] = ev_st
                    pv += 1
            ev_hfree = ev_last_f2
        ev_done = [sch.val(S.sdv), sch.val(S.sac), sch.val(S.spe),
                   sch.val(dst_sem), sch.val(S.sw0), sch.val(S.sw1)]
        sch.barrier(ev_done)
        fs.close()

    ffn_phase(w1_w, w2_w, W1B, XNB, W2Bb, PBs, S.sst, False, "w")
    if stop_after == "pw":
        return _early_out()
    ffn_phase(f1_w, f2_w, CF1, Z2B, F2Bb, out_t, S.sout, True, "f")

    sch.wait("sync", sch.val(S.sout))

    sch.run_into(nc)
    return nc



def _prep_common(inputs):
    f = np.ascontiguousarray
    w = {
        "q_w": f(inputs["L0_attn_q_w"], dtype=np.float32),
        "q_b": f(inputs["L0_attn_q_b"], dtype=np.float32).reshape(1, D),
        "k_w": f(inputs["L0_attn_k_w"], dtype=np.float32),
        "k_b": f(inputs["L0_attn_k_b"], dtype=np.float32).reshape(1, D),
        "v_w": f(inputs["L0_attn_v_w"], dtype=np.float32),
        "v_b": f(inputs["L0_attn_v_b"], dtype=np.float32).reshape(1, D),
        "o_w": f(inputs["L0_attn_o_w"], dtype=np.float32),
        "o_b": f(inputs["L0_attn_o_b"], dtype=np.float32).reshape(1, D),
        "dec_w": f(inputs["L0_attn_dec_w"], dtype=np.float32),
        "dec_b": f(inputs["L0_attn_dec_b"], dtype=np.float32).reshape(1, H),
        "ln1_w": f(inputs["L0_attn_ln1_w"], dtype=np.float32).reshape(1, D),
        "ln1_b": f(inputs["L0_attn_ln1_b"], dtype=np.float32).reshape(1, D),
        "ln2_w": f(inputs["L0_attn_ln2_w"], dtype=np.float32).reshape(1, D),
        "ln2_b": f(inputs["L0_attn_ln2_b"], dtype=np.float32).reshape(1, D),
        "f1_w": f(inputs["L0_attn_f1_w"], dtype=np.float32),
        "f1_b": f(inputs["L0_attn_f1_b"], dtype=np.float32).reshape(1, FFN),
        "f2_w": f(inputs["L0_attn_f2_w"], dtype=np.float32),
        "f2_b": f(inputs["L0_attn_f2_b"], dtype=np.float32).reshape(1, D),
        "w1_w": f(inputs["L0_ffn_w1"], dtype=np.float32),
        "w1_b": f(inputs["L0_ffn_b1"], dtype=np.float32).reshape(1, FFN),
        "w2_w": f(inputs["L0_ffn_w2"], dtype=np.float32),
        "w2_b": f(inputs["L0_ffn_b2"], dtype=np.float32).reshape(1, D),
        "ident": np.eye(128, dtype=np.float32).astype(bf16),
        "identf": np.eye(128, dtype=np.float32),
        "onesrow": np.ones((1, 128), dtype=np.float32).astype(bf16),
        "onesrowf": np.ones((1, 128), dtype=np.float32),
    }
    return w


def kernel(**inputs):
    x = np.ascontiguousarray(np.asarray(inputs["L0_attn_x"], dtype=np.float32))
    B, T, _ = x.shape
    g = Geo(t_seq=T)
    n_cores = 2 * B
    w = _prep_common(inputs)
    in_maps = []
    for core in range(n_cores):
        b, half = core // 2, core % 2
        own = x[b, half * g.t_own:(half + 1) * g.t_own]
        rem = x[b, (1 - half) * g.t_own:(2 - half) * g.t_own]
        m = dict(w)
        m["x_own"] = np.ascontiguousarray(own)
        m["x_rem"] = np.ascontiguousarray(rem)
        m["hmask"] = np.array([[float(half), float(1 - half)]],
                              dtype=np.float32)
        in_maps.append(m)
    nc = build_program(g, n_cores=n_cores)
    res = run_bass_kernel_spmd(nc, in_maps, core_ids=list(range(n_cores)))
    outs = [res.results[i]["out"] for i in range(n_cores)]
    full = np.empty((B, T, D), dtype=np.float32)
    for core in range(n_cores):
        b, half = core // 2, core % 2
        full[b, half * g.t_own:(half + 1) * g.t_own] = outs[core]
    return full


# version 28
# speedup vs baseline: 1.0170x; 1.0046x over previous
from collections import defaultdict
from contextlib import ExitStack

import numpy as np
import ml_dtypes
import concourse.bass as bass
import concourse.mybir as mybir
from concourse.bass_utils import run_bass_kernel_spmd

F32 = mybir.dt.float32
BF16 = mybir.dt.bfloat16
AF = mybir.ActivationFunctionType
ALU = mybir.AluOpType
bf16 = ml_dtypes.bfloat16

D, H, KD, KV, FFN, C = 1024, 16, 64, 65, 4096, 128
DC, FC = D // 128, FFN // 128
HKV = H * KV
LN_EPS, LAM_MIN, DEN_EPS = 1e-5, 1e-8, 1e-6


class Geo:
    def __init__(self, t_seq=8192, group=8, midgroup=512):
        self.t_seq = t_seq
        self.t_own = t_seq // 2
        self.nc_own = self.t_own // C
        self.group = min(group, self.nc_own)
        assert self.nc_own % self.group == 0
        self.n_groups = self.nc_own // self.group
        self.midgroup = min(midgroup, self.t_own)
        assert self.t_own % self.midgroup == 0
        self.n_mid = self.t_own // self.midgroup


def AP(t, off, pat):
    return bass.AP(t, off, pat)


class Sched:
    ENGS = ("sync", "gpsimd", "tensor", "vector", "scalar")
    PARANOID = False

    def __init__(self):
        self.ops = {e: [] for e in self.ENGS}
        self.ctr = defaultdict(int)
        self.dma_issuer = {}
        self.progress = {}
        self.serial_sems = set()

    def set_progress_sems(self, mapping):
        self.progress = mapping

    def _emit(self, eng, fn):
        self.ops[eng].append(fn)

    DRAIN_ENGS = ("vector", "scalar")

    def step(self, eng, make, sem=None, n=1):
        drain = eng in self.DRAIN_ENGS or self.PARANOID
        if sem is None:
            if drain:
                self._emit(eng, lambda e, mk=make: (mk(e), e.drain()))
            else:
                self._emit(eng, lambda e, mk=make: mk(e))
            return None
        self.ctr[sem.name] += n
        if drain and eng in self.DRAIN_ENGS:
            self._emit(eng, lambda e, mk=make, s=sem, k=n:
                       (mk(e), e.drain().then_inc(s, k)))
        elif drain:
            self._emit(eng, lambda e, mk=make, s=sem, k=n:
                       (mk(e).then_inc(s, k), e.drain()))
        else:
            self._emit(eng, lambda e, mk=make, s=sem, k=n:
                       mk(e).then_inc(s, k))
        return (sem, self.ctr[sem.name])

    def dma(self, eng, dst, src, sem, slow=False):
        self.dma_issuer[sem.name] = (eng, sem)
        if slow:
            return self.step(
                eng, lambda e, d=dst, s=src: e.dma_start(
                    d, s, allow_slow_non_contiguous=True), sem, 16)
        return self.step(eng, lambda e, d=dst, s=src: e.dma_start(d, s),
                         sem, 16)

    def dmat(self, eng, dst, src, sem):
        self.dma_issuer[sem.name] = (eng, sem)
        return self.step(
            eng, lambda e, d=dst, s=src: e.dma_start(d, s, transpose=True),
            sem, 16)

    def val(self, sem):
        return (sem, self.ctr[sem.name])

    def wait(self, eng, ev):
        if ev is None:
            return
        sem, v = ev
        if v <= 0:
            return
        iss = self.dma_issuer.get(sem.name)
        if iss is not None and sem.name in self.serial_sems:
            self._emit(eng, lambda e, s=sem, vv=v: e.wait_ge(s, vv))
            return
        if iss is not None and iss[0] != eng:
            ieng = iss[0]
            full = self.ctr[sem.name]
            self._emit(ieng, lambda e, s=sem, vv=full: e.wait_ge(s, vv))
            pev = self.step(ieng, lambda e: e.nop(), self.progress[ieng])
            self._emit(eng, lambda e, s=pev[0], vv=pev[1]: e.wait_ge(s, vv))
        else:
            self._emit(eng, lambda e, s=sem, vv=v: e.wait_ge(s, vv))

    def barrier(self, evs):
        for eng in self.ENGS:
            for ev in evs:
                self.wait(eng, ev)

    def run_into(self, nc):
        with nc.Block() as block:
            for name in self.ENGS:
                lst = self.ops[name]

                def make_body(lst_):
                    def body(e):
                        for fn in lst_:
                            fn(e)
                    return body

                getattr(block, name)(make_body(lst))


def build_program(g: Geo, n_cores=8, stop_after=None, paranoid=False):
    nc = bass.Bass("TRN2", target_bir_lowering=False,
                   detect_race_conditions=paranoid)
    es = ExitStack()
    sch = Sched()
    sch.PARANOID = paranoid
    inv_d = 1.0 / D

    def din(name, shape, dt=F32):
        return nc.dram_tensor(name, list(shape), dt, kind="ExternalInput")

    x_own = din("x_own", (g.t_own, D))
    x_rem = din("x_rem", (g.t_own, D))
    q_w = din("q_w", (D, D)); q_b = din("q_b", (1, D))
    k_w = din("k_w", (D, D)); k_b = din("k_b", (1, D))
    v_w = din("v_w", (D, D)); v_b = din("v_b", (1, D))
    o_w = din("o_w", (D, D)); o_b = din("o_b", (1, D))
    dec_w = din("dec_w", (D, H)); dec_b = din("dec_b", (1, H))
    ln1_w = din("ln1_w", (1, D)); ln1_b = din("ln1_b", (1, D))
    ln2_w = din("ln2_w", (1, D)); ln2_b = din("ln2_b", (1, D))
    f1_w = din("f1_w", (D, FFN)); f1_b = din("f1_b", (1, FFN))
    f2_w = din("f2_w", (FFN, D)); f2_b = din("f2_b", (1, D))
    w1_w = din("w1_w", (D, FFN)); w1_b = din("w1_b", (1, FFN))
    w2_w = din("w2_w", (FFN, D)); w2_b = din("w2_b", (1, D))
    hmask_in = din("hmask", (1, 2))
    ident_in = din("ident", (128, 128), BF16)
    identf_in = din("identf", (128, 128), F32)
    ones_in = din("onesrow", (1, 128), BF16)
    onesf_in = din("onesrowf", (1, 128), F32)

    out_t = nc.dram_tensor("out", [g.t_own, D], F32, kind="ExternalOutput")

    def dscr(name, shape, dt):
        return nc.dram_tensor(name, list(shape), dt, kind="Internal")

    XN32 = dscr("XN32", (g.t_own, D), F32)
    XNB = dscr("XNB", (g.t_own, D), BF16)
    ZFMs = dscr("ZFMs", (DC, 128, g.t_own), BF16)
    Z2B = dscr("Z2B", (g.t_own, D), BF16)
    PBs = dscr("PBs", (g.t_own, D), F32)
    LROW = dscr("LROW", (g.nc_own, H * C), F32)
    E12ROW = dscr("E12ROW", (g.nc_own, 2 * H * C), BF16)
    PHIKs = dscr("PHIKs", (g.nc_own, 128, D), BF16)
    VREs = dscr("VREs", (g.nc_own, 128, HKV), BF16)
    CQB = dscr("CQB", (1, D), F32)
    CKB = dscr("CKB", (1, D), F32)
    CVB = dscr("CVB", (1, D), F32)
    CDB = dscr("CDB", (1, H), F32)
    CF1B = dscr("CF1B", (1, FFN), F32)
    EXCI = dscr("EXCI", (128, HKV), F32)
    EXCO = dscr("EXCO", (2 * 128, HKV), F32)

    SEM = {}
    for s in ("sx0", "sx1", "sy0", "sy1", "sz0", "sz1", "sw0", "sw1",
              "sp0", "sp1", "sst", "sld", "sout",
              "spe", "sdv", "sac", "sgp", "sqp", "gqp", "scc", "sca"):
        SEM[s] = es.enter_context(nc.semaphore(s))
    S = type("S", (), SEM)()
    sch.set_progress_sems({"sync": S.sqp, "gpsimd": S.gqp})
    sch.serial_sems = {"sx0", "sx1", "sy0", "sy1", "sz0", "sz1",
                       "sp0", "sp1"}

    def sb(name, shape, dt, stack=None):
        return (stack or es).enter_context(nc.sbuf_tensor(name, list(shape), dt))

    def ps(name, shape, dt=F32, stack=None):
        return (stack or es).enter_context(nc.psum_tensor(name, list(shape), dt))

    IDENT = sb("IDENT", (128, 128), BF16)
    IDF = sb("IDF", (128, 128), F32)
    ONES1 = sb("ONES1", (1, 128), BF16)
    ONES1F = sb("ONES1F", (1, 128), F32)
    HM = sb("HM", (128, 2), F32)
    OBb = sb("OBb", (128, D), F32)
    F2Bb = sb("F2Bb", (128, D), F32)
    W2Bb = sb("W2Bb", (128, D), F32)
    W1B = sb("W1B", (128, FC), F32)
    CF1 = sb("CF1", (128, FC), F32)
    LW2 = sb("LW2", (128, DC), F32)

    att = ExitStack()
    WD = sb("WD", (128, DC * H), BF16, att)
    NCD = sb("NCD", (H, 1), F32, att)
    LW1 = sb("LW1", (128, DC), F32, att)
    LB1 = sb("LB1", (128, DC), BF16, att)
    E1T = sb("E1T", (128, g.nc_own * H), F32, att)
    E2T = sb("E2T", (128, g.nc_own * H), F32, att)
    LTM = sb("LTM", (128, g.nc_own * H), F32, att)
    GROW = sb("GROW", (1, g.nc_own * H), F32, att)
    RANCH = sb("RANCH", (128, g.n_groups * HKV), BF16, att)
    STATE = sb("STATE", (128, HKV), F32, att)
    GAMR = sb("GAMR", (128, H), F32, att)
    GB = sb("GB", (128, H), F32, att)
    FA = sb("FA", (128, H), F32, att)
    FB = sb("FB", (128, H), F32, att)

    EROW = g.nc_own * H

    def _early_out():
        evd = sch.dma("sync", AP(out_t, 0, [[D, 128], [1, D]]), OBb[:], S.sout)
        sch.wait("sync", evd)
        sch.run_into(nc)
        return nc

    p0s = ExitStack()
    sch.dma("sync", IDENT[:], ident_in.ap(), S.sld)
    sch.dma("sync", IDF[:], identf_in.ap(), S.sld)
    sch.dma("sync", ONES1[:], ones_in.ap(), S.sld)
    sch.dma("sync", ONES1F[:], onesf_in.ap(), S.sld)
    sch.dma("sync", HM[:], AP(hmask_in, 0, [[0, 128], [1, 2]]), S.sld)
    for dst, srct, nn_ in ((OBb, o_b, D), (F2Bb, f2_b, D), (W2Bb, w2_b, D)):
        sch.dma("sync", dst[:], AP(srct, 0, [[0, 128], [1, nn_]]), S.sld)
    sch.dma("sync", W1B[:], AP(w1_b, 0, [[1, 128], [128, FC]]), S.sld, slow=True)
    sch.dma("sync", LW1[:], AP(ln1_w, 0, [[1, 128], [128, DC]]), S.sld, slow=True)
    sch.dma("gpsimd", LB1[:], AP(ln1_b, 0, [[1, 128], [128, DC]]), S.sw0, slow=True)
    sch.dma("sync", LW2[:], AP(ln2_w, 0, [[1, 128], [128, DC]]), S.sld, slow=True)
    LB2 = sb("LB2", (128, DC), BF16, p0s)
    sch.dma("gpsimd", LB2[:], AP(ln2_b, 0, [[1, 128], [128, DC]]), S.sw0, slow=True)
    ev_sld0 = sch.val(S.sld)

    def wload(dst, srct, douts):
        pat = [[douts, 128], [128 * douts, DC], [1, douts]]
        return sch.dma("gpsimd", dst[:], AP(srct, 0, pat), S.sw0)

    wload(WD, dec_w, H)
    ev_w_all = sch.val(S.sw0)

    PF = [ps(f"PF{i}", (1, 512), F32, stack=p0s) for i in range(FFN // 512)]
    TB0 = sb("TB0", (1, FFN), F32, p0s)
    BR0 = sb("BR0", (1, FFN), F32, p0s)
    sch.wait("tensor", ev_w_all)
    sch.wait("tensor", ev_sld0)

    def bias_fold(W, b_t, cdram, douts):
        evv = None
        ev_br = sch.dma("sync", BR0[0:1, 0:douts],
                        AP(b_t, 0, [[0, 1], [1, douts]]), S.sld)
        sch.wait("vector", ev_br)
        sch.wait("vector", sch.val(S.sst))
        for gi in range((douts + 511) // 512):
            nn_ = min(512, douts - gi * 512)
            evm = None
            for dc in range(DC):
                def mk(e, dc=dc, gi=gi, nn_=nn_, W=W, douts=douts):
                    return e.matmul(PF[0][0:1, 0:nn_], LB1[:, dc:dc + 1],
                                    W[:, dc * douts + gi * 512:
                                      dc * douts + gi * 512 + nn_],
                                    start=(dc == 0), stop=(dc == DC - 1))
                evm = sch.step("tensor", mk, S.spe if dc == DC - 1 else None)
            sch.wait("vector", evm)
            def mkv(e, gi=gi, nn_=nn_):
                return e.tensor_tensor(TB0[0:1, gi * 512: gi * 512 + nn_],
                                       PF[0][0:1, 0:nn_],
                                       BR0[0:1, gi * 512: gi * 512 + nn_],
                                       ALU.add)
            evv = sch.step("vector", mkv, S.sdv)
            sch.wait("tensor", evv)
        sch.wait("sync", evv)
        return sch.dma("sync", AP(cdram, 0, [[0, 1], [1, douts]]),
                       TB0[0:1, 0:douts], S.sst)

    evfd = bias_fold(WD, dec_b, CDB, H)

    sch.wait("sync", evfd)
    ev_cd = sch.dma("sync", NCD[:], AP(CDB, 0, [[1, H], [1, 1]]), S.sld)
    sch.wait("vector", ev_cd)
    sch.step("vector", lambda e: e.tensor_scalar_mul(NCD[:], NCD[:], -1.0), S.sdv)

    for W, douts in ((WD, H),):
        for dc in range(DC):
            def mks(e, W=W, dc=dc, douts=douts):
                return e.tensor_scalar_mul(W[:, dc * douts:(dc + 1) * douts],
                                           W[:, dc * douts:(dc + 1) * douts],
                                           LW1[:, dc:dc + 1])
            sch.step("vector", mks, S.sdv)

    sch.step("vector", lambda e: e.memset(FA[64:128, :], 1.0))
    sch.step("vector", lambda e: e.memset(FB[0:64, :], 1.0), S.sdv)

    FST = [sb(f"FST{i}", (128, FFN), BF16, p0s) for i in range(2)]
    evm = None
    for dc in range(DC):
        par = dc % 2
        if dc >= 2 and evm is not None:
            sch.wait("gpsimd", evm)
        evl = sch.dma("gpsimd", FST[par][:],
                      AP(f1_w, dc * 128 * FFN, [[FFN, 128], [1, FFN]]),
                      (S.sw0, S.sw1)[par])
        sch.wait("tensor", evl)
        for gi in range(FFN // 512):
            def mk(e, dc=dc, gi=gi, par=par):
                return e.matmul(PF[gi][0:1, :], LB2[:, dc:dc + 1],
                                FST[par][:, gi * 512:(gi + 1) * 512],
                                start=(dc == 0), stop=(dc == DC - 1))
            evm = sch.step("tensor", mk,
                           S.spe if gi == FFN // 512 - 1 else None)
    ev_br2 = sch.dma("sync", BR0[0:1, :], AP(f1_b, 0, [[0, 1], [1, FFN]]),
                     S.sld)
    sch.wait("vector", evm)
    sch.wait("vector", ev_br2)
    sch.wait("vector", sch.val(S.sst))
    evv2 = None
    for gi in range(FFN // 512):
        def mkv(e, gi=gi):
            return e.tensor_tensor(TB0[0:1, gi * 512:(gi + 1) * 512],
                                   PF[gi][0:1, :],
                                   BR0[0:1, gi * 512:(gi + 1) * 512],
                                   ALU.add)
        evv2 = sch.step("vector", mkv, S.sdv)
    sch.wait("sync", evv2)
    ev_cf = sch.dma("sync", AP(CF1B, 0, [[0, 1], [1, FFN]]), TB0[0:1, :], S.sst)
    sch.wait("sync", ev_cf)
    sch.dma("sync", CF1[:], AP(CF1B, 0, [[1, 128], [128, FC]]), S.sld, slow=True)

    sch.barrier([sch.val(S.sdv), sch.val(S.sld), sch.val(S.spe)])
    p0s.close()
    if stop_after == "p0":
        return _early_out()

    p1s = ExitStack()
    WK = sb("WK", (128, DC * D), BF16, p1s)
    WV = sb("WV", (128, DC * D), BF16, p1s)
    CKb = sb("CKb", (128, D), F32, p1s)
    CVb = sb("CVb", (128, D), F32, p1s)
    TB1 = sb("TB1", (1, D), F32, p1s)
    BR1 = sb("BR1", (1, D), F32, p1s)
    XT = [sb(f"XT{i}", (128, D), F32, p1s) for i in range(2)]
    XNT = [sb(f"XNT{i}", (128, D), F32, p1s) for i in range(2)]
    ZT = [sb(f"ZT{i}", (128, D), BF16, p1s) for i in range(2)]
    ZF = [sb(f"ZF{i}", (128, D), BF16, p1s) for i in range(2)]
    XNBT_ = sb("XNBT", (128, D), BF16, p1s)
    XNBT = [XNBT_, XNBT_]
    SQ = [sb(f"SQ{i}", (128, D), F32, p1s) for i in range(2)]
    ST1 = [sb(f"ST1{i}", (128, 8), F32, p1s) for i in range(2)]
    PHT = [sb(f"PHT{i}", (128, D), F32, p1s) for i in range(2)]
    PHE = [sb(f"PHE{i}", (128, D), F32, p1s) for i in range(2)]
    PHB = [sb(f"PHB{i}", (128, D), BF16, p1s) for i in range(2)]
    VT = [sb(f"VT{i}", (128, HKV), BF16, p1s) for i in range(2)]
    GSTORE = sb("GSTORE", (128, 2 * g.n_groups * H * (KV + 1)), BF16, p1s)
    GSW = 2 * g.n_groups * H * (KV + 1)
    LAM = [sb(f"LAM{i}", (H, C), F32, p1s) for i in range(2)]
    LLF = [sb(f"LLF{i}", (H, C), F32, p1s) for i in range(2)]
    LF = [sb(f"LF{i}", (H, C), F32, p1s) for i in range(2)]
    ZROF = sb("ZROF", (H, C), F32, p1s)
    E1F = [sb(f"E1F{i}", (H, C), F32, p1s) for i in range(2)]
    E2F = [sb(f"E2F{i}", (H, C), F32, p1s) for i in range(2)]
    ETT = [sb(f"ETT{i}", (128, 48), F32, p1s) for i in range(2)]
    GR1 = [sb(f"GR1{i}", (1, H), F32, p1s) for i in range(2)]
    GB = [sb(f"GBp{i}", (128, H), F32, p1s) for i in range(2)]
    KB = [sb(f"KBp{i}", (128, H * 128), BF16, p1s) for i in range(2)]
    UW = [sb(f"UWp{i}", (128, HKV), F32, p1s) for i in range(2)]
    TTMP = sb("TTMP", (128, HKV), F32, p1s)

    PTZ = ps("PTZ", (128, 1024), BF16, stack=p1s)
    PKV = ps("PKV", (128, 512), F32, stack=p1s)
    PKV2 = ps("PKV2", (128, 512), F32, stack=p1s)
    PDEC = ps("PDEC", (H, C), F32, stack=p1s)
    PUW1 = ps("PUW1", (128, 7 * KV), F32, stack=p1s)
    PUW2 = ps("PUW2", (128, 7 * KV), F32, stack=p1s)
    PUW3 = ps("PUW3", (128, 2 * KV), F32, stack=p1s)
    PMIS = ps("PMIS", (128, 64), F32, stack=p1s)

    def ln_affine(x_ap, out_ap, SQ_, ST_, tmp_ap=None):
        sch.step("vector",
                 lambda e: e.tensor_scalar(SQ_[:], x_ap, 1.0, None, ALU.mult,
                                           ALU.add, accum_out=ST_[:, 0:1]),
                 S.sdv)
        sch.step("vector",
                 lambda e: e.scalar_tensor_tensor(SQ_[:], x_ap, 1.0, x_ap,
                                                  ALU.mult, ALU.mult,
                                                  accum_out=ST_[:, 1:2]), S.sdv)
        sch.step("vector", lambda e: e.tensor_scalar_mul(ST_[:, 2:3], ST_[:, 0:1],
                                                         inv_d))
        sch.step("vector", lambda e: e.tensor_tensor(ST_[:, 3:4], ST_[:, 2:3],
                                                     ST_[:, 2:3], ALU.mult))
        sch.step("vector",
                 lambda e: e.scalar_tensor_tensor(ST_[:, 3:4], ST_[:, 1:2], inv_d,
                                                  ST_[:, 3:4], ALU.mult,
                                                  ALU.subtract))
        sch.step("vector", lambda e: e.tensor_scalar_add(ST_[:, 3:4],
                                                         ST_[:, 3:4], LN_EPS))
        ev3 = sch.step("vector",
                       lambda e: e.tensor_scalar_mul(ST_[:, 5:6], ST_[:, 2:3],
                                                     -1.0), S.sdv)
        sch.wait("scalar", ev3)
        sch.step("scalar", lambda e: e.activation(ST_[:, 4:5], ST_[:, 3:4],
                                                  AF.Ln))
        sch.step("scalar", lambda e: e.activation(ST_[:, 4:5], ST_[:, 4:5],
                                                  AF.Exp, scale=-0.5))
        t_ap = tmp_ap if tmp_ap is not None else SQ_[:]
        sch.step("scalar", lambda e: e.activation(t_ap, x_ap, AF.Identity,
                                                  bias=ST_[:, 5:6]))
        return sch.step("scalar",
                        lambda e: e.activation(out_ap, t_ap, AF.Identity,
                                               scale=ST_[:, 4:5]), S.sac)

    EV = {}

    def p1_stageA(ci, par, x_t, store):
        prev = EV.get(("Aend", ci - 2))
        if prev is not None:
            sch.wait("sync", prev)
        ev_x = sch.dma("sync", XT[par][:],
                       AP(x_t, ci * C * D, [[D, 128], [1, D]]),
                       (S.sx0, S.sx1)[par])
        sch.wait("vector", ev_x)
        sch.wait("scalar", ev_x)
        pz = EV.get(("zf", ci - 2))
        if pz is not None:
            sch.wait("scalar", pz)
        if store:
            sch.wait("scalar", sch.val(S.sst))
            sch.wait("vector", sch.val(S.sst))
            sch.wait("scalar", sch.val(S.sgp))
        ev_xn = ln_affine(x_tile_ap(par), XNT[par][:], SQ[par], ST1[par])
        sch.wait("vector", ev_xn)
        ev_z = ln_affine(XNT[par][:], ZT[par][:], SQ[par], ST1[par])
        if store:
            sch.wait("sync", ev_xn)
            sch.dma("sync", AP(XN32, ci * C * D, [[D, 128], [1, D]]),
                    XNT[par][:], S.sst)
            ev_xb = sch.step("scalar",
                             lambda e, par=par: e.activation(
                                 XNBT[par][:], XNT[par][:], AF.Identity),
                             S.sac)
            sch.wait("gpsimd", ev_xb)
            sch.dma("gpsimd", AP(XNB, ci * C * D, [[D, 128], [1, D]]),
                    XNBT[par][:], S.sgp)
        EV[("x", ci)] = ev_x
        EV[("z", ci)] = ev_z
        EV[("Aend", ci)] = ev_z

    def x_tile_ap(par):
        return XT[par][:]

    def p1_stageB(ci, par, store):
        ev_z = EV[("z", ci)]
        sch.wait("tensor", ev_z)
        pz = EV.get(("zf", ci - 1))
        if pz is not None:
            sch.wait("tensor", pz)
        evt = None
        for dc in range(DC):
            def mkt(e, dc=dc, par=par):
                return e.transpose(PTZ[:, dc * 128:(dc + 1) * 128],
                                   ZT[par][:, dc * 128:(dc + 1) * 128],
                                   IDENT[:])
            evt = sch.step("tensor", mkt, S.spe if dc == DC - 1 else None)
        pm = EV.get(("mmlast", ci - 2))
        if pm is not None:
            sch.wait("vector", pm)
        sch.wait("vector", evt)
        ev_zf = sch.step("vector",
                         lambda e, par=par: e.tensor_copy(ZF[par][:], PTZ[:]),
                         S.sdv)
        EV[("zf", ci)] = ev_zf
        if store:
            sch.wait("sync", ev_zf)
            sch.dma("sync",
                    AP(ZFMs, ci * C, [[g.t_own, 128], [128 * g.t_own, DC],
                                      [1, C]]),
                    ZF[par][:], S.sst)

        pc = EV.get(("kb", ci - 2))
        if pc is not None:
            sch.wait("vector", pc)
        pu = EV.get(("uwmm", ci - 2))
        if pu is not None:
            sch.wait("vector", pu)

        sch.wait("tensor", ev_zf)
        ev_phik = None
        for gi in range(2):
            bank = (PKV, PKV2)[gi]
            evm = None
            for dc in range(DC):
                def mk(e, dc=dc, gi=gi, bank=bank, par=par):
                    return e.matmul(bank[:, :],
                                    ZF[par][:, dc * 128:(dc + 1) * 128],
                                    WK[:, dc * D + gi * 512:
                                       dc * D + gi * 512 + 512],
                                    start=(dc == 0), stop=(dc == DC - 1))
                evm = sch.step("tensor", mk, S.spe if dc == DC - 1 else None)
            sch.wait("vector", evm)
            def mk1(e, gi=gi, bank=bank, par=par):
                return e.tensor_tensor(PHT[par][:, gi * 512:(gi + 1) * 512],
                                       bank[:, :],
                                       CKb[:, gi * 512:(gi + 1) * 512], ALU.add)
            ev1 = sch.step("vector", mk1, S.sdv)
            sch.wait("tensor", ev1)
            sch.wait("scalar", ev1)
            def mk2(e, gi=gi, par=par):
                return e.activation(PHE[par][:, gi * 512:(gi + 1) * 512],
                                    PHT[par][:, gi * 512:(gi + 1) * 512],
                                    AF.Exp)
            ev2 = sch.step("scalar", mk2, S.sac)
            def mk4(e, gi=gi, par=par):
                return e.activation(PHT[par][:, gi * 512:(gi + 1) * 512],
                                    PHT[par][:, gi * 512:(gi + 1) * 512],
                                    AF.Relu)
            ev2b = sch.step("scalar", mk4, S.sac)
            sch.wait("vector", ev2b)
            def mk3(e, gi=gi, par=par):
                return e.tensor_scalar_min(
                    PHE[par][:, gi * 512:(gi + 1) * 512],
                    PHE[par][:, gi * 512:(gi + 1) * 512], 1.0)
            sch.step("vector", mk3)
            def mk5(e, gi=gi, par=par):
                return e.tensor_tensor(
                    AP(PHB[par], gi * 512, [[D, 128], [1, 512]]),
                    PHE[par][:, gi * 512:(gi + 1) * 512],
                    PHT[par][:, gi * 512:(gi + 1) * 512], ALU.add)
            ev_phik = sch.step("vector", mk5, S.sdv)

        for gi in range(2):
            bank = (PKV, PKV2)[gi]
            evm = None
            for dc in range(DC):
                def mk(e, dc=dc, gi=gi, bank=bank, par=par):
                    return e.matmul(bank[:, :],
                                    ZF[par][:, dc * 128:(dc + 1) * 128],
                                    WV[:, dc * D + gi * 512:
                                       dc * D + gi * 512 + 512],
                                    start=(dc == 0), stop=(dc == DC - 1))
                evm = sch.step("tensor", mk, S.spe if dc == DC - 1 else None)
            sch.wait("vector", evm)
            def mkv(e, gi=gi, bank=bank, par=par):
                dst = AP(VT[par], gi * 8 * KV, [[HKV, 128], [KV, 8], [1, KD]])
                srcp = AP(bank, 0, [[512, 128], [KD, 8], [1, KD]])
                bias = AP(CVb, gi * 512, [[D, 128], [KD, 8], [1, KD]])
                return e.tensor_tensor(dst, srcp, bias, ALU.add)
            ev_v = sch.step("vector", mkv, S.sdv)
            sch.wait("tensor", ev_v)

        if store:
            sch.wait("sync", ev_phik)
            sch.wait("sync", sch.val(S.sdv))
            sch.dma("sync", AP(PHIKs, ci * 128 * D, [[D, 128], [1, D]]),
                    PHB[par][:], S.sst)
            sch.dma("sync", AP(VREs, ci * 128 * HKV, [[HKV, 128], [1, HKV]]),
                    VT[par][:], S.sst)

        evm = None
        for dc in range(DC):
            def mk(e, dc=dc, par=par):
                return e.matmul(PDEC[:, :], WD[:, dc * H:(dc + 1) * H],
                                ZF[par][:, dc * 128:(dc + 1) * 128],
                                start=(dc == 0), stop=(dc == DC - 1))
            evm = sch.step("tensor", mk, S.spe if dc == DC - 1 else None)
        EV[("mmlast", ci)] = evm
        sch.wait("scalar", evm)
        ev1 = sch.step("scalar",
                       lambda e, par=par: e.activation(LAM[par][:], PDEC[:, :],
                                                       AF.Exp, scale=-1.0,
                                                       bias=NCD[:]), S.sac)
        sch.wait("vector", ev1)
        sch.wait("tensor", ev1)
        sch.step("vector", lambda e, par=par: e.tensor_scalar_add(
            LAM[par][:], LAM[par][:], 1.0))
        sch.step("vector", lambda e, par=par: e.reciprocal(LAM[par][:],
                                                           LAM[par][:]))
        ev2 = sch.step("vector",
                       lambda e, par=par: e.tensor_scalar_max(
                           LAM[par][:], LAM[par][:], LAM_MIN), S.sdv)
        sch.wait("scalar", ev2)
        ev3 = sch.step("scalar",
                       lambda e, par=par: e.activation(LLF[par][:], LAM[par][:],
                                                       AF.Ln), S.sac)
        sch.wait("vector", ev3)
        ev4 = sch.step("vector",
                       lambda e, par=par: e.tensor_tensor_scan(
                           LF[par][:], LLF[par][:], ZROF[:], 0.0,
                           ALU.add, ALU.add), S.sdv)
        sch.wait("scalar", ev4)
        if store:
            pe12 = EV.get(("e12row", ci - 2))
            if pe12 is not None:
                sch.wait("scalar", pe12)
        ev5 = sch.step("scalar",
                       lambda e, par=par: e.activation(
                           E1F[par][:], LF[par][:], AF.Exp, scale=-1.0,
                           bias=LF[par][:, C - 1:C]), S.sac)
        ev6 = sch.step("scalar",
                       lambda e, par=par: e.activation(E2F[par][:], LF[par][:],
                                                       AF.Exp), S.sac)
        if store:
            sch.wait("gpsimd", ev6)
            sch.dma("gpsimd", AP(E12ROW, ci * 2 * H * C, [[0, 1], [1, H * C]]),
                    E1F[par][:, :], S.sgp)
            ev_e12 = sch.dma("gpsimd",
                             AP(E12ROW, ci * 2 * H * C + H * C,
                                [[0, 1], [1, H * C]]),
                             E2F[par][:, :], S.sgp)
            EV[("e12row", ci)] = ev_e12
        sch.wait("tensor", ev6)
        pg = EV.get(("gb", ci - 1))
        if pg is not None:
            sch.wait("tensor", pg)
        ev7a = sch.step("tensor", lambda e, par=par: e.transpose(
            PMIS[:, 0:H], E1F[par][:], IDF[0:H, 0:H]), S.spe)
        ev7b = sch.step("tensor", lambda e, par=par: e.transpose(
            PMIS[:, H:2 * H], E2F[par][:], IDF[0:H, 0:H]), S.spe)
        ev7c = sch.step("tensor", lambda e, par=par: e.transpose(
            PMIS[:, 2 * H:3 * H], LF[par][:], IDF[0:H, 0:H]), S.spe)
        sch.wait("vector", ev7c)
        if store:
            def mke(e, ci=ci):
                return e.tensor_copy(
                    AP(E1T, ci * H, [[EROW, 128], [1, H]]), PMIS[:, 0:H])
            sch.step("vector", mke)
            def mke2(e, ci=ci):
                return e.tensor_copy(
                    AP(E2T, ci * H, [[EROW, 128], [1, H]]), PMIS[:, H:2 * H])
            sch.step("vector", mke2)
            def mke3(e, ci=ci):
                return e.tensor_copy(
                    AP(LTM, ci * H, [[EROW, 128], [1, H]]), PMIS[:, 2 * H:3 * H])
            ev9 = sch.step("vector", mke3, S.sdv)
            sch.wait("sync", ev4)
            sch.dma("sync", AP(LROW, ci * H * C, [[0, 1], [1, H * C]]),
                    LF[par][:, :], S.sst)
        else:
            ev9 = sch.step("vector",
                           lambda e, par=par: e.tensor_copy(
                               ETT[par][:, 0:3 * H], PMIS[:, 0:3 * H]), S.sdv)
        EV[("e9", ci)] = ev9

        ev10 = sch.step("tensor",
                        lambda e, par=par: e.transpose(
                            PMIS[0:1, 3 * H:3 * H + H],
                            E2F[par][:, C - 1:C], IDF[0:H, 0:H]), S.spe)
        sch.wait("vector", ev10)
        ev11 = sch.step("vector",
                        lambda e, par=par: e.tensor_copy(
                            GR1[par][:], PMIS[0:1, 3 * H:3 * H + H]), S.sdv)
        if store:
            def mkg(e, ci=ci, par=par):
                return e.tensor_copy(GROW[0:1, ci * H:(ci + 1) * H], GR1[par][:])
            sch.step("vector", mkg)
        sch.wait("tensor", ev11)
        ev12 = sch.step("tensor",
                        lambda e, par=par: e.matmul(
                            PMIS[:, 3 * H:4 * H], ONES1F[0:1, :],
                            GR1[par][:], start=True, stop=True), S.spe)
        sch.wait("vector", ev12)
        ev_gb = sch.step("vector",
                         lambda e, par=par: e.tensor_copy(
                             GB[par][:], PMIS[:, 3 * H:4 * H]), S.sdv)
        sch.wait("tensor", ev_gb)
        EV[("gb", ci)] = ev_gb
        EV[("phik", ci)] = ev_phik

    def p1_stageC(ci, par, store, part):
        if store:
            e1_t, e1_off, e1_row = E1T, ci * H, EROW
            e2_t, e2_off, e2_row = E2T, ci * H, EROW
        else:
            e1_t, e1_off, e1_row = ETT[par], 0, 48
            e2_t, e2_off, e2_row = ETT[par], H, 48
        def mkk1(e, par=par, e1_t=e1_t, e1_off=e1_off, e1_row=e1_row):
            dst = AP(KB[par], 0, [[H * 128, 128], [128, H], [1, KD]])
            srcp = AP(PHB[par], 0, [[D, 128], [KD, H], [1, KD]])
            fac = AP(e1_t, e1_off, [[e1_row, 128], [1, H], [0, KD]])
            return e.tensor_tensor(dst, srcp, fac, ALU.mult)
        sch.step("vector", mkk1)
        def mkk2(e, par=par, e2_t=e2_t, e2_off=e2_off, e2_row=e2_row):
            dst = AP(KB[par], KD, [[H * 128, 128], [128, H], [1, KD]])
            srcp = AP(PHB[par], 0, [[D, 128], [KD, H], [1, KD]])
            fac = AP(e2_t, e2_off, [[e2_row, 128], [1, H], [0, KD]])
            return e.tensor_tensor(dst, srcp, fac, ALU.mult)
        evk2 = sch.step("vector", mkk2, S.sdv)
        EV[("kb", ci)] = evk2
        sch.wait("tensor", evk2)
        pe_prev = EV.get(("uwev", ci - 1))
        if pe_prev is not None:
            sch.wait("tensor", pe_prev)
        uw_banks = [(PUW1, 0, 7), (PUW2, 7, 7), (PUW3, 14, 2)]
        ev_uw = None
        for bank, h0, nh in uw_banks:
            for hh in range(nh):
                h = h0 + hh
                def mk(e, h=h, hh=hh, bank=bank, par=par):
                    rhs = AP(VT[par], h * KV, [[HKV, 128], [1, KV]])
                    return e.matmul(bank[:, hh * KV:(hh + 1) * KV],
                                    KB[par][:, h * 128:(h + 1) * 128], rhs,
                                    start=True, stop=True)
                ev_uw = sch.step("tensor", mk, S.spe if h == H - 1 else None)
        EV[("uwmm", ci)] = ev_uw
        sch.wait("scalar", ev_uw)
        sch.step("scalar", lambda e, par=par: e.activation(
            UW[par][:, 0:7 * KV], PUW1[:, :], AF.Identity))
        sch.step("scalar", lambda e, par=par: e.activation(
            UW[par][:, 7 * KV:14 * KV], PUW2[:, :], AF.Identity))
        evc = sch.step("scalar", lambda e, par=par: e.activation(
            UW[par][:, 14 * KV:16 * KV], PUW3[:, :], AF.Identity), S.sac)
        EV[("uwev", ci)] = evc
        sch.wait("vector", evc)
        sch.wait("vector", EV[("gb", ci)])

        sch.step("vector", lambda e, par=par: e.tensor_copy(FA[0:64, :],
                                                            GB[par][0:64, :]))
        sch.step("vector", lambda e: e.tensor_copy(FB[64:128, :],
                                                   GAMR[64:128, :]))
        def mk_t(e, par=par):
            fac = AP(FB, 0, [[H, 128], [1, H], [0, KV]])
            return e.tensor_tensor(TTMP[:, :], UW[par][:, :], fac, ALU.mult)
        sch.step("vector", mk_t)
        def mk_s(e):
            fac = AP(FA, 0, [[H, 128], [1, H], [0, KV]])
            return e.tensor_tensor(STATE[:, :], STATE[:, :], fac, ALU.mult)
        sch.step("vector", mk_s)
        sch.step("vector", lambda e: e.tensor_tensor(STATE[:, :], STATE[:, :],
                                                     TTMP[:, :], ALU.add))
        ev_end = sch.step("vector",
                          lambda e, par=par: e.tensor_tensor(
                              GAMR[64:128, :], GAMR[64:128, :],
                              GB[par][64:128, :], ALU.mult), S.sdv)
        if (ci + 1) % g.group == 0:
            gidx = ci // g.group
            base = (0 if part == "own" else g.n_groups) + gidx
            off = base * (H * (KV + 1))
            def mg1(e, off=off):
                return e.tensor_copy(GSTORE[64:128, off:off + HKV],
                                     STATE[64:128, :])
            sch.step("vector", mg1)
            def mg2(e, off=off):
                return e.tensor_copy(
                    GSTORE[64:128, off + HKV:off + HKV + H], GAMR[64:128, :])
            sch.step("vector", mg2)
            sch.step("vector", lambda e: e.memset(STATE[64:128, :], 0.0))
            ev_end = sch.step("vector",
                              lambda e: e.memset(GAMR[64:128, :], 1.0), S.sdv)
        EV[("Cend", ci)] = ev_end

    def pass1_part(part, x_t, store):
        for _eng in ("vector", "scalar", "tensor"):
            sch.wait(_eng, sch.val(S.spe))
            sch.wait(_eng, sch.val(S.sdv))
            sch.wait(_eng, sch.val(S.sac))
        sch.wait("vector", sch.val(S.sst))
        sch.step("vector", lambda e: e.memset(VT[0][:], 1.0))
        sch.step("vector", lambda e: e.memset(VT[1][:], 1.0))
        sch.step("vector", lambda e: e.memset(STATE[:], 0.0))
        sch.step("vector", lambda e: e.memset(GAMR[64:128, :], 1.0))
        sch.step("vector", lambda e: e.memset(ZROF[:], 0.0), S.sdv)
        EV.clear()
        n = g.nc_own
        for i in range(n + 2):
            if i < n:
                p1_stageA(i, i % 2, x_t, store)
            if 0 <= i - 1 < n:
                p1_stageB(i - 1, (i - 1) % 2, store)
            if 0 <= i - 2 < n:
                p1_stageC(i - 2, (i - 2) % 2, store, part)

    ev_wk = wload(WK, k_w, D)
    ev_wv = wload(WV, v_w, D)
    sch.wait("tensor", ev_wv)

    def bias_fold2(W, b_t, cdram, ckbuf):
        ev_br = sch.dma("sync", BR1[0:1, :], AP(b_t, 0, [[0, 1], [1, D]]),
                        S.sld)
        sch.wait("vector", ev_br)
        evv = None
        for gi in range(2):
            evm = None
            for dc in range(DC):
                def mk(e, dc=dc, gi=gi, W=W):
                    return e.matmul(PKV[0:1, :], LB1[:, dc:dc + 1],
                                    W[:, dc * D + gi * 512:
                                      dc * D + gi * 512 + 512],
                                    start=(dc == 0), stop=(dc == DC - 1))
                evm = sch.step("tensor", mk, S.spe if dc == DC - 1 else None)
            sch.wait("vector", evm)
            def mkv(e, gi=gi):
                return e.tensor_tensor(TB1[0:1, gi * 512:(gi + 1) * 512],
                                       PKV[0:1, :],
                                       BR1[0:1, gi * 512:(gi + 1) * 512],
                                       ALU.add)
            evv = sch.step("vector", mkv, S.sdv)
            sch.wait("tensor", evv)
        sch.wait("sync", evv)
        evs = sch.dma("sync", AP(cdram, 0, [[0, 1], [1, D]]), TB1[0:1, :],
                      S.sst)
        sch.wait("sync", evs)
        evl = sch.dma("sync", ckbuf[:], AP(cdram, 0, [[0, 128], [1, D]]),
                      S.sld)
        return evl

    ev_ck = bias_fold2(WK, k_b, CKB, CKb)
    ev_cv = bias_fold2(WV, v_b, CVB, CVb)
    for W in (WK, WV):
        for dc in range(DC):
            def mks(e, W=W, dc=dc):
                return e.tensor_scalar_mul(W[:, dc * D:(dc + 1) * D],
                                           W[:, dc * D:(dc + 1) * D],
                                           LW1[:, dc:dc + 1])
            sch.step("vector", mks, S.sdv)
    sch.barrier([sch.val(S.sdv), sch.val(S.sld)])

    pass1_part("own", x_own, True)

    sch.step("vector", lambda e: e.memset(TTMP[64:128, :], 0.0))
    def roll_groups_own():
        for gidx in reversed(range(g.n_groups)):
            off = gidx * (H * (KV + 1))
            def mk1(e, off=off):
                fac = AP(GSTORE, 64 * GSW + off + HKV,
                         [[GSW, 64], [1, H], [0, KV]])
                return e.tensor_tensor(TTMP[64:128, :], TTMP[64:128, :], fac,
                                       ALU.mult)
            sch.step("vector", mk1)
            def mk2(e, off=off):
                return e.tensor_tensor(TTMP[64:128, :], TTMP[64:128, :],
                                       GSTORE[64:128, off:off + HKV], ALU.add)
            sch.step("vector", mk2)
    roll_groups_own()
    ev_blob = sch.step("vector", lambda e: e.engine_nop(), S.sdv)
    sch.wait("gpsimd", ev_blob)
    sch.dma("gpsimd", AP(EXCI, 0, [[HKV, 64], [1, HKV]]), STATE[0:64, :],
            S.sgp)
    ev_exp = sch.dma("gpsimd", AP(EXCI, 64 * HKV, [[HKV, 64], [1, HKV]]),
                     TTMP[64:128, :], S.sgp)
    sch.ops["gpsimd"].append(lambda e, v=sch.ctr["sgp"]: e.wait_ge(S.sgp, v))
    groups = [[2 * i, 2 * i + 1] for i in range(max(1, n_cores // 2))]
    ev_cc = sch.step(
        "gpsimd",
        lambda e: e.collective_compute(
            "AllGather", ALU.bypass, replica_groups=groups,
            ins=[EXCI.ap().opt()], outs=[EXCO.ap().opt()]), S.scc)
    sch.wait("sync", ev_cc)
    sch.dma("sync", TTMP[0:64, :], AP(EXCO, 0, [[HKV, 64], [1, HKV]]), S.sld)
    ev_im = sch.dma("sync", TTMP[64:128, :],
                    AP(EXCO, 128 * HKV + 64 * HKV, [[HKV, 64], [1, HKV]]),
                    S.sld)
    sch.wait("vector", ev_im)
    sch.step("vector",
             lambda e: e.tensor_scalar_mul(STATE[0:64, :], TTMP[0:64, :],
                                           HM[0:64, 0:1]))
    sch.step("vector", lambda e: e.tensor_scalar_mul(TTMP[64:128, :],
                                                     TTMP[64:128, :],
                                                     HM[64:128, 1:2]))
    for gidx in reversed(range(g.n_groups)):
        def mka(e, gidx=gidx):
            return e.tensor_copy(RANCH[64:128, gidx * HKV:(gidx + 1) * HKV],
                                 TTMP[64:128, :])
        sch.step("vector", mka)
        if gidx > 0:
            off = gidx * (H * (KV + 1))
            def mk1(e, off=off):
                fac = AP(GSTORE, 64 * GSW + off + HKV,
                         [[GSW, 64], [1, H], [0, KV]])
                return e.tensor_tensor(TTMP[64:128, :], TTMP[64:128, :], fac,
                                       ALU.mult)
            sch.step("vector", mk1)
            def mk2(e, off=off):
                return e.tensor_tensor(TTMP[64:128, :], TTMP[64:128, :],
                                       GSTORE[64:128, off:off + HKV], ALU.add)
            sch.step("vector", mk2)
    ev_p1 = sch.step("vector", lambda e: e.engine_nop(), S.sdv)
    sch.barrier([ev_p1, sch.val(S.sst), sch.val(S.spe), sch.val(S.sac),
                 sch.val(S.sgp)])
    p1s.close()
    if stop_after == "p1":
        return _early_out()

    pas = ExitStack()
    WQ = sb("WQ", (128, DC * D), BF16, pas)
    WO = sb("WO", (128, DC * D), BF16, pas)
    CQ = sb("CQ", (128, DC), F32, pas)
    TB2 = sb("TB2", (1, D), F32, pas)
    BR2 = sb("BR2", (1, D), F32, pas)
    XT2_ = sb("XT2", (128, D), F32, pas)
    XT2 = [XT2_, XT2_]
    XN2 = [sb(f"XN2{i}", (128, D), F32, pas) for i in range(2)]
    ZF2 = [sb(f"ZF2{i}", (128, D), BF16, pas) for i in range(2)]
    LB4 = [sb(f"LB4{i}", (128, H * C), F32, pas) for i in range(2)]
    PKT = [sb(f"PKT{i}", (128, D), BF16, pas) for i in range(2)]
    PVT = [sb(f"PVT{i}", (128, HKV), BF16, pas) for i in range(2)]
    PHQ = [sb(f"PHQ{i}", (128, D), BF16, pas) for i in range(2)]
    PHQ2 = [sb(f"PHQ2{i}", (64, H * 128), BF16, pas) for i in range(2)]
    PHKF2 = [sb(f"PHKF2{i}", (64, H * 128), BF16, pas) for i in range(2)]
    ATT = [sb(f"ATT{i}", (128, D), BF16, pas) for i in range(2)]
    TQ = sb("TQ", (128, D), BF16, pas)
    TE = sb("TE", (128, D), BF16, pas)
    DT = sb("DT", (128, 512), F32, pas)
    ABB = [sb(f"ABB{i}", (128, 512), BF16, pas) for i in range(2)]
    E12B_ = sb("E12B", (64, 2 * H * C), BF16, pas)
    E12B = [E12B_, E12B_]
    PHQA2 = sb("PHQA2", (64, H * 128), BF16, pas)
    PHQB2 = sb("PHQB2", (64, H * 128), BF16, pas)
    RD8 = sb("RD8", (128, 4), F32, pas)
    ATF = sb("ATF", (128, D), BF16, pas)
    YT = sb("YT", (128, D), F32, pas)
    Z2T = sb("Z2T", (128, D), BF16, pas)
    ST2 = sb("ST2", (128, 8), F32, pas)
    RBUF = sb("RBUF", (128, g.group * HKV), BF16, pas)
    KB2 = sb("KB2", (128, H * 128), BF16, pas)
    SBF = sb("SBF", (128, HKV), BF16, pas)
    GKT = sb("GKT", (128, D), BF16, pas)
    GVT = sb("GVT", (128, HKV), BF16, pas)
    UW2 = sb("UW2", (128, HKV), F32, pas)
    GB2 = sb("GB2", (128, H), F32, pas)
    GBALL = sb("GBALL", (128, g.nc_own * H), F32, pas)

    PQ = [ps(f"PQ{i}", (128, 512), F32, stack=pas) for i in range(2)]
    PTR2 = ps("PTR2", (128, 1024), BF16, stack=pas)
    PSC = ps("PSC", (128, 512), F32, stack=pas)
    PG = ps("PG", (128, 512), F32, stack=pas)
    PO = [ps(f"PO{i}", (128, 4 * KV), F32, stack=pas) for i in range(2)]

    def ln_affine2(x_ap, out_ap, junk):
        return ln_affine(x_ap, out_ap, junk, ST2, tmp_ap=x_ap)

    def gamma_bcast(ci, evgate=None):
        if evgate is not None:
            sch.wait("tensor", evgate)
        def mk(e, ci=ci):
            return e.matmul(PO[1][:, 0:H], ONES1F[0:1, :],
                            GROW[0:1, ci * H:(ci + 1) * H],
                            start=True, stop=True)
        ev = sch.step("tensor", mk, S.spe)
        sch.wait("vector", ev)
        evc = sch.step("vector", lambda e: e.tensor_copy(GB2[:], PO[1][:, 0:H]),
                       S.sdv)
        sch.wait("tensor", evc)
        return evc

    def recompute_uw(ci, want_u, want_w, kt, vt):
        def mkk1(e, ci=ci, kt=kt):
            dst = AP(KB2, 0, [[H * 128, 128], [128, H], [1, KD]])
            srcp = AP(kt, 0, [[D, 128], [KD, H], [1, KD]])
            fac = AP(E1T, ci * H, [[EROW, 128], [1, H], [0, KD]])
            return e.tensor_tensor(dst, srcp, fac, ALU.mult)
        def mkk2(e, ci=ci, kt=kt):
            dst = AP(KB2, KD, [[H * 128, 128], [128, H], [1, KD]])
            srcp = AP(kt, 0, [[D, 128], [KD, H], [1, KD]])
            fac = AP(E2T, ci * H, [[EROW, 128], [1, H], [0, KD]])
            return e.tensor_tensor(dst, srcp, fac, ALU.mult)
        if want_u:
            evk = sch.step("vector", mkk1, S.sdv if not want_w else None)
        if want_w:
            evk = sch.step("vector", mkk2, S.sdv)
        sch.wait("tensor", evk)
        banks = [(PSC, 0, 7, 0), (PG, 7, 7, 0), (PO[0], 14, 2, 0)]
        ev_uw = None
        for bank, h0, nh, co in banks:
            for hh in range(nh):
                h = h0 + hh
                def mk(e, h=h, hh=hh, bank=bank, vt=vt):
                    rhs = AP(vt, h * KV, [[HKV, 128], [1, KV]])
                    return e.matmul(bank[:, hh * KV:(hh + 1) * KV],
                                    KB2[:, h * 128:(h + 1) * 128], rhs,
                                    start=True, stop=True)
                ev_uw = sch.step("tensor", mk, S.spe if h == H - 1 else None)
        sch.wait("scalar", ev_uw)
        ev = None
        rows = []
        if want_u:
            rows.append((0, 64))
        if want_w:
            rows.append((64, 128))
        for r0, r1 in rows:
            sch.step("scalar", lambda e, r0=r0, r1=r1:
                     e.activation(UW2[r0:r1, 0:7 * KV], PSC[r0:r1, 0:7 * KV],
                                  AF.Identity))
            sch.step("scalar", lambda e, r0=r0, r1=r1:
                     e.activation(UW2[r0:r1, 7 * KV:14 * KV],
                                  PG[r0:r1, 0:7 * KV], AF.Identity))
            ev = sch.step("scalar", lambda e, r0=r0, r1=r1:
                          e.activation(UW2[r0:r1, 14 * KV:16 * KV],
                                       PO[0][r0:r1, 0:2 * KV], AF.Identity),
                          S.sac)
        sch.wait("tensor", ev)
        sch.wait("vector", ev)
        return ev

    xs = (S.sx0, S.sx1)
    ys = (S.sy0, S.sy1)
    zs = (S.sz0, S.sz1)
    EV2 = {}
    ev_sb0 = sch.step("vector",
                      lambda e: e.tensor_copy(SBF[0:64, :], STATE[0:64, :]),
                      S.sdv)
    sch.step("vector", lambda e: e.memset(KB2[:], 0.0), S.sdv)
    sch.wait("tensor", ev_sb0)

    ev_wq = wload(WQ, q_w, D)
    ev_wo = wload(WO, o_w, D)
    sch.wait("tensor", ev_wq)
    ev_br = sch.dma("sync", BR2[0:1, :], AP(q_b, 0, [[0, 1], [1, D]]), S.sld)
    sch.wait("vector", ev_br)
    evv = None
    for gi in range(2):
        evm = None
        for dc in range(DC):
            def mk(e, dc=dc, gi=gi):
                return e.matmul(PQ[0][0:1, :], LB1[:, dc:dc + 1],
                                WQ[:, dc * D + gi * 512:
                                   dc * D + gi * 512 + 512],
                                start=(dc == 0), stop=(dc == DC - 1))
            evm = sch.step("tensor", mk, S.spe if dc == DC - 1 else None)
        sch.wait("vector", evm)
        def mkv(e, gi=gi):
            return e.tensor_tensor(TB2[0:1, gi * 512:(gi + 1) * 512],
                                   PQ[0][0:1, :],
                                   BR2[0:1, gi * 512:(gi + 1) * 512], ALU.add)
        evv = sch.step("vector", mkv, S.sdv)
        sch.wait("tensor", evv)
    sch.wait("sync", evv)
    ev_cq = sch.dma("sync", AP(CQB, 0, [[0, 1], [1, D]]), TB2[0:1, :], S.sst)
    sch.wait("sync", ev_cq)
    sch.dma("sync", CQ[:], AP(CQB, 0, [[1, 128], [128, DC]]), S.sld, slow=True)
    for dc in range(DC):
        def mks(e, dc=dc):
            return e.tensor_scalar_mul(WQ[:, dc * D:(dc + 1) * D],
                                       WQ[:, dc * D:(dc + 1) * D],
                                       LW1[:, dc:dc + 1])
        sch.step("vector", mks, S.sdv)
    sch.wait("tensor", ev_wo)
    sch.barrier([sch.val(S.sdv), sch.val(S.sld)])
    NCH = g.nc_own * H
    def mkgall(e):
        return e.matmul(PG[:, 0:NCH], ONES1F[0:1, :], GROW[0:1, 0:NCH],
                        start=True, stop=True)
    ev_ga = sch.step("tensor", mkgall, S.spe)
    sch.wait("vector", ev_ga)
    ev_gac = sch.step("vector",
                      lambda e: e.tensor_copy(GBALL[:], PG[:, 0:NCH]),
                      S.sdv)
    sch.wait("tensor", ev_gac)

    def pa_stageA(ci, par):
        if ci % g.group == 0:
            prev_sr = EV2.get(("sroll", ci - 1))
            if prev_sr is not None:
                sch.wait("vector", prev_sr)
                sch.wait("gpsimd", prev_sr)
            gidx = ci // g.group
            last = min(ci + g.group, g.nc_own) - 1
            def mkan(e, gidx=gidx):
                return e.tensor_copy(STATE[64:128, :],
                                     RANCH[64:128, gidx * HKV:(gidx + 1) * HKV])
            sch.step("vector", mkan)
            def mkst(e, lc=last - ci):
                return e.tensor_copy(
                    AP(RBUF, 64 * (g.group * HKV) + lc * HKV,
                       [[g.group * HKV, 64], [1, HKV]]),
                    STATE[64:128, :])
            ev_r = sch.step("vector", mkst, S.sdv)
            sch.wait("gpsimd", ev_r)
            sch.dma("gpsimd",
                    AP(RBUF, (last - ci) * HKV, [[g.group * HKV, 64], [1, HKV]]),
                    AP(RBUF, 64 * (g.group * HKV) + (last - ci) * HKV,
                       [[g.group * HKV, 64], [1, HKV]]), S.sgp)
            ev_gload = None
            for c2 in range(last - 1, ci - 1, -1):
                sch.dma("sync", GKT[:],
                        AP(PHIKs, (c2 + 1) * 128 * D, [[D, 128], [1, D]]),
                        S.sld)
                ev_gload = sch.dma(
                    "sync", GVT[:],
                    AP(VREs, (c2 + 1) * 128 * HKV, [[HKV, 128], [1, HKV]]),
                    S.sld)
                sch.wait("vector", ev_gload)
                sch.wait("tensor", ev_gload)
                ev_w = recompute_uw(c2 + 1, False, True, GKT, GVT)
                sch.wait("vector", ev_w)
                sch.wait("sync", ev_w)
                def mkr1(e, c2=c2):
                    fac = AP(GBALL, 64 * NCH + (c2 + 1) * H,
                             [[NCH, 64], [1, H], [0, KV]])
                    return e.tensor_tensor(STATE[64:128, :], STATE[64:128, :],
                                           fac, ALU.mult)
                sch.step("vector", mkr1)
                sch.step("vector",
                         lambda e: e.tensor_tensor(STATE[64:128, :],
                                                   STATE[64:128, :],
                                                   UW2[64:128, :], ALU.add))
                def mkst2(e, lc=c2 - ci):
                    return e.tensor_copy(
                        AP(RBUF, 64 * (g.group * HKV) + lc * HKV,
                           [[g.group * HKV, 64], [1, HKV]]),
                        STATE[64:128, :])
                ev_r = sch.step("vector", mkst2, S.sdv)
                sch.wait("gpsimd", ev_r)
                sch.dma("gpsimd",
                        AP(RBUF, (c2 - ci) * HKV, [[g.group * HKV, 64], [1, HKV]]),
                        AP(RBUF, 64 * (g.group * HKV) + (c2 - ci) * HKV,
                           [[g.group * HKV, 64], [1, HKV]]), S.sgp)
            sch.wait("tensor", sch.val(S.sgp))

        pq = EV2.get(("qmm", ci - 2))
        if pq is not None:
            sch.wait("sync", pq)
        sch.dma("sync", ZF2[par][:],
                AP(ZFMs, ci * C, [[g.t_own, 128], [128 * g.t_own, DC], [1, C]]),
                zs[par])
        patt = EV2.get(("att", ci - 2))
        if patt is not None:
            sch.wait("sync", patt)
        sch.dma("sync", LB4[par][:],
                AP(LROW, ci * H * C, [[0, 128], [1, H * C]]), zs[par])
        psr = EV2.get(("sroll", ci - 2))
        if psr is not None:
            sch.wait("sync", psr)
        sch.dma("sync", PKT[par][:],
                AP(PHIKs, ci * 128 * D, [[D, 128], [1, D]]), zs[par])
        ev_z = sch.dma("sync", PVT[par][:],
                       AP(VREs, ci * 128 * HKV, [[HKV, 128], [1, HKV]]),
                       zs[par])
        EV2[("z", ci)] = ev_z

        sch.wait("tensor", ev_z)
        pqf = EV2.get(("qev", ci - 1))
        if pqf is not None:
            sch.wait("tensor", pqf)
        po = EV2.get(("oev", ci - 2))
        if po is not None:
            sch.wait("tensor", po)
        evm = None
        for do in range(DC):
            bank = PQ[do // 4]
            col = (do % 4) * 128
            for dc in range(DC):
                def mk(e, do=do, dc=dc, bank=bank, col=col, par=par):
                    return e.matmul(bank[:, col:col + 128],
                                    WQ[:, dc * D + do * 128:
                                       dc * D + do * 128 + 128],
                                    ZF2[par][:, dc * 128:(dc + 1) * 128],
                                    start=(dc == 0), stop=(dc == DC - 1))
                evm = sch.step("tensor", mk,
                               S.spe if (dc == DC - 1) else None)
        EV2[("qmm", ci)] = evm
        sch.wait("vector", evm)
        def mkb0(e):
            fac = AP(CQ, 0, [[DC, 128], [1, 4], [0, 128]])
            return e.tensor_tensor(TQ[:, 0:512], PQ[0][:, :], fac, ALU.add)
        sch.step("vector", mkb0)
        def mkb1(e):
            fac = AP(CQ, 4, [[DC, 128], [1, 4], [0, 128]])
            return e.tensor_tensor(TQ[:, 512:1024], PQ[1][:, :], fac, ALU.add)
        evq = sch.step("vector", mkb1, S.sdv)
        sch.wait("scalar", evq)
        sch.step("scalar", lambda e: e.activation(TE[:], TQ[:], AF.Exp))
        evr = sch.step("scalar", lambda e: e.activation(TQ[:], TQ[:], AF.Relu),
                       S.sac)
        sch.wait("vector", evr)
        sch.step("vector", lambda e: e.tensor_scalar_min(TE[:], TE[:], 1.0))
        def mkphi(e, par=par):
            return e.tensor_tensor(PHQ[par][:], TE[:], TQ[:], ALU.add)
        evphi = sch.step("vector", mkphi, S.sdv)
        EV2[("qev", ci)] = evphi
        def mkq2(e, par=par):
            dst = AP(PHQ2[par], 0, [[H * 128, 64], [256, 8], [1, 128]])
            srcp = AP(PHQ[par], 0, [[D, 64], [128, 8], [1, 128]])
            return e.tensor_copy(dst, srcp)
        ev_q2a = sch.step("vector", mkq2, S.sdv)
        sch.wait("gpsimd", ev_q2a)
        def mkq2b(e, par=par):
            dst = AP(PHQ2[par], 128, [[H * 128, 64], [256, 8], [1, 128]])
            srcp = AP(PHQ[par], 64 * D, [[D, 64], [128, 8], [1, 128]])
            return e.dma_start(dst, srcp)
        ev_q2b = sch.step("gpsimd", mkq2b, S.sgp, 16)
        EV2[("q2", ci)] = ev_q2b

    def pa_stageB(ci, par):
        pc2 = EV2.get(("c2", ci - 2))
        if pc2 is not None:
            sch.wait("sync", pc2)
        sch.dma("sync", XN2[par][:],
                AP(XN32, ci * C * D, [[D, 128], [1, D]]), ys[par])
        pqs = EV2.get(("qsc", ci - 1))
        if pqs is not None:
            sch.wait("sync", pqs)
        ev_y = sch.dma("sync", E12B[par][:],
                       AP(E12ROW, ci * 2 * H * C, [[0, 64], [1, 2 * H * C]]),
                       ys[par])
        EV2[("y", ci)] = ev_y
        EV2[("e12", ci)] = ev_y
        ps_ = EV2.get(("sroll", ci - 1))
        if ps_ is not None:
            sch.wait("tensor", ps_)
        ptrf = EV2.get(("atf", ci - 1))
        if ptrf is not None:
            sch.wait("tensor", ptrf)
        for rnd in range(2):
            evt = None
            for hh in range(8):
                h = rnd * 8 + hh
                def mkt(e, h=h, hh=hh, par=par):
                    off = (h // 2) * 128 + (h % 2) * 64
                    return e.transpose(
                        PTR2[0:64, hh * 128:(hh + 1) * 128],
                        AP(PKT[par], off, [[D, 128], [1, 64]]), IDENT[:])
                evt = sch.step("tensor", mkt, S.spe if hh == 7 else None)
            sch.wait("vector", evt)
            def mkev(e, rnd=rnd, par=par):
                return e.tensor_copy(PHKF2[par][0:64, rnd * 1024:(rnd + 1) * 1024],
                                     PTR2[0:64, 0:1024])
            ev_pkf = sch.step("vector", mkev, S.sdv)
            sch.wait("tensor", ev_pkf)

        sch.wait("tensor", EV2[("q2", ci)])
        sch.wait("vector", EV2[("q2", ci)])
        sch.wait("vector", EV2[("e12", ci)])
        pao = EV2.get(("abb", (ci - 1, 3)))
        if pao is not None:
            sch.wait("vector", pao)
        def mksa(e, par=par):
            return e.tensor_tensor(PHQA2[:], PHQ2[par][:],
                                   E12B[par][:, H * C:2 * H * C], ALU.mult)
        sch.step("vector", mksa)
        def mksb(e, par=par):
            return e.tensor_tensor(PHQB2[:], PHQ2[par][:],
                                   E12B[par][:, 0:H * C], ALU.mult)
        ev_qsc = sch.step("vector", mksb, S.sdv)
        EV2[("qsc", ci)] = ev_qsc
        sch.wait("tensor", ev_qsc)
        ev_combine = None
        for b4 in range(4):
            def mksub(e, b4=b4, ci=ci, par=par):
                fac = AP(LTM, ci * H + b4 * 4, [[EROW, 128], [1, 4], [0, 128]])
                return e.tensor_tensor(DT[:],
                                       LB4[par][:, b4 * 512:(b4 + 1) * 512],
                                       fac, ALU.subtract)
            ev_g = sch.step("vector", mksub, S.sdv)
            sch.wait("scalar", ev_g)
            sch.step("scalar", lambda e: e.activation(DT[:], DT[:], AF.Abs))
            ev_d = sch.step("scalar",
                            lambda e: e.activation(DT[:], DT[:], AF.Exp,
                                                   scale=-1.0), S.sac)
            ev_sc = None
            for hh in range(4):
                h = b4 * 4 + hh
                def mksc(e, h=h, hh=hh, par=par):
                    lhsT = AP(PHKF2[par], h * 128, [[H * 128, 64], [1, 128]])
                    rhs = AP(PHQ2[par], h * 128, [[H * 128, 64], [1, 128]])
                    return e.matmul(PSC[:, hh * 128:(hh + 1) * 128], lhsT,
                                    rhs, start=True, stop=True)
                ev_sc = sch.step("tensor", mksc, S.spe if hh == 3 else None)
            sch.wait("vector", ev_sc)
            sch.wait("vector", ev_d)
            pab = EV2.get(("abb", (ci, b4 - 2)))
            if pab is not None:
                sch.wait("vector", pab)
            def mka(e, b4=b4):
                return e.tensor_tensor(ABB[b4 % 2][:], PSC[:, :], DT[:],
                                       ALU.mult)
            ev_a = sch.step("vector", mka, S.sdv)
            sch.wait("tensor", ev_a)
            pdv = (EV2.get(("div", (ci, b4 - 2))) if b4 >= 2 else
                   EV2.get(("div", (ci - 1, b4 + 2))))
            if pdv is not None:
                sch.wait("tensor", pdv)
            ev_o = None
            for hh in range(4):
                h = b4 * 4 + hh
                def mko1(e, hh=hh, par=par, h=h, b4=b4):
                    rhs = AP(PVT[par], h * KV, [[HKV, 128], [1, KV]])
                    return e.matmul(PO[b4 % 2][:, hh * KV:(hh + 1) * KV],
                                    AP(ABB[b4 % 2], hh * 128,
                                       [[512, 128], [1, 128]]),
                                    rhs, start=True, stop=False)
                sch.step("tensor", mko1)
                def mko2(e, h=h, hh=hh, b4=b4):
                    lhsT = AP(PHQA2, h * 128, [[H * 128, 64], [1, 128]])
                    rhs = AP(SBF, h * KV, [[HKV, 64], [1, KV]])
                    return e.matmul(PO[b4 % 2][:, hh * KV:(hh + 1) * KV], lhsT,
                                    rhs, start=False, stop=False)
                sch.step("tensor", mko2)
                def mko3(e, h=h, hh=hh, b4=b4, lc=ci % g.group):
                    lhsT = AP(PHQB2, h * 128, [[H * 128, 64], [1, 128]])
                    rhs = AP(RBUF, lc * HKV + h * KV,
                             [[g.group * HKV, 64], [1, KV]])
                    return e.matmul(PO[b4 % 2][:, hh * KV:(hh + 1) * KV], lhsT,
                                    rhs, start=False, stop=True)
                ev_o = sch.step("tensor", mko3, S.spe if hh == 3 else None)
            EV2[("abb", (ci, b4))] = ev_o
            sch.wait("vector", ev_o)
            def mkden(e, b4=b4):
                srcp = AP(PO[b4 % 2], KD, [[4 * KV, 128], [KV, 4]])
                return e.tensor_scalar_max(RD8[:], srcp, DEN_EPS)
            sch.step("vector", mkden)
            sch.step("vector", lambda e: e.reciprocal(RD8[:], RD8[:]))
            if b4 == 0:
                paf = EV2.get(("atf", ci - 2))
                if paf is not None:
                    sch.wait("vector", paf)
            def mkdiv(e, b4=b4, par=par):
                srcp = AP(PO[b4 % 2], 0, [[4 * KV, 128], [KV, 4], [1, KD]])
                fac = AP(RD8, 0, [[4, 128], [1, 4], [0, KD]])
                dst = AP(ATT[par], b4 * 256, [[D, 128], [KD, 4], [1, KD]])
                return e.tensor_tensor(dst, srcp, fac, ALU.mult)
            ev_combine = sch.step("vector", mkdiv, S.sdv)
            EV2[("div", (ci, b4))] = ev_combine
        EV2[("att", ci)] = ev_combine
        ev_u = recompute_uw(ci, True, False, PKT[par], PVT[par])
        sch.wait("vector", ev_u)
        def mks1(e, ci=ci):
            fac = AP(GBALL, ci * H, [[NCH, 64], [1, H], [0, KV]])
            return e.tensor_tensor(STATE[0:64, :], STATE[0:64, :], fac, ALU.mult)
        sch.step("vector", mks1)
        sch.step("vector",
                 lambda e: e.tensor_tensor(STATE[0:64, :], STATE[0:64, :],
                                           UW2[0:64, :], ALU.add))
        ev_sr = sch.step("vector",
                         lambda e: e.tensor_copy(SBF[0:64, :], STATE[0:64, :]),
                         S.sdv)
        sch.wait("tensor", ev_sr)
        EV2[("sroll", ci)] = ev_sr

    def pa_stageC(ci, par):
        sch.wait("tensor", EV2[("att", ci)])
        evt = None
        for dc in range(DC):
            def mkt(e, dc=dc, par=par):
                return e.transpose(PTR2[:, dc * 128:(dc + 1) * 128],
                                   ATT[par][:, dc * 128:(dc + 1) * 128],
                                   IDENT[:])
            evt = sch.step("tensor", mkt, S.spe if dc == DC - 1 else None)
        sch.wait("vector", evt)
        ev_atf = sch.step("vector", lambda e: e.tensor_copy(ATF[:], PTR2[:]),
                          S.sdv)
        EV2[("atf", ci)] = ev_atf
        sch.wait("tensor", ev_atf)
        sch.wait("sync", sch.val(S.sst))
        ev_x = sch.dma("sync", XT2[par][:],
                       AP(x_own, ci * C * D, [[D, 128], [1, D]]), xs[par])
        sch.wait("vector", EV2[("y", ci)])
        sch.wait("vector", sch.val(S.sst))
        ev_yb = None
        for got in range(2):
            bank = PQ[got]
            evm = None
            for dc in range(DC):
                def mk(e, dc=dc, got=got, bank=bank):
                    return e.matmul(bank[:, :],
                                    ATF[:, dc * 128:(dc + 1) * 128],
                                    WO[:, dc * D + got * 512:
                                       dc * D + got * 512 + 512],
                                    start=(dc == 0), stop=(dc == DC - 1))
                evm = sch.step("tensor", mk, S.spe if dc == DC - 1 else None)
            sch.wait("vector", evm)
            def mky(e, got=got, bank=bank, par=par):
                return e.scalar_tensor_tensor(
                    YT[:, got * 512:(got + 1) * 512], bank[:, :], 1.0,
                    XN2[par][:, got * 512:(got + 1) * 512], ALU.mult, ALU.add)
            sch.step("vector", mky)
            ev_yb = sch.step("vector",
                             lambda e, got=got:
                             e.tensor_tensor(YT[:, got * 512:(got + 1) * 512],
                                             YT[:, got * 512:(got + 1) * 512],
                                             OBb[:, got * 512:(got + 1) * 512],
                                             ALU.add), S.sdv)
            sch.wait("tensor", ev_yb)
        EV2[("oev", ci)] = ev_yb
        sch.wait("vector", ev_x)
        def mkp(e, par=par):
            return e.tensor_tensor(XT2[par][:], YT[:], XT2[par][:], ALU.add)
        ev_pt = sch.step("vector", mkp, S.sdv)
        sch.wait("sync", ev_pt)
        sch.dma("sync", AP(PBs, ci * C * D, [[D, 128], [1, D]]), XT2[par][:],
                S.sst)
        sch.wait("scalar", sch.val(S.sst))
        ev_z2 = ln_affine2(YT[:], Z2T[:], XN2[par])
        sch.wait("sync", ev_z2)
        sch.dma("sync", AP(Z2B, ci * C * D, [[D, 128], [1, D]]), Z2T[:], S.sst)
        ev_c2 = sch.step("vector", lambda e: e.engine_nop(), S.sca)
        EV2[("c2", ci)] = ev_c2

    npair = g.nc_own // 2
    for j in range(npair + 2):
        for cc in (2 * (j - 2), 2 * (j - 2) + 1):
            if 0 <= cc < g.nc_own:
                pa_stageC(cc, cc % 2)
        for cc in (2 * (j - 1), 2 * (j - 1) + 1):
            if 0 <= cc < g.nc_own:
                pa_stageB(cc, cc % 2)
        for cc in (2 * j, 2 * j + 1):
            if 0 <= cc < g.nc_own:
                pa_stageA(cc, cc % 2)

    ev_pa = sch.step("vector", lambda e: e.engine_nop(), S.sdv)
    sch.barrier([ev_pa, sch.val(S.sst), sch.val(S.spe), sch.val(S.sac)])
    pas.close()
    att.close()
    if stop_after == "pa":
        return _early_out()

    def ffn_phase(Wa_t, Wb_t, bias1_fm, in_scr, bias2_bc, dst_t, dst_sem,
                  scale_lw2, tag):
        fs = ExitStack()
        mg = g.midgroup
        ntg = mg // 128
        W1S = sb(f"W1S{tag}", (128, DC * FFN), BF16, fs)
        W2S = sb(f"W2S{tag}", (128, FC * D), BF16, fs)
        HMID = sb(f"HMID{tag}", (128, FC * mg), BF16, fs)
        XFW = sb(f"XFW{tag}", (128, DC * mg), BF16, fs)
        PSL = [sb(f"PSL{tag}{i}", (128, D), F32, fs) for i in range(2)]
        OT = [sb(f"OT{tag}{i}", (128, 512), F32, fs) for i in range(2)]
        PH = [ps(f"PH{tag}{i}", (128, mg), F32, stack=fs) for i in range(2)]
        PACC = [ps(f"PACC{tag}{i}", (128, 512), F32, stack=fs) for i in range(2)]

        ev_w1s = []
        for j in range(8):
            ev = sch.dma("gpsimd",
                         AP(W1S, j * 512,
                            [[DC * FFN, 128], [FFN, DC], [1, 512]]),
                         AP(Wa_t, j * 512,
                            [[FFN, 128], [128 * FFN, DC], [1, 512]]),
                         S.sw0)
            ev_w1s.append(ev)
        ev_w2s = []
        for j in range(8):
            ev = sch.dma("gpsimd",
                         AP(W2S, j * 4 * D, [[FC * D, 128], [1, 4 * D]]),
                         AP(Wb_t, j * 4 * 128 * D,
                            [[D, 128], [128 * D, 4], [1, D]]),
                         S.sw1)
            ev_w2s.append(ev)
        if scale_lw2:
            for j in range(8):
                sch.wait("vector", ev_w1s[j])
                evf = None
                for dc in range(DC):
                    def mks(e, dc=dc, j=j):
                        sl = slice(dc * FFN + j * 512, dc * FFN + j * 512 + 512)
                        return e.tensor_scalar_mul(W1S[:, sl], W1S[:, sl],
                                                   LW2[:, dc:dc + 1])
                    evf = sch.step("vector", mks,
                                   S.sdv if dc == DC - 1 else None)
                ev_w1s[j] = evf

        ev_hold = [None, None]
        ev_accfree = [None, None]
        ev_otfree = [None, None]
        ev_pfree = [None, None]
        ev_xfree = None
        ev_hfree = None
        def load_xfw(m):
            t0 = m * mg
            ev_in = None
            for dc in range(DC):
                ev_in = sch.dmat("sync", XFW[:, dc * mg:(dc + 1) * mg],
                                 AP(in_scr, t0 * D + dc * 128,
                                    [[D, mg], [1, 128]]), S.sz0)
            return ev_in
        ev_in = load_xfw(0)
        for m in range(g.n_mid):
            t0 = m * mg
            sch.wait("tensor", ev_in)
            if ev_hfree is not None:
                sch.wait("scalar", ev_hfree)
            ev_f1last = None
            ev_g = None
            for c in range(FC):
                bank = PH[c % 2]
                if m == 0:
                    sch.wait("tensor", ev_w1s[c // 4])
                if ev_hold[c % 2] is not None:
                    sch.wait("tensor", ev_hold[c % 2])
                evm = None
                for dc in range(DC):
                    def mk(e, c=c, dc=dc, bank=bank):
                        return e.matmul(bank[:, :],
                                        W1S[:, dc * FFN + c * 128:
                                            dc * FFN + c * 128 + 128],
                                        XFW[:, dc * mg:(dc + 1) * mg],
                                        start=(dc == 0), stop=(dc == DC - 1))
                    evm = sch.step("tensor", mk,
                                   S.spe if dc == DC - 1 else None)
                ev_f1last = evm
                sch.wait("scalar", evm)
                def mkg(e, c=c, bank=bank):
                    return e.activation(HMID[:, c * mg:(c + 1) * mg],
                                        bank[:, :], AF.Gelu,
                                        bias=bias1_fm[:, c:c + 1])
                ev_g = sch.step("scalar", mkg, S.sac)
                ev_hold[c % 2] = ev_g
            ev_xfree = ev_f1last
            if m + 1 < g.n_mid:
                sch.wait("sync", ev_xfree)
                ev_in = load_xfw(m + 1)
            sch.wait("tensor", ev_g)
            pv = 0
            ev_p = [None, None]
            ev_last_f2 = None
            for tg in range(ntg):
                tpar = tg % 2
                if ev_pfree[tpar] is not None:
                    sch.wait("sync", ev_pfree[tpar])
                ev_p[tpar] = sch.dma("sync", PSL[tpar][:],
                                     AP(PBs, (t0 + tg * 128) * D,
                                        [[D, 128], [1, D]]),
                                     (S.sp0, S.sp1)[tpar])
                for ng in range(2):
                    opar = pv % 2
                    bank = PACC[opar]
                    if ev_accfree[opar] is not None:
                        sch.wait("tensor", ev_accfree[opar])
                    evm = None
                    for c in range(FC):
                        if m == 0 and tg == 0 and ng == 0:
                            sch.wait("tensor", ev_w2s[c // 4])
                        def mk(e, c=c, tg=tg, ng=ng, bank=bank):
                            return e.matmul(
                                bank[:, :],
                                AP(HMID, c * mg + tg * 128,
                                   [[FC * mg, 128], [1, 128]]),
                                W2S[:, c * D + ng * 512:
                                    c * D + ng * 512 + 512],
                                start=(c == 0), stop=(c == FC - 1))
                        evm = sch.step("tensor", mk,
                                       S.spe if c == FC - 1 else None)
                    ev_last_f2 = evm
                    sch.wait("vector", evm)
                    sch.wait("vector", ev_p[tpar])
                    if ev_otfree[opar] is not None:
                        sch.wait("vector", ev_otfree[opar])
                    def mko(e, ng=ng, bank=bank, opar=opar, tpar=tpar):
                        return e.scalar_tensor_tensor(
                            OT[opar][:], bank[:, :], 1.0,
                            PSL[tpar][:, ng * 512:(ng + 1) * 512],
                            ALU.mult, ALU.add)
                    sch.step("vector", mko)
                    def mko2(e, ng=ng, opar=opar):
                        return e.tensor_tensor(
                            OT[opar][:], OT[opar][:],
                            bias2_bc[:, ng * 512:(ng + 1) * 512], ALU.add)
                    ev_ot = sch.step("vector", mko2, S.sdv)
                    ev_accfree[opar] = ev_ot
                    sch.wait("sync", ev_ot)
                    ev_st = sch.dma("sync",
                                    AP(dst_t, (t0 + tg * 128) * D + ng * 512,
                                       [[D, 128], [1, 512]]),
                                    OT[opar][:], dst_sem)
                    ev_otfree[opar] = ev_st
                    if ng == 1:
                        ev_pfree[tpar] = ev_st
                    pv += 1
            ev_hfree = ev_last_f2
        ev_done = [sch.val(S.sdv), sch.val(S.sac), sch.val(S.spe),
                   sch.val(dst_sem), sch.val(S.sw0), sch.val(S.sw1)]
        sch.barrier(ev_done)
        fs.close()

    ffn_phase(w1_w, w2_w, W1B, XNB, W2Bb, PBs, S.sst, False, "w")
    if stop_after == "pw":
        return _early_out()
    ffn_phase(f1_w, f2_w, CF1, Z2B, F2Bb, out_t, S.sout, True, "f")

    sch.wait("sync", sch.val(S.sout))

    sch.run_into(nc)
    return nc



def _prep_common(inputs):
    f = np.ascontiguousarray
    w = {
        "q_w": f(inputs["L0_attn_q_w"], dtype=np.float32),
        "q_b": f(inputs["L0_attn_q_b"], dtype=np.float32).reshape(1, D),
        "k_w": f(inputs["L0_attn_k_w"], dtype=np.float32),
        "k_b": f(inputs["L0_attn_k_b"], dtype=np.float32).reshape(1, D),
        "v_w": f(inputs["L0_attn_v_w"], dtype=np.float32),
        "v_b": f(inputs["L0_attn_v_b"], dtype=np.float32).reshape(1, D),
        "o_w": f(inputs["L0_attn_o_w"], dtype=np.float32),
        "o_b": f(inputs["L0_attn_o_b"], dtype=np.float32).reshape(1, D),
        "dec_w": f(inputs["L0_attn_dec_w"], dtype=np.float32),
        "dec_b": f(inputs["L0_attn_dec_b"], dtype=np.float32).reshape(1, H),
        "ln1_w": f(inputs["L0_attn_ln1_w"], dtype=np.float32).reshape(1, D),
        "ln1_b": f(inputs["L0_attn_ln1_b"], dtype=np.float32).reshape(1, D),
        "ln2_w": f(inputs["L0_attn_ln2_w"], dtype=np.float32).reshape(1, D),
        "ln2_b": f(inputs["L0_attn_ln2_b"], dtype=np.float32).reshape(1, D),
        "f1_w": f(inputs["L0_attn_f1_w"], dtype=np.float32),
        "f1_b": f(inputs["L0_attn_f1_b"], dtype=np.float32).reshape(1, FFN),
        "f2_w": f(inputs["L0_attn_f2_w"], dtype=np.float32),
        "f2_b": f(inputs["L0_attn_f2_b"], dtype=np.float32).reshape(1, D),
        "w1_w": f(inputs["L0_ffn_w1"], dtype=np.float32),
        "w1_b": f(inputs["L0_ffn_b1"], dtype=np.float32).reshape(1, FFN),
        "w2_w": f(inputs["L0_ffn_w2"], dtype=np.float32),
        "w2_b": f(inputs["L0_ffn_b2"], dtype=np.float32).reshape(1, D),
        "ident": np.eye(128, dtype=np.float32).astype(bf16),
        "identf": np.eye(128, dtype=np.float32),
        "onesrow": np.ones((1, 128), dtype=np.float32).astype(bf16),
        "onesrowf": np.ones((1, 128), dtype=np.float32),
    }
    return w


def kernel(**inputs):
    x = np.ascontiguousarray(np.asarray(inputs["L0_attn_x"], dtype=np.float32))
    B, T, _ = x.shape
    g = Geo(t_seq=T)
    n_cores = 2 * B
    w = _prep_common(inputs)
    in_maps = []
    for core in range(n_cores):
        b, half = core // 2, core % 2
        own = x[b, half * g.t_own:(half + 1) * g.t_own]
        rem = x[b, (1 - half) * g.t_own:(2 - half) * g.t_own]
        m = dict(w)
        m["x_own"] = np.ascontiguousarray(own)
        m["x_rem"] = np.ascontiguousarray(rem)
        m["hmask"] = np.array([[float(half), float(1 - half)]],
                              dtype=np.float32)
        in_maps.append(m)
    nc = build_program(g, n_cores=n_cores)
    res = run_bass_kernel_spmd(nc, in_maps, core_ids=list(range(n_cores)))
    outs = [res.results[i]["out"] for i in range(n_cores)]
    full = np.empty((B, T, D), dtype=np.float32)
    for core in range(n_cores):
        b, half = core // 2, core % 2
        full[b, half * g.t_own:(half + 1) * g.t_own] = outs[core]
    return full
